# version 17
# baseline (speedup 1.0000x reference)
import numpy as np
import ml_dtypes
from contextlib import ExitStack

import concourse.bass as bass
import concourse.mybir as mybir
from concourse.bass_utils import run_bass_kernel_spmd

F32 = mybir.dt.float32
BF16 = mybir.dt.bfloat16
AF = mybir.ActivationFunctionType
ALU = mybir.AluOpType
AX = mybir.AxisListType


class P:

    def __init__(self, ncore=8, tok=2048, d=4096, dkv=1024, tt=512):
        self.NCORE = ncore
        self.TOK = tok
        self.D = d
        self.DKV = dkv
        self.TT = tt
        self.NC = d // 128
        self.KVH = dkv // 128
        self.NREP = self.NC // self.KVH
        self.NTILE = tok // tt
        self.NG = tt // 128
        self.NS1G = tok // 128
        self.NSTRIP = max(1, d // 512)
        self.SW = d // self.NSTRIP
        self.CH = self.NC // 2
        self.NKC = max(1, dkv // 512)
        self.KW = dkv // self.NKC
        self.E, self.R, self.ER = 8, 16, 128
        self.LNP = max(1, d // 1024)
        self.LPW = d // self.LNP
        self.NQT = self.LPW // 128
        self.NB = (self.NQT + 3) // 4


class Sem:
    def __init__(self, ctx, nc, name):
        self.h = ctx.enter_context(nc.semaphore(name))
        self.n = 0

    def inc(self, inst, k=1):
        inst.then_inc(self.h, k)
        self.n += k
        return self.n


def build_program(p: P):
    nc = bass.Bass("TRN2", target_bir_lowering=False)
    NC, TT, NG, D = p.NC, p.TT, p.NG, p.D
    NSG = NG * p.NSTRIP

    x_fmt = nc.dram_tensor("x_fmt", [p.NTILE, 128, NC, TT], F32, kind="ExternalInput")
    x_tm = nc.dram_tensor("x_tm", [p.TOK, D], F32, kind="ExternalInput")
    wq_t = nc.dram_tensor("wq_t", [NC, 128, NC, 128], F32, kind="ExternalInput")
    wk_t = nc.dram_tensor("wk_t", [128, NC, p.DKV], F32, kind="ExternalInput")
    wv_t = nc.dram_tensor("wv_t", [128, NC, p.DKV], F32, kind="ExternalInput")
    wo_t = nc.dram_tensor("wo_t", [p.NSTRIP, 128, NC, p.SW], F32, kind="ExternalInput")
    bw_t = nc.dram_tensor("bw_t", [p.NSTRIP, 128, NC, p.SW], F32, kind="ExternalInput")
    rw_t = nc.dram_tensor("rw_t", [128, NC, p.E], F32, kind="ExternalInput")
    ab_t = nc.dram_tensor("ab_t", [128, NC * 128 + D], F32, kind="ExternalInput")
    bq_fm_d = nc.dram_tensor("bq_fm", [128, NC], F32, kind="ExternalInput")
    bk_bc_d = nc.dram_tensor("bk_bc", [128, p.DKV], F32, kind="ExternalInput")
    bv_bc_d = nc.dram_tensor("bv_bc", [128, p.DKV], F32, kind="ExternalInput")
    bo_bc_d = nc.dram_tensor("bo_bc", [128, D], F32, kind="ExternalInput")
    cst_bf_d = nc.dram_tensor("cst_bf", [128, 832], BF16, kind="ExternalInput")
    cst_f_d = nc.dram_tensor("cst_f", [128, 258], F32, kind="ExternalInput")
    y_out = nc.dram_tensor("y", [p.TOK, D], F32, kind="ExternalOutput")

    kv_loc = nc.dram_tensor("kv_loc", [p.DKV + p.KVH, 128], F32)
    kv_glob = nc.dram_tensor("kv_glob", [p.DKV + p.KVH, 128], F32)
    QCW = (p.NC // 4) * (p.D // max(1, p.D // 512))
    wq_c = nc.dram_tensor("wq_c", [p.NC, 128, p.NC * 128], BF16)
    wo_c = nc.dram_tensor("wo_c", [p.NSTRIP, 4, 128, QCW], BF16)
    bw_c = nc.dram_tensor("bw_c", [p.NSTRIP, 4, 128, QCW], BF16)

    with ExitStack() as glob:
        gsb = lambda n, s, d: glob.enter_context(nc.sbuf_tensor(n, s, d))
        cst_bf = gsb("cst_bf_s", [128, 832], BF16)
        cst_f = gsb("cst_f_s", [128, 258], F32)
        eps_ln = None
        bq_fm = gsb("bq_fm_s", [128, NC], F32)
        nbq_fm = gsb("nbq_fm_s", [128, NC], F32)
        rw_bf = gsb("rw_bf_s", [128, NC * p.E], BF16)
        kvt_bf = gsb("kvt_bf_s", [128, p.DKV], BF16)
        inv_nm = gsb("inv_nm_s", [128, p.KVH], F32)
        ones_col = cst_bf[:, 0:1]
        zrow_bf = cst_bf[0:1, 1:129]
        z512_bf = cst_bf[0:1, 129:641]
        ident_bf = cst_bf[:, 704:832]
        ident_f = cst_f[:, 0:128]
        e_mat = cst_f[0:8, 128:256]
        eps_ln = cst_f[:, 256:257]

        cc = Sem(glob, nc, "cc")
        kvl = Sem(glob, nc, "kvl")
        abl = Sem(glob, nc, "abl")
        t_n = Sem(glob, nc, "t_n")
        v_n = Sem(glob, nc, "v_n")

        with ExitStack() as ca:
            asb = lambda n, s, d: ca.enter_context(nc.sbuf_tensor(n, s, d))
            aps = lambda n, s: ca.enter_context(nc.psum_tensor(n, s, F32))
            wk_sb = asb("wk_sb", [128, NC * p.DKV], BF16)
            wv_sb = asb("wv_sb", [128, NC * p.DKV], BF16)
            bk_bc = asb("bk_bc_s", [128, p.DKV], F32)
            bv_bc = asb("bv_bc_s", [128, p.DKV], F32)
            xs = [asb(f"xs{i}", [128, NC * 128], BF16) for i in range(2)]
            kt1 = asb("kt1", [128, p.DKV], F32)
            kt2 = asb("kt2", [128, p.DKV], F32)
            ke = asb("ke", [128, p.DKV], F32)
            k_bf = asb("k_bf", [128, p.DKV], BF16)
            v_bf = asb("v_bf", [128, p.DKV], BF16)
            kv_sb = asb("kv_sb", [128, p.DKV], F32)
            nm_sb = asb("nm_sb", [1, p.DKV], F32)
            PW = max(512, p.DKV)
            pk = aps("pk", [128, PW])
            pv = aps("pv", [128, PW])
            pkv = aps("pkv", [128, PW])
            pn = aps("pn", [128, PW])

            ld0 = Sem(ca, nc, "ld0")
            wks = [Sem(ca, nc, f"wks{i}") for i in range(8)]
            wvs = [Sem(ca, nc, f"wvs{i}") for i in range(8)]
            xsem = [Sem(ca, nc, f"xs_s{i}") for i in range(2)]
            tk = Sem(ca, nc, "tk")
            tv = Sem(ca, nc, "tv")
            tkv = Sem(ca, nc, "tkv")
            vk1 = Sem(ca, nc, "vk1")
            vk2 = Sem(ca, nc, "vk2")
            vkb = Sem(ca, nc, "vkb")
            vv1 = Sem(ca, nc, "vv1")
            as1 = Sem(ca, nc, "as1")
            vend = Sem(ca, nc, "vend")
            s1st = Sem(ca, nc, "s1st")

            with nc.Block() as blk:
                def a_g(g):
                    for gi in range(min(2, p.NS1G)):
                        tau, off = gi // NG, (gi % NG) * 128
                        src = x_fmt[tau][:, :, off:off + 128]
                        dst = xs[gi % 2].ap().rearrange("p (c t) -> p c t", c=NC)
                        xsem[gi % 2].inc(g.dma_start(out=dst, in_=src), 16)
                    ld0.inc(g.dma_start(out=cst_bf[:, :], in_=cst_bf_d.ap()), 16)
                    ld0.inc(g.dma_start(out=cst_f[:, :], in_=cst_f_d.ap()), 16)
                    ld0.inc(g.dma_start(out=bq_fm[:, :], in_=bq_fm_d.ap()), 16)
                    ld0.inc(g.dma_start(out=bk_bc[:, :], in_=bk_bc_d.ap()), 16)
                    ld0.inc(g.dma_start(out=bv_bc[:, :], in_=bv_bc_d.ap()), 16)
                    ld0.inc(g.dma_start(
                        out=rw_bf[:, :],
                        in_=rw_t.ap().rearrange("p c e -> p (c e)")), 16)
                    qn = max(1, NC // 8)
                    for qq in range(0, NC, qn):
                        wks[qq // qn].inc(g.dma_start(
                            out=wk_sb[:, qq * p.DKV:(qq + qn) * p.DKV],
                            in_=wk_t[:, qq:qq + qn, :].rearrange(
                                "p c o -> p (c o)")), 16)
                        wvs[qq // qn].inc(g.dma_start(
                            out=wv_sb[:, qq * p.DKV:(qq + qn) * p.DKV],
                            in_=wv_t[:, qq:qq + qn, :].rearrange(
                                "p c o -> p (c o)")), 16)
                    for gi in range(2, p.NS1G):
                        if gi >= 2:
                            g.wait_ge(tv.h, gi - 1)
                        tau, off = gi // NG, (gi % NG) * 128
                        src = x_fmt[tau][:, :, off:off + 128]
                        dst = xs[gi % 2].ap().rearrange("p (c t) -> p c t", c=NC)
                        xsem[gi % 2].inc(g.dma_start(out=dst, in_=src), 16)
                    g.wait_ge(s1st.h, 32)
                    cc.inc(g.collective_compute(
                        "AllReduce", ALU.add,
                        replica_groups=[list(range(p.NCORE))],
                        ins=[kv_loc.ap().opt()],
                        outs=[kv_glob.ap().opt()]))
                blk.gpsimd(a_g)

                def a_t(t):
                    t.wait_ge(ld0.h, 96)
                    for ps in (pkv, pn):
                        for hh in range(PW // 512):
                            t.matmul(ps[:, hh * 512:(hh + 1) * 512],
                                     zrow_bf, z512_bf, start=True, stop=False)
                    for gi in range(p.NS1G):
                        par = gi % 2
                        t.wait_ge(xsem[par].h, 16 * (gi // 2 + 1))
                        if gi >= 1:
                            t.wait_ge(vk1.h, gi)
                        qn = max(1, NC // 8)
                        for c in range(NC):
                            if gi == 0 and c % qn == 0:
                                t.wait_ge(wks[c // qn].h, 16)
                            for hh in range(p.NKC):
                                inst = t.matmul(
                                    pk[:, hh * p.KW:(hh + 1) * p.KW],
                                    xs[par][:, c * 128:(c + 1) * 128],
                                    wk_sb[:, c * p.DKV + hh * p.KW:
                                          c * p.DKV + (hh + 1) * p.KW],
                                    start=(c == 0), stop=(c == NC - 1))
                        tk.inc(inst)
                        if gi >= 1:
                            t.wait_ge(vv1.h, gi)
                        for c in range(NC):
                            if gi == 0 and c % qn == 0:
                                t.wait_ge(wvs[c // qn].h, 16)
                            for hh in range(p.NKC):
                                inst = t.matmul(
                                    pv[:, hh * p.KW:(hh + 1) * p.KW],
                                    xs[par][:, c * 128:(c + 1) * 128],
                                    wv_sb[:, c * p.DKV + hh * p.KW:
                                          c * p.DKV + (hh + 1) * p.KW],
                                    start=(c == 0), stop=(c == NC - 1))
                        tv.inc(inst)
                        t.wait_ge(vkb.h, gi + 1)
                        t.wait_ge(vv1.h, gi + 1)
                        for h in range(p.KVH):
                            last_in_bank = (h % 4 == 3) or (h == p.KVH - 1)
                            inst = t.matmul(pkv[:, h * 128:(h + 1) * 128],
                                            k_bf[:, h * 128:(h + 1) * 128],
                                            v_bf[:, h * 128:(h + 1) * 128],
                                            start=False,
                                            stop=(gi == p.NS1G - 1
                                                  and last_in_bank))
                        for hh in range(p.NKC):
                            inst = t.matmul(pn[0:1, hh * p.KW:(hh + 1) * p.KW],
                                            ones_col,
                                            k_bf[:, hh * p.KW:(hh + 1) * p.KW],
                                            start=False, stop=False)
                        tkv.inc(inst)
                    for hh in range(PW // 512):
                        tkv.inc(t.matmul(pn[:, hh * 512:(hh + 1) * 512],
                                         zrow_bf, z512_bf,
                                         start=False, stop=True))
                blk.tensor(a_t)

                def a_s(s):
                    for gi in range(p.NS1G):
                        s.wait_ge(vk2.h, gi + 1)
                        as1.inc(s.activation(ke[:, :], kt2[:, :], AF.Exp))
                blk.scalar(a_s)

                def a_v(v):
                    v.wait_ge(ld0.h, 96)
                    v.tensor_single_scalar(nbq_fm[:, :], bq_fm[:, :],
                                           -1.0, ALU.mult)
                    for gi in range(p.NS1G):
                        v.wait_ge(tk.h, gi + 1)
                        vk1.inc(v.tensor_add(kt1[:, :], pk[:, 0:p.DKV], bk_bc[:, :]))
                        v.wait_ge(vk1.h, vk1.n)
                        if gi >= 1:
                            v.wait_ge(as1.h, gi)
                        vk2.inc(v.tensor_single_scalar(kt2[:, :], kt1[:, :],
                                                       0.0, ALU.min))
                        v.wait_ge(as1.h, gi + 1)
                        if gi >= 1:
                            v.wait_ge(tkv.h, gi)
                        vkb.inc(v.scalar_tensor_tensor(k_bf[:, :], kt1[:, :], 0.0,
                                                       ke[:, :], ALU.max, ALU.add))
                        v.wait_ge(tv.h, gi + 1)
                        vv1.inc(v.tensor_add(v_bf[:, :], pv[:, 0:p.DKV],
                                             bv_bc[:, :]))
                    v.wait_ge(tkv.h, p.NS1G + PW // 512)
                    vend.inc(v.tensor_copy(kv_sb[:, :], pkv[:, 0:p.DKV]))
                    vend.inc(v.tensor_copy(nm_sb[0:1, :], pn[0:1, 0:p.DKV]))
                blk.vector(a_v)

                def a_sy(sy):
                    sy.wait_ge(vend.h, 2)
                    src = kv_sb.ap().rearrange("p (h v) -> p h v", h=p.KVH)
                    dst = kv_loc[0:p.DKV].rearrange("(h q) v -> q h v", q=128)
                    s1st.inc(sy.dma_start(out=dst, in_=src), 16)
                    src2 = nm_sb.ap().rearrange("a (h v) -> a h v", h=p.KVH)
                    dst2 = kv_loc[p.DKV:p.DKV + p.KVH].rearrange(
                        "(a h) v -> a h v", a=1)
                    s1st.inc(sy.dma_start(out=dst2, in_=src2), 16)
                    sy.wait_ge(s1st.h, 32)
                blk.sync(a_sy)

        with ExitStack() as cb:
            bsb = lambda n, s, d: cb.enter_context(nc.sbuf_tensor(n, s, d))
            bps = lambda n, s: cb.enter_context(nc.psum_tensor(n, s, F32))
            bufAB = [bsb(f"bufab{i}", [128, NC * TT], BF16) for i in range(2)]
            zt = bsb("zt", [128, NG * D], F32)
            arena = [bsb(f"wa{i}", [128, NC * 128], BF16) for i in range(5)]
            ab_sb = bsb("ab_sb", [128, NC * 128 + D], BF16)
            bo_s = [bsb(f"bo{i}", [128, p.SW], F32) for i in range(2)]
            qy_bf = bsb("qy_bf", [128, max(2 * TT, p.LPW)], BF16)
            t2s = bsb("t2s", [128, TT], F32)
            t1r = bsb("t1r", [128, TT], F32)
            e_t = bsb("e_t", [128, TT], F32)
            tvs = bsb("tvs", [128, p.SW], F32)

            l_tm = bsb("l_tm", [128, p.E], F32)
            prb = bsb("prb", [128, p.E], F32)
            msk = bsb("msk", [128, p.E], F32)
            wfin = bsb("wfin", [128, p.E], F32)

            midw = bsb("midw", [128, TT], BF16)
            lg_sb = e_t[0:p.E, 0:TT]
            wrow = t1r[0:p.E, 0:TT]
            wexp = t2s
            stats = bsb("stats", [128, 6 * (D // 512)], F32)
            stats4 = bsb("stats4", [128, NG * 6 * (D // 512)], F32)
            aggr = bsb("aggr", [128, 2], F32)
            scal = bsb("scal", [128, 8], F32)
            nm8 = bsb("nm8", [p.KVH, 128], F32)
            nmt = bsb("nmt", [128, p.KVH], F32)
            PS = [bps(f"ps{i}", [128, 512]) for i in range(8)]

            w_s = [Sem(cb, nc, f"w_s{i}") for i in range(5)]
            cwslot = [Sem(cb, nc, f"cws{i}") for i in range(5)]
            wc_o = Sem(cb, nc, "wc_o")
            wc_b = Sem(cb, nc, "wc_b")
            xf = Sem(cb, nc, "xf")
            xfq = [Sem(cb, nc, f"xfq{i}") for i in range(4)]
            xt = [Sem(cb, nc, f"xt{i}") for i in range(NG)]
            bo_sem = [Sem(cb, nc, f"bo_sem{i}") for i in range(2)]
            st = Sem(cb, nc, "st")
            qm = Sem(cb, nc, "qm")
            am = Sem(cb, nc, "am")
            vq1 = Sem(cb, nc, "vq1")
            vqp = Sem(cb, nc, "vqp")
            vqb = Sem(cb, nc, "vqb")
            va = Sem(cb, nc, "va")
            aq1 = Sem(cb, nc, "aq1")
            om = Sem(cb, nc, "om")
            vo1 = Sem(cb, nc, "vo1")
            vo2 = Sem(cb, nc, "vo2")
            vst = Sem(cb, nc, "vst")
            vl1 = Sem(cb, nc, "vl1")
            al1 = Sem(cb, nc, "al1")
            vl2 = Sem(cb, nc, "vl2")
            al2 = Sem(cb, nc, "al2")
            vl3 = Sem(cb, nc, "vl3")
            ac = Sem(cb, nc, "ac")
            tl = Sem(cb, nc, "tl")
            vf = Sem(cb, nc, "vf")
            tf1 = Sem(cb, nc, "tf1")
            vf1 = Sem(cb, nc, "vf1")
            tf2 = Sem(cb, nc, "tf2")
            vf2 = Sem(cb, nc, "vf2")
            vf2a = Sem(cb, nc, "vf2a")
            as_f = Sem(cb, nc, "as_f")
            tf3 = Sem(cb, nc, "tf3")
            vf3 = Sem(cb, nc, "vf3")
            tf4 = Sem(cb, nc, "tf4")
            tf5 = Sem(cb, nc, "tf5")
            vf4 = Sem(cb, nc, "vf4")
            f4 = Sem(cb, nc, "f4")
            v42 = Sem(cb, nc, "v42")
            vl1b = Sem(cb, nc, "vl1b")
            al1b = Sem(cb, nc, "al1b")
            vl2b = Sem(cb, nc, "vl2b")
            al4 = Sem(cb, nc, "al4")
            vse = Sem(cb, nc, "vse")

            def vedge(v, inst):
                vse.inc(inst)
                v.wait_ge(vse.h, vse.n)

            QC = NC // 4
            BLK = NC + 8 * p.NSTRIP

            def blocks_of_tile():
                out = [("wq", oc, None) for oc in range(NC)]
                for s in range(p.NSTRIP):
                    for qq in range(4):
                        out.append(("wo", (s, qq), None))
                for s in range(p.NSTRIP):
                    for qq in range(4):
                        out.append(("bw", (s, qq), None))
                return out

            def block_src(kind, idx):
                if kind == "wq":
                    return wq_t[idx].rearrange("p c o -> p (c o)")
                s, qq = idx
                base = wo_t if kind == "wo" else bw_t
                sl = base[s].rearrange("p c o -> p c o")[
                    :, qq * QC:(qq + 1) * QC, :]
                return sl.rearrange("p c o -> p (c o)")

            def slot_of(j):
                return 4 if j == 0 else (j - 1) % 4

            def slot_count(j):
                return 1 if j == 0 else (j - 1) // 4 + 1

            def block_src_c(kind, idx):
                if kind == "wq":
                    return wq_c[idx].rearrange("p n -> p n")
                sq, qq = idx
                base = wo_c if kind == "wo" else bw_c
                return base[sq][qq].rearrange("p n -> p n")

            PST = [0] * 4
            for jj in range(1, NC + 8 * p.NSTRIP):
                PST[(jj - 1) % 4] += 1

            def wwait(t, b, tau):
                j = b - BLK * tau
                if j == 0:
                    t.wait_ge(w_s[4].h, 16 * (tau + 1))
                else:
                    par = (j - 1) % 4
                    cnt = tau * PST[par] + (j - 1) // 4 + 1
                    t.wait_ge(w_s[par].h, 16 * cnt)

            def consumed_wait(kind, idx, ptau):
                if kind == "wq":
                    return (qm.h, NC * ptau + idx + 1)
                s, qq = idx
                if kind == "wo":
                    if qq < 3:
                        return (wc_o.h, 3 * p.NSTRIP * ptau + 3 * s + qq + 1)
                    return (om.h, NSG * ptau + NG * (s + 1))
                if qq < 3:
                    return (wc_b.h, 3 * p.NSTRIP * ptau + 3 * s + qq + 1)
                return (f4.h, NSG * ptau + NG * (s + 1))

            with nc.Block() as blk:


                tb = blocks_of_tile()
                for tau in range(p.NTILE):
                    b0 = BLK * tau
                    qm0, am0, vq10, vqp0, vqb0, va0, aq10 = (
                        qm.n, am.n, vq1.n, vqp.n, vqb.n, va.n, aq1.n)
                    om0, vo10, vo20, vst0 = om.n, vo1.n, vo2.n, vst.n
                    vl10, al10, vl20, al20, vl30, tl0, vf0, ac0 = (
                        vl1.n, al1.n, vl2.n, al2.n, vl3.n, tl.n, vf.n, ac.n)
                    tf10, vf10, tf20, vf20, vf2a0, asf0, tf30, vf30 = (
                        tf1.n, vf1.n, tf2.n, vf2.n, vf2a.n, as_f.n, tf3.n, vf3.n)
                    tf40, tf50, vf40, f40, v420 = tf4.n, tf5.n, vf4.n, f4.n, v42.n
                    vl1b0, al1b0, vl2b0, al40 = vl1b.n, al1b.n, vl2b.n, al4.n

                    XFb = bufAB[tau % 2]
                    ATb = bufAB[(tau + 1) % 2]

                    def ring_wait(g, j, tau=tau):
                        if j == 0:
                            pj, ptau = 0, tau - 1
                        elif j - 4 >= 1:
                            pj, ptau = j - 4, tau
                        else:
                            ptau = tau - 1
                            pslot = (j - 1) % 4
                            pj = max(jj for jj in range(1, BLK)
                                     if (jj - 1) % 4 == pslot)
                        if ptau < 0:
                            return
                        kind, idx, _ = tb[pj]
                        sem_h, val = consumed_wait(kind, idx, ptau)
                        g.wait_ge(sem_h, val)

                    def b_g(g, tau=tau):
                        if tau > 0:
                            g.wait_ge(om.h, NSG * tau)
                        if tau == 0:
                            qn = max(1, NC // 4)
                            for qq in range(0, NC, qn):
                                xfq[qq // qn].inc(g.dma_start(
                                    out=XFb[:, qq * TT:(qq + qn) * TT],
                                    in_=x_fmt[tau][:, qq:qq + qn, :].rearrange(
                                        "p c t -> p (c t)")), 16)
                        else:
                            src = x_fmt[tau].rearrange("p c t -> p (c t)")
                            xf.inc(g.dma_start(out=XFb[:, :], in_=src), 16)
                        for j, (kind, idx, _) in enumerate(tb):
                            if j == 0 and tau > 0:
                                continue
                            ring_wait(g, j)
                            if tau == 0 and j >= 5:
                                pj = j - 4 if j >= 5 else j
                                g.wait_ge(cwslot[slot_of(pj)].h,
                                          16 * slot_count(pj))
                            if tau == 1:
                                g.wait_ge(cwslot[slot_of(j)].h,
                                          16 * slot_count(j))
                                if 1 <= j <= 4:
                                    pslot = (j - 1) % 4
                                    pj = max(jj for jj in range(1, BLK)
                                             if (jj - 1) % 4 == pslot)
                                    g.wait_ge(cwslot[slot_of(pj)].h,
                                              16 * slot_count(pj))
                            par = slot_of(j)
                            src = (block_src(kind, idx) if tau == 0
                                   else block_src_c(kind, idx))
                            w_s[par].inc(
                                g.dma_start(out=arena[par][:, :],
                                            in_=src), 16)
                            if tau == 0 and j == 3:
                                g.wait_ge(cc.h, 1)
                                srck = kv_glob[0:p.DKV].rearrange(
                                    "(h q) v -> q h v", q=128)
                                dstk = kvt_bf.ap().rearrange(
                                    "p (h v) -> p h v", h=p.KVH)
                                kvl.inc(g.dma_start(out=dstk, in_=srck), 16)
                                kvl.inc(g.dma_start(
                                    out=nm8[:, :],
                                    in_=kv_glob[p.DKV:p.DKV + p.KVH]), 16)
                                abl.inc(g.dma_start(out=ab_sb[:, :],
                                                    in_=ab_t.ap()), 16)
                        if tau < p.NTILE - 1:
                            g.wait_ge(qm.h, NC * tau + 1)
                            g.wait_ge(cwslot[slot_of(0)].h, 16)
                            w_s[4].inc(g.dma_start(
                                out=arena[4][:, :],
                                in_=block_src_c("wq", 0)), 16)
                    blk.gpsimd(b_g)

                    def b_sy(sy, tau=tau, al40=al40, vo10=vo10):
                        for gi in range(NG):
                            if tau > 0:
                                sy.wait_ge(al4.h, al40 - NG + gi + 1)
                                r0 = (tau - 1) * TT + gi * 128
                                st.inc(sy.dma_start(
                                    out=y_out[r0:r0 + 128, :],
                                    in_=zt[:, gi * D:(gi + 1) * D]), 16)
                                sy.wait_ge(st.h, st.n)
                            r0 = tau * TT + gi * 128
                            xt[gi].inc(sy.dma_start(
                                out=zt[:, gi * D:(gi + 1) * D],
                                in_=x_tm[r0:r0 + 128, :]), 16)
                        def emit_bo(s2):
                            if s2 >= 2:
                                sy.wait_ge(vo1.h, vo10 + NG * (s2 - 2) + NG)
                            bo_sem[s2 % 2].inc(sy.dma_start(
                                out=bo_s[s2 % 2][:, :],
                                in_=bo_bc_d[:, s2 * p.SW:(s2 + 1) * p.SW]), 16)

                        if tau == 0:
                            for s2 in range(min(2, p.NSTRIP)):
                                emit_bo(s2)
                            for j, (kind, idx, _) in enumerate(tb):
                                par = slot_of(j)
                                sy.wait_ge(w_s[par].h, 16 * slot_count(j))
                                if kind == "wq":
                                    dst = wq_c[idx].rearrange("p n -> p n")
                                    width = NC * 128
                                else:
                                    sq, qq2 = idx
                                    basec = wo_c if kind == "wo" else bw_c
                                    dst = basec[sq][qq2].rearrange("p n -> p n")
                                    width = QC * p.SW
                                cwslot[par].inc(sy.dma_start(
                                    out=dst, in_=arena[par][:, 0:width]), 16)
                                if (kind == "wo" and idx[1] == 3
                                        and idx[0] + 2 < p.NSTRIP):
                                    emit_bo(idx[0] + 2)
                        else:
                            for s2 in range(p.NSTRIP):
                                emit_bo(s2)
                    blk.sync(b_sy)

                    def q_t(t, tau=tau, vqp0=vqp0, vqb0=vqb0, va0=va0,
                            v420=v420, aq10=aq10):
                        if tau > 0:
                            t.wait_ge(xf.h, 16 * tau)
                            t.wait_ge(v42.h, v420)
                        for oc in range(NC):
                            b = b0 + oc
                            par = slot_of(oc)
                            wwait(t, b, tau)
                            if oc >= 2:
                                t.wait_ge(aq1.h, aq10 + 3 * (oc - 1))
                            for kc in range(NC):
                                if tau == 0 and oc == 0 and \
                                        kc % max(1, NC // 4) == 0:
                                    t.wait_ge(xfq[kc // max(1, NC // 4)].h, 16)
                                inst = t.matmul(
                                    PS[oc % 2][:, 0:TT],
                                    arena[par][:, kc * 128:(kc + 1) * 128],
                                    XFb[:, kc * TT:(kc + 1) * TT],
                                    start=(kc == 0), stop=(kc == NC - 1))
                            qm.inc(inst)
                            if tau == 0 and oc == 0:
                                t.wait_ge(kvl.h, 32)
                                t_n.inc(t.matmul(PS[7][:, 0:p.KVH], nm8[:, :],
                                                 ident_f[0:p.KVH, 0:p.KVH],
                                                 start=True, stop=True))
                            if oc >= 1:
                                emit_attn(t, oc - 1)
                        emit_attn(t, NC - 1)
                    def emit_attn(t, j, tau=tau, vqb0=vqb0, va0=va0):
                        t.wait_ge(vqb.h, vqb0 + j + 1)
                        if j >= 2:
                            t.wait_ge(va.h, va0 + j - 1)
                        h = j // p.NREP
                        am.inc(t.matmul(PS[2 + j % 2][:, 0:TT],
                                        kvt_bf[:, h * 128:(h + 1) * 128],
                                        qy_bf[:, (j % 2) * TT:(j % 2 + 1) * TT],
                                        start=True, stop=True))
                    blk.tensor(q_t)

                    def q_s(s, tau=tau, qm0=qm0, am0=am0, vqb0=vqb0,
                            va0=va0, aq10=aq10):
                        for oc in range(NC):
                            s.wait_ge(qm.h, qm0 + oc + 1)
                            aq1.inc(s.activation(t2s[:, 0:TT],
                                                 PS[oc % 2][:, 0:TT], AF.Relu,
                                                 bias=nbq_fm[:, oc:oc + 1],
                                                 scale=-1.0))
                            s.wait_ge(aq1.h, aq1.n)
                            if oc >= 1:
                                s.wait_ge(vqb.h, vqb0 + oc)
                            aq1.inc(s.activation(e_t[:, 0:TT], t2s[:, 0:TT],
                                                 AF.Exp, scale=-1.0))
                            s.wait_ge(aq1.h, aq1.n)
                            if oc >= 1:
                                s.wait_ge(vqb.h, vqb0 + oc)
                            aq1.inc(s.activation(t1r[:, 0:TT],
                                                 PS[oc % 2][:, 0:TT], AF.Relu,
                                                 bias=bq_fm[:, oc:oc + 1]))
                            if oc >= 1:
                                j = oc - 1
                                if tau == 0 and j == 0:
                                    s.wait_ge(v_n.h, 2)
                                s.wait_ge(am.h, am0 + j + 1)
                                h = j // p.NREP
                                va.inc(s.activation(
                                    ATb[:, j * TT:(j + 1) * TT],
                                    PS[2 + j % 2][:, 0:TT], AF.Identity,
                                    scale=inv_nm[:, h:h + 1]))
                        j = NC - 1
                        s.wait_ge(am.h, am0 + j + 1)
                        h = j // p.NREP
                        va.inc(s.activation(
                            ATb[:, j * TT:(j + 1) * TT],
                            PS[2 + j % 2][:, 0:TT], AF.Identity,
                            scale=inv_nm[:, h:h + 1]))
                    blk.scalar(q_s)

                    def q_v(v, tau=tau, qm0=qm0, am0=am0, aq10=aq10):
                        if tau == 0:
                            v.wait_ge(t_n.h, 1)
                            v_n.inc(v.tensor_single_scalar(
                                nmt[:, :], PS[7][:, 0:p.KVH], 1e-10, ALU.add))
                            v.wait_ge(v_n.h, v_n.n)
                            v_n.inc(v.reciprocal(inv_nm[:, :], nmt[:, :]))
                        for oc in range(NC):
                            v.wait_ge(aq1.h, aq10 + 3 * oc + 3)
                            if oc >= 2:
                                v.wait_ge(am.h, am0 + oc - 1)
                            vqb.inc(v.tensor_add(
                                qy_bf[:, (oc % 2) * TT:(oc % 2 + 1) * TT],
                                t1r[:, 0:TT], e_t[:, 0:TT]))
                    blk.vector(q_v)

                    def o_t(t, tau=tau, va0=va0, vqp0=vqp0, vo10=vo10,
                            aq10=aq10):
                        for s in range(p.NSTRIP):
                            sp = s % 2
                            for qq in range(4):
                                b = b0 + NC + 4 * s + qq
                                par = slot_of(NC + 4 * s + qq)
                                wwait(t, b, tau)
                                if s == 0 and qq == 0:
                                    t.wait_ge(va.h, va0 + NC)
                                    t.wait_ge(aq1.h, aq10 + 3 * NC)
                                if tau == 0 and s == min(1, p.NSTRIP - 1) \
                                        and qq == 0:
                                    t.wait_ge(v_n.h, 2)
                                if s >= 2 and qq == 0:
                                    t.wait_ge(vo1.h, vo10 + NG * (s - 2) + NG)
                                for gi in range(NG):
                                    for c in range(QC):
                                        fc = qq * QC + c
                                        inst = t.matmul(
                                            PS[4 * sp + gi][:, 0:p.SW],
                                            ATb[:, fc * TT + gi * 128:
                                                fc * TT + (gi + 1) * 128],
                                            arena[par][:, c * p.SW:(c + 1) * p.SW],
                                            start=(qq == 0 and c == 0),
                                            stop=(qq == 3 and c == QC - 1))
                                    if qq == 3:
                                        om.inc(inst)
                                if qq < 3:
                                    wc_o.inc(inst)
                    blk.tensor(o_t)

                    def o_v(v, tau=tau, om0=om0):
                        nbo = (p.NSTRIP + 1) // 2
                        for s in range(p.NSTRIP):
                            for gi in range(NG):
                                v.wait_ge(om.h, om0 + NG * s + gi + 1)
                                v.wait_ge(bo_sem[s % 2].h,
                                          16 * (tau * nbo + s // 2 + 1))
                                v.wait_ge(xt[gi].h, 16 * (tau + 1))
                                vo1.inc(v.tensor_add(
                                    tvs[:, :], PS[4 * (s % 2) + gi][:, 0:p.SW],
                                    bo_s[s % 2][:, :]))
                                v.wait_ge(vo1.h, vo1.n)
                                sl = zt[:, gi * D + s * p.SW:
                                        gi * D + (s + 1) * p.SW]
                                vo2.inc(v.tensor_add(sl, tvs[:, :], sl))
                                v.wait_ge(vo2.h, vo2.n)
                                nst = D // 512
                                for i2 in range(p.SW // 512):
                                    i = (s * p.SW) // 512 + i2
                                    vst.inc(v.bn_stats(
                                        stats4[:, gi * 6 * nst + 6 * i:
                                               gi * 6 * nst + 6 * (i + 1)],
                                        zt[:, gi * D + 512 * i:
                                           gi * D + 512 * (i + 1)]))
                    blk.vector(o_v)

                    def l_v(v, tau=tau, vo20=vo20, al10=al10, al40=al40,
                            tl0=tl0, vf0=vf0):
                        nst = D // 512
                        nps = p.SW // 512
                        for gi in range(NG):
                            v.wait_ge(vst.h, vst0 + nps * (NG * (p.NSTRIP - 1)
                                                           + gi + 1))
                            if gi == 0 and tau > 0:
                                v.wait_ge(al4.h, al40)
                            vl1.inc(v.bn_aggr(aggr[:, 0:2],
                                              stats4[:, gi * 6 * nst:
                                                     (gi + 1) * 6 * nst]))
                            v.wait_ge(vl1.h, vl1.n)
                            v.wait_ge(al1.h, al10 + gi + 1)
                            rec = v.reciprocal(scal[:, gi:gi + 1],
                                               stats[:, 1:2])
                            nmop = v.tensor_single_scalar(stats[:, 0:1],
                                                          aggr[:, 0:1],
                                                          -1.0, ALU.mult)
                            vedge(v, rec)
                            vedge(v, nmop)
                            vl2.inc(v.tensor_mul(scal[:, 4 + gi:5 + gi],
                                                 stats[:, 0:1],
                                                 scal[:, gi:gi + 1]))
                            v.wait_ge(vl2.h, vl2.n)
                        dst = XFb.ap().rearrange("p (c t) -> p c t", c=NC)
                        for pc in range(p.LNP):
                            for gi in range(NG):
                                idx = pc * NG + gi
                                v.wait_ge(tl.h, tl0 + idx + 1)
                                for bi in range(p.NB):
                                    nch = min(4, p.NQT - 4 * bi)
                                    c0 = pc * p.NQT + bi * 4
                                    sl = dst[:, c0:c0 + nch,
                                             gi * 128:(gi + 1) * 128]
                                    vf.inc(v.tensor_copy(
                                        sl,
                                        PS[6 + bi][:, 0:128 * nch].rearrange(
                                            "p (c t) -> p c t", c=nch)))
                    blk.vector(l_v)

                    def l_s(s, tau=tau, vl10=vl10, vl20=vl20, tl0=tl0,
                            al20=al20):
                        for gi in range(NG):
                            s.wait_ge(vl1.h, vl10 + gi + 1)
                            al1.inc(s.activation(stats[:, 1:2], aggr[:, 1:2],
                                                 AF.Sqrt, bias=eps_ln))
                        for pc in range(p.LNP):
                            for gi in range(NG):
                                idx = pc * NG + gi
                                if pc == 0:
                                    s.wait_ge(vl2.h, vl20 + gi + 1)
                                z = zt[:, gi * D + pc * p.LPW:
                                       gi * D + (pc + 1) * p.LPW]
                                al2.inc(s.activation(
                                    z, z, AF.Identity,
                                    bias=scal[:, 4 + gi:5 + gi],
                                    scale=scal[:, gi:gi + 1]))
                                s.wait_ge(al2.h, al2.n)
                                if idx >= 2 and 2 * TT >= p.LPW:
                                    s.wait_ge(tl.h, tl0 + idx - 1)
                                elif idx >= 1 and 2 * TT < p.LPW:
                                    s.wait_ge(tl.h, tl0 + idx)
                                use2 = (2 * TT >= p.LPW) and idx % 2 == 1
                                dstq = (t2s.ap().bitcast(BF16)[:, 0:p.LPW]
                                        if use2 else qy_bf[:, 0:p.LPW])
                                ac.inc(s.activation(dstq, z, AF.Identity))
                    blk.scalar(l_s)

                    def f_t(t, tau=tau, vf0=vf0, vf10=vf10, vf20=vf20,
                            vf2a0=vf2a0, vf30=vf30, vf40=vf40, v420=v420,
                            ac0=ac0, tl0=tl0, vo20=vo20):
                        if tau == 0:
                            t.wait_ge(abl.h, 16)
                        t.wait_ge(vo2.h, vo20 + NG * p.NSTRIP)

                        def emit_chunks(pcx):
                            t.wait_ge(vf.h, vf0 + (pcx + 1) * NG * p.NB)
                            for c in range(pcx * p.NQT, (pcx + 1) * p.NQT):
                                inst = t.matmul(
                                    PS[0][0:p.E, 0:TT],
                                    rw_bf[:, c * p.E:(c + 1) * p.E],
                                    XFb[:, c * TT:(c + 1) * TT],
                                    start=(c == 0), stop=(c == NC - 1))
                                if c == NC - 1:
                                    tf1.inc(inst)
                                inst = t.matmul(
                                    PS[4][:, 0:TT],
                                    ab_sb[:, c * 128:(c + 1) * 128],
                                    XFb[:, c * TT:(c + 1) * TT],
                                    start=(c == 0), stop=(c == NC - 1))
                                if c == NC - 1:
                                    tf5.inc(inst)

                        for pc in range(p.LNP):
                            for gi in range(NG):
                                idx = pc * NG + gi
                                t.wait_ge(ac.h, ac0 + idx + 1)
                                if idx >= 1:
                                    t.wait_ge(vf.h, vf0 + idx * p.NB)
                                use2 = (2 * TT >= p.LPW) and idx % 2 == 1
                                srcq = (t2s.ap().bitcast(BF16) if use2
                                        else qy_bf)
                                for j in range(p.NQT):
                                    bi, jo = j // 4, j % 4
                                    inst = t.matmul(
                                        PS[6 + bi][:, jo * 128:(jo + 1) * 128],
                                        srcq[:, j * 128:(j + 1) * 128],
                                        ident_bf, start=True, stop=True)
                                tl.inc(inst)
                            if pc >= 1:
                                emit_chunks(pc - 1)
                        emit_chunks(p.LNP - 1)
                        def emit_tf2(gi):
                            if gi == 0:
                                t.wait_ge(vf1.h, vf10 + 1)
                            else:
                                t.wait_ge(vf2a.h, vf2a0 + gi)
                            tf2.inc(t.matmul(
                                PS[5][:, 0:p.E],
                                lg_sb[:, gi * 128:(gi + 1) * 128],
                                ident_f[0:p.E, 0:p.E],
                                start=True, stop=True))

                        def emit_tf3(gi):
                            t.wait_ge(vf2.h, vf20 + gi + 1)
                            tf3.inc(t.matmul(
                                PS[6][0:p.E, gi * 128:(gi + 1) * 128],
                                wfin[:, :], ident_f, start=True, stop=True))

                        def emit_tf4():
                            t.wait_ge(vf3.h, vf30 + 1)
                            tf4.inc(t.matmul(PS[7][:, 0:TT], e_mat,
                                             wrow[:, 0:TT],
                                             start=True, stop=True))
                        for s in range(p.NSTRIP):
                            sp = s % 2
                            for qq in range(4):
                                b = b0 + NC + 4 * p.NSTRIP + 4 * s + qq
                                par = slot_of(NC + 4 * p.NSTRIP + 4 * s + qq)
                                wwait(t, b, tau)
                                if s == 0 and qq == 0:
                                    t.wait_ge(vf1.h, vf10 + 1)
                                if s == 0 and qq == 1:
                                    emit_tf2(0)
                                    emit_tf3(0)
                                    if NG >= 2:
                                        emit_tf2(1)
                                if s == 0 and qq == 2:
                                    if NG >= 2:
                                        emit_tf3(1)
                                    for g2i in range(2, NG):
                                        emit_tf2(g2i)
                                        emit_tf3(g2i)
                                if s == 0 and qq == 3:
                                    emit_tf4()
                                if s == 1 and qq == 0:
                                    t.wait_ge(vf4.h, vf40 + 2)
                                if s >= 2 and qq == 0:
                                    t.wait_ge(v42.h, v420 + NG * (s - 2) + NG)
                                for gi in range(NG):
                                    for c in range(QC):
                                        fc = qq * QC + c
                                        inst = t.matmul(
                                            PS[4 * sp + gi][:, 0:p.SW],
                                            XFb[:, fc * TT + gi * 128:
                                                fc * TT + (gi + 1) * 128],
                                            arena[par][:, c * p.SW:(c + 1) * p.SW],
                                            start=(qq == 0 and c == 0),
                                            stop=False)
                                    if qq == 3:
                                        if s == 0 and gi == 0:
                                            t.wait_ge(vf4.h, vf40 + 2)
                                        f4.inc(t.matmul(
                                            PS[4 * sp + gi][:, 0:p.SW],
                                            midw[:, gi * 128:(gi + 1) * 128],
                                            ab_sb[:, NC * 128 + s * p.SW:
                                                  NC * 128 + (s + 1) * p.SW],
                                            start=False, stop=True))
                                if qq < 3:
                                    wc_b.inc(inst)
                    blk.tensor(f_t)

                    def f_s(s, tau=tau, vl1b0=vl1b0, vl2b0=vl2b0):
                        for gi in range(NG):
                            s.wait_ge(vl1b.h, vl1b0 + gi + 1)
                            as_f.inc(s.activation(prb[:, :], l_tm[:, :], AF.Exp,
                                                  bias=scal[:, 0:1],
                                                  accum_out=scal[:, 1:2]))
                        for gi in range(NG):
                            s.wait_ge(vl2b.h, vl2b0 + 2 * gi + 1)
                            al1b.inc(s.activation(scal[:, 4:5], aggr[:, 1:2],
                                                  AF.Sqrt, bias=eps_ln))
                            s.wait_ge(vl2b.h, vl2b0 + 2 * gi + 2)
                            al4.inc(s.activation(zt[:, gi * D:(gi + 1) * D],
                                                 zt[:, gi * D:(gi + 1) * D],
                                                 AF.Identity, bias=scal[:, 7:8],
                                                 scale=scal[:, 5:6]))
                    blk.scalar(f_s)

                    def f_v(v, tau=tau, tf10=tf10, tf20=tf20, tf30=tf30,
                            tf40=tf40, tf50=tf50, f40=f40, v420=v420,
                            asf0=asf0, al20=al20, al1b0=al1b0, al40=al40,
                            vf2a0=vf2a0):
                        v.wait_ge(tf1.h, tf10 + 1)
                        v.wait_ge(al2.h, al20 + NG * p.LNP)
                        vf1.inc(v.tensor_copy(lg_sb[:, :], PS[0][0:p.E, 0:TT]))
                        for gi in range(NG):
                            v.wait_ge(tf2.h, tf20 + gi + 1)
                            vf2a.inc(v.tensor_copy(l_tm[:, :], PS[5][:, 0:p.E]))
                            v.wait_ge(vf2a.h, vf2a.n)
                            vl1b.inc(v.tensor_reduce(scal[:, 0:1], l_tm[:, :],
                                                     AX.X, ALU.max, negate=True))
                            v.wait_ge(as_f.h, asf0 + gi + 1)
                            rec = v.reciprocal(scal[:, 2:3], scal[:, 1:2])
                            vedge(v, rec)
                            vedge(v, v.tensor_scalar(prb[:, :], prb[:, :],
                                                     scal[:, 2:3], None,
                                                     ALU.mult))
                            m1 = v.tensor_reduce(scal[:, 3:4], prb[:, :],
                                                 AX.X, ALU.max)
                            vedge(v, m1)
                            vedge(v, v.tensor_scalar(msk[:, :], prb[:, :],
                                                     scal[:, 3:4], None,
                                                     ALU.is_equal))
                            vedge(v, v.scalar_tensor_tensor(
                                msk[:, :], msk[:, :], -2.0,
                                prb[:, :], ALU.mult, ALU.add))
                            m2 = v.tensor_reduce(scal[:, 4:5], msk[:, :],
                                                 AX.X, ALU.max)
                            vedge(v, m2)
                            vedge(v, v.tensor_scalar(msk[:, :], prb[:, :],
                                                     scal[:, 4:5], None,
                                                     ALU.is_ge))
                            dn = v.tensor_add(scal[:, 6:7], scal[:, 3:4],
                                              scal[:, 4:5])
                            vedge(v, dn)
                            rd = v.reciprocal(scal[:, 5:6], scal[:, 6:7])
                            vedge(v, rd)
                            sc2 = v.tensor_single_scalar(scal[:, 6:7],
                                                         scal[:, 5:6],
                                                         2.0, ALU.mult)
                            vedge(v, sc2)
                            vedge(v, v.tensor_mul(wfin[:, :], prb[:, :],
                                                  msk[:, :]))
                            vf2.inc(v.tensor_scalar(wfin[:, :], wfin[:, :],
                                                    scal[:, 6:7], None,
                                                    ALU.mult))
                        v.wait_ge(tf3.h, tf30 + NG)
                        vf3.inc(v.tensor_copy(wrow[:, 0:TT],
                                              PS[6][0:p.E, 0:TT]))
                        v.wait_ge(tf4.h, tf40 + 1)
                        vf4.inc(v.tensor_copy(wexp[:, 0:TT], PS[7][:, 0:TT]))
                        v.wait_ge(vf4.h, vf4.n)
                        v.wait_ge(tf5.h, tf50 + 1)
                        vf4.inc(v.tensor_mul(midw[:, 0:TT], PS[4][:, 0:TT],
                                             wexp[:, 0:TT]))
                        for s in range(p.NSTRIP):
                            for gi in range(NG):
                                v.wait_ge(f4.h, f40 + NG * s + gi + 1)
                                sl = zt[:, gi * D + s * p.SW:
                                        gi * D + (s + 1) * p.SW]
                                v42.inc(v.tensor_add(
                                    sl, PS[4 * (s % 2) + gi][:, 0:p.SW], sl))
                        for gi in range(NG):
                            v.wait_ge(v42.h,
                                      v420 + NG * (p.NSTRIP - 1) + gi + 1)
                            if gi >= 1:
                                v.wait_ge(al4.h, al40 + gi)
                            else:
                                v.wait_ge(as_f.h, asf0 + NG)
                            nst = D // 512
                            for i in range(nst):
                                ii = v.bn_stats(stats[:, 6 * i:6 * (i + 1)],
                                                zt[:, gi * D + 512 * i:
                                                   gi * D + 512 * (i + 1)])
                            vedge(v, ii)
                            vl2b.inc(v.bn_aggr(aggr[:, 0:2],
                                               stats[:, 0:6 * nst]))
                            v.wait_ge(vl2b.h, vl2b.n)
                            v.wait_ge(al1b.h, al1b0 + gi + 1)
                            rec = v.reciprocal(scal[:, 5:6], scal[:, 4:5])
                            nmop = v.tensor_single_scalar(scal[:, 0:1],
                                                          aggr[:, 0:1],
                                                          -1.0, ALU.mult)
                            vedge(v, rec)
                            vedge(v, nmop)
                            vl2b.inc(v.tensor_mul(scal[:, 7:8], scal[:, 0:1],
                                                  scal[:, 5:6]))
                    blk.vector(f_v)

                def fin_sy(sy):
                    for gi in range(NG):
                        sy.wait_ge(al4.h, NG * (p.NTILE - 1) + gi + 1)
                        r0 = (p.NTILE - 1) * TT + gi * 128
                        st.inc(sy.dma_start(out=y_out[r0:r0 + 128, :],
                                            in_=zt[:, gi * D:(gi + 1) * D]), 16)
                    sy.wait_ge(st.h, st.n)
                blk.sync(fin_sy)

    return nc



def _tile_x_fm(x, p):
    return np.ascontiguousarray(
        x.reshape(p.NTILE, p.TT, p.NC, 128).transpose(0, 3, 2, 1))


def _tile_wq(w, p):
    return np.ascontiguousarray(
        w.reshape(p.NC, 128, p.NC, 128).transpose(0, 3, 2, 1))


def _tile_wkv(w, p):
    return np.ascontiguousarray(
        w.reshape(w.shape[0], p.NC, 128).transpose(2, 1, 0))


def _tile_strip(w, p):
    return np.ascontiguousarray(
        w.reshape(p.NSTRIP, p.SW, p.NC, 128).transpose(0, 3, 2, 1))


def _tile_rw(w, p):
    return np.ascontiguousarray(
        w.reshape(p.E, p.NC, 128).transpose(2, 1, 0))


def _tile_ab(la, lb, p):
    a = la.transpose(1, 0, 2).reshape(p.D, p.E * p.R)
    a = a.reshape(p.NC, 128, p.ER).transpose(1, 0, 2).reshape(128, p.NC * p.ER)
    b = lb.reshape(p.ER, p.D)
    return np.ascontiguousarray(np.concatenate([a, b], axis=1))


def make_consts(p):
    cst_bf = np.zeros((128, 832), dtype=ml_dtypes.bfloat16)
    cst_bf[:, 0] = 1.0
    cst_bf[:, 704:832] = np.eye(128, dtype=np.float32).astype(ml_dtypes.bfloat16)
    cst_f = np.zeros((128, 258), dtype=np.float32)
    cst_f[:, 256] = 1e-5
    cst_f[:, 0:128] = np.eye(128, dtype=np.float32)
    emat = np.repeat(np.eye(p.E, dtype=np.float32), p.R, axis=1)
    cst_f[0:p.E, 128:256] = emat
    return cst_bf, cst_f


def make_core_inputs(p, x_core, wq, bq, wk, bk, wv, bv, wo, bo,
                     base_w, router_w, lora_a, lora_b):
    cst_bf, cst_f = make_consts(p)
    return {
        "x_fmt": _tile_x_fm(x_core, p),
        "x_tm": np.ascontiguousarray(x_core),
        "wq_t": _tile_wq(wq, p),
        "wk_t": _tile_wkv(wk, p),
        "wv_t": _tile_wkv(wv, p),
        "wo_t": _tile_strip(wo, p),
        "bw_t": _tile_strip(base_w, p),
        "rw_t": _tile_rw(router_w, p),
        "ab_t": _tile_ab(lora_a, lora_b, p),
        "bq_fm": np.ascontiguousarray(bq.reshape(p.NC, 128).T),
        "bk_bc": np.ascontiguousarray(
            np.broadcast_to(bk[None, :], (128, p.DKV))),
        "bv_bc": np.ascontiguousarray(
            np.broadcast_to(bv[None, :], (128, p.DKV))),
        "bo_bc": np.ascontiguousarray(
            np.broadcast_to(bo[None, :], (128, p.D))),
        "cst_bf": cst_bf,
        "cst_f": cst_f,
    }


def kernel(L0_attn_hidden_states, L0_attn_Wq, L0_attn_bq, L0_attn_Wk,
           L0_attn_bk, L0_attn_Wv, L0_attn_bv, L0_attn_Wo, L0_attn_bo,
           L0_ffn_base_W, L0_ffn_router_W, L0_ffn_lora_A, L0_ffn_lora_B):
    p = P()
    x = np.asarray(L0_attn_hidden_states, dtype=np.float32).reshape(16384, 4096)
    args = [np.asarray(a, dtype=np.float32) for a in (
        L0_attn_Wq, L0_attn_bq, L0_attn_Wk, L0_attn_bk, L0_attn_Wv,
        L0_attn_bv, L0_attn_Wo, L0_attn_bo, L0_ffn_base_W, L0_ffn_router_W,
        L0_ffn_lora_A, L0_ffn_lora_B)]
    wq, bq, wk, bk, wv, bv, wo, bo, base_w, router_w, lora_a, lora_b = args

    nc = build_program(p)
    in_maps = []
    for c in range(p.NCORE):
        xc = x[c * p.TOK:(c + 1) * p.TOK]
        in_maps.append(make_core_inputs(p, xc, wq, bq, wk, bk, wv, bv, wo, bo,
                                        base_w, router_w, lora_a, lora_b))
    res = run_bass_kernel_spmd(nc, in_maps, core_ids=list(range(p.NCORE)))
    out = np.concatenate([np.asarray(r["y"]) for r in res.results], axis=0)
    return out.reshape(1, 16384, 4096).astype(np.float32)


# version 18
# speedup vs baseline: 1.0055x; 1.0055x over previous
import numpy as np
import ml_dtypes
from contextlib import ExitStack

import concourse.bass as bass
import concourse.mybir as mybir
from concourse.bass_utils import run_bass_kernel_spmd

F32 = mybir.dt.float32
BF16 = mybir.dt.bfloat16
AF = mybir.ActivationFunctionType
ALU = mybir.AluOpType
AX = mybir.AxisListType


class P:

    def __init__(self, ncore=8, tok=2048, d=4096, dkv=1024, tt=512):
        self.NCORE = ncore
        self.TOK = tok
        self.D = d
        self.DKV = dkv
        self.TT = tt
        self.NC = d // 128
        self.KVH = dkv // 128
        self.NREP = self.NC // self.KVH
        self.NTILE = tok // tt
        self.NG = tt // 128
        self.NS1G = tok // 128
        self.NSTRIP = max(1, d // 512)
        self.SW = d // self.NSTRIP
        self.CH = self.NC // 2
        self.NKC = max(1, dkv // 512)
        self.KW = dkv // self.NKC
        self.E, self.R, self.ER = 8, 16, 128
        self.LNP = max(1, d // 1024)
        self.LPW = d // self.LNP
        self.NQT = self.LPW // 128
        self.NB = (self.NQT + 3) // 4


class Sem:
    def __init__(self, ctx, nc, name):
        self.h = ctx.enter_context(nc.semaphore(name))
        self.n = 0

    def inc(self, inst, k=1):
        inst.then_inc(self.h, k)
        self.n += k
        return self.n


def build_program(p: P):
    nc = bass.Bass("TRN2", target_bir_lowering=False)
    NC, TT, NG, D = p.NC, p.TT, p.NG, p.D
    NSG = NG * p.NSTRIP

    x_fmt = nc.dram_tensor("x_fmt", [p.NTILE, 128, NC, TT], F32, kind="ExternalInput")
    x_tm = nc.dram_tensor("x_tm", [p.TOK, D], F32, kind="ExternalInput")
    wq_t = nc.dram_tensor("wq_t", [NC, 128, NC, 128], F32, kind="ExternalInput")
    wk_t = nc.dram_tensor("wk_t", [128, NC, p.DKV], F32, kind="ExternalInput")
    wv_t = nc.dram_tensor("wv_t", [128, NC, p.DKV], F32, kind="ExternalInput")
    wo_t = nc.dram_tensor("wo_t", [p.NSTRIP, 128, NC, p.SW], F32, kind="ExternalInput")
    bw_t = nc.dram_tensor("bw_t", [p.NSTRIP, 128, NC, p.SW], F32, kind="ExternalInput")
    rw_t = nc.dram_tensor("rw_t", [128, NC, p.E], F32, kind="ExternalInput")
    ab_t = nc.dram_tensor("ab_t", [128, NC * 128 + D], F32, kind="ExternalInput")
    bq_fm_d = nc.dram_tensor("bq_fm", [128, NC], F32, kind="ExternalInput")
    bk_bc_d = nc.dram_tensor("bk_bc", [128, p.DKV], F32, kind="ExternalInput")
    bv_bc_d = nc.dram_tensor("bv_bc", [128, p.DKV], F32, kind="ExternalInput")
    bo_bc_d = nc.dram_tensor("bo_bc", [128, D], F32, kind="ExternalInput")
    cst_bf_d = nc.dram_tensor("cst_bf", [128, 832], BF16, kind="ExternalInput")
    cst_f_d = nc.dram_tensor("cst_f", [128, 258], F32, kind="ExternalInput")
    y_out = nc.dram_tensor("y", [p.TOK, D], F32, kind="ExternalOutput")

    kv_loc = nc.dram_tensor("kv_loc", [p.DKV + p.KVH, 128], F32)
    kv_glob = nc.dram_tensor("kv_glob", [p.DKV + p.KVH, 128], F32)

    with ExitStack() as glob:
        gsb = lambda n, s, d: glob.enter_context(nc.sbuf_tensor(n, s, d))
        cst_bf = gsb("cst_bf_s", [128, 832], BF16)
        cst_f = gsb("cst_f_s", [128, 258], F32)
        eps_ln = None
        bq_fm = gsb("bq_fm_s", [128, NC], F32)
        nbq_fm = gsb("nbq_fm_s", [128, NC], F32)
        rw_bf = gsb("rw_bf_s", [128, NC * p.E], BF16)
        kvt_bf = gsb("kvt_bf_s", [128, p.DKV], BF16)
        inv_nm = gsb("inv_nm_s", [128, p.KVH], F32)
        ones_col = cst_bf[:, 0:1]
        zrow_bf = cst_bf[0:1, 1:129]
        z512_bf = cst_bf[0:1, 129:641]
        ident_bf = cst_bf[:, 704:832]
        ident_f = cst_f[:, 0:128]
        e_mat = cst_f[0:8, 128:256]
        eps_ln = cst_f[:, 256:257]

        cc = Sem(glob, nc, "cc")
        kvl = Sem(glob, nc, "kvl")
        abl = Sem(glob, nc, "abl")
        t_n = Sem(glob, nc, "t_n")
        v_n = Sem(glob, nc, "v_n")

        with ExitStack() as ca:
            asb = lambda n, s, d: ca.enter_context(nc.sbuf_tensor(n, s, d))
            aps = lambda n, s: ca.enter_context(nc.psum_tensor(n, s, F32))
            wk_sb = asb("wk_sb", [128, NC * p.DKV], BF16)
            wv_sb = asb("wv_sb", [128, NC * p.DKV], BF16)
            bk_bc = asb("bk_bc_s", [128, p.DKV], F32)
            bv_bc = asb("bv_bc_s", [128, p.DKV], F32)
            xs = [asb(f"xs{i}", [128, NC * 128], BF16) for i in range(2)]
            kt1 = asb("kt1", [128, p.DKV], F32)
            kt2 = asb("kt2", [128, p.DKV], F32)
            ke = asb("ke", [128, p.DKV], F32)
            k_bf = asb("k_bf", [128, p.DKV], BF16)
            v_bf = asb("v_bf", [128, p.DKV], BF16)
            kv_sb = asb("kv_sb", [128, p.DKV], F32)
            nm_sb = asb("nm_sb", [1, p.DKV], F32)
            PW = max(512, p.DKV)
            pk = aps("pk", [128, PW])
            pv = aps("pv", [128, PW])
            pkv = aps("pkv", [128, PW])
            pn = aps("pn", [128, PW])

            ld0 = Sem(ca, nc, "ld0")
            wks = [Sem(ca, nc, f"wks{i}") for i in range(8)]
            wvs = [Sem(ca, nc, f"wvs{i}") for i in range(8)]
            xsem = [Sem(ca, nc, f"xs_s{i}") for i in range(2)]
            tk = Sem(ca, nc, "tk")
            tv = Sem(ca, nc, "tv")
            tkv = Sem(ca, nc, "tkv")
            vk1 = Sem(ca, nc, "vk1")
            vk2 = Sem(ca, nc, "vk2")
            vkb = Sem(ca, nc, "vkb")
            vv1 = Sem(ca, nc, "vv1")
            as1 = Sem(ca, nc, "as1")
            vend = Sem(ca, nc, "vend")
            s1st = Sem(ca, nc, "s1st")

            with nc.Block() as blk:
                def a_g(g):
                    for gi in range(min(2, p.NS1G)):
                        tau, off = gi // NG, (gi % NG) * 128
                        src = x_fmt[tau][:, :, off:off + 128]
                        dst = xs[gi % 2].ap().rearrange("p (c t) -> p c t", c=NC)
                        xsem[gi % 2].inc(g.dma_start(out=dst, in_=src), 16)
                    ld0.inc(g.dma_start(out=cst_bf[:, :], in_=cst_bf_d.ap()), 16)
                    ld0.inc(g.dma_start(out=cst_f[:, :], in_=cst_f_d.ap()), 16)
                    ld0.inc(g.dma_start(out=bq_fm[:, :], in_=bq_fm_d.ap()), 16)
                    ld0.inc(g.dma_start(out=bk_bc[:, :], in_=bk_bc_d.ap()), 16)
                    ld0.inc(g.dma_start(out=bv_bc[:, :], in_=bv_bc_d.ap()), 16)
                    ld0.inc(g.dma_start(
                        out=rw_bf[:, :],
                        in_=rw_t.ap().rearrange("p c e -> p (c e)")), 16)
                    qn = max(1, NC // 8)
                    for qq in range(0, NC, qn):
                        wks[qq // qn].inc(g.dma_start(
                            out=wk_sb[:, qq * p.DKV:(qq + qn) * p.DKV],
                            in_=wk_t[:, qq:qq + qn, :].rearrange(
                                "p c o -> p (c o)")), 16)
                        wvs[qq // qn].inc(g.dma_start(
                            out=wv_sb[:, qq * p.DKV:(qq + qn) * p.DKV],
                            in_=wv_t[:, qq:qq + qn, :].rearrange(
                                "p c o -> p (c o)")), 16)
                    for gi in range(2, p.NS1G):
                        if gi >= 2:
                            g.wait_ge(tv.h, gi - 1)
                        tau, off = gi // NG, (gi % NG) * 128
                        src = x_fmt[tau][:, :, off:off + 128]
                        dst = xs[gi % 2].ap().rearrange("p (c t) -> p c t", c=NC)
                        xsem[gi % 2].inc(g.dma_start(out=dst, in_=src), 16)
                    g.wait_ge(s1st.h, 32)
                    cc.inc(g.collective_compute(
                        "AllReduce", ALU.add,
                        replica_groups=[list(range(p.NCORE))],
                        ins=[kv_loc.ap().opt()],
                        outs=[kv_glob.ap().opt()]))
                blk.gpsimd(a_g)

                def a_t(t):
                    t.wait_ge(ld0.h, 96)
                    for ps in (pkv, pn):
                        for hh in range(PW // 512):
                            t.matmul(ps[:, hh * 512:(hh + 1) * 512],
                                     zrow_bf, z512_bf, start=True, stop=False)
                    for gi in range(p.NS1G):
                        par = gi % 2
                        t.wait_ge(xsem[par].h, 16 * (gi // 2 + 1))
                        if gi >= 1:
                            t.wait_ge(vk1.h, gi)
                        qn = max(1, NC // 8)
                        for c in range(NC):
                            if gi == 0 and c % qn == 0:
                                t.wait_ge(wks[c // qn].h, 16)
                            for hh in range(p.NKC):
                                inst = t.matmul(
                                    pk[:, hh * p.KW:(hh + 1) * p.KW],
                                    xs[par][:, c * 128:(c + 1) * 128],
                                    wk_sb[:, c * p.DKV + hh * p.KW:
                                          c * p.DKV + (hh + 1) * p.KW],
                                    start=(c == 0), stop=(c == NC - 1))
                        tk.inc(inst)
                        if gi >= 1:
                            t.wait_ge(vv1.h, gi)
                        for c in range(NC):
                            if gi == 0 and c % qn == 0:
                                t.wait_ge(wvs[c // qn].h, 16)
                            for hh in range(p.NKC):
                                inst = t.matmul(
                                    pv[:, hh * p.KW:(hh + 1) * p.KW],
                                    xs[par][:, c * 128:(c + 1) * 128],
                                    wv_sb[:, c * p.DKV + hh * p.KW:
                                          c * p.DKV + (hh + 1) * p.KW],
                                    start=(c == 0), stop=(c == NC - 1))
                        tv.inc(inst)
                        t.wait_ge(vkb.h, gi + 1)
                        t.wait_ge(vv1.h, gi + 1)
                        for h in range(p.KVH):
                            last_in_bank = (h % 4 == 3) or (h == p.KVH - 1)
                            inst = t.matmul(pkv[:, h * 128:(h + 1) * 128],
                                            k_bf[:, h * 128:(h + 1) * 128],
                                            v_bf[:, h * 128:(h + 1) * 128],
                                            start=False,
                                            stop=(gi == p.NS1G - 1
                                                  and last_in_bank))
                        for hh in range(p.NKC):
                            inst = t.matmul(pn[0:1, hh * p.KW:(hh + 1) * p.KW],
                                            ones_col,
                                            k_bf[:, hh * p.KW:(hh + 1) * p.KW],
                                            start=False, stop=False)
                        tkv.inc(inst)
                    for hh in range(PW // 512):
                        tkv.inc(t.matmul(pn[:, hh * 512:(hh + 1) * 512],
                                         zrow_bf, z512_bf,
                                         start=False, stop=True))
                blk.tensor(a_t)

                def a_s(s):
                    for gi in range(p.NS1G):
                        s.wait_ge(vk2.h, gi + 1)
                        as1.inc(s.activation(ke[:, :], kt2[:, :], AF.Exp))
                blk.scalar(a_s)

                def a_v(v):
                    v.wait_ge(ld0.h, 96)
                    v.tensor_single_scalar(nbq_fm[:, :], bq_fm[:, :],
                                           -1.0, ALU.mult)
                    for gi in range(p.NS1G):
                        v.wait_ge(tk.h, gi + 1)
                        vk1.inc(v.tensor_add(kt1[:, :], pk[:, 0:p.DKV], bk_bc[:, :]))
                        v.wait_ge(vk1.h, vk1.n)
                        if gi >= 1:
                            v.wait_ge(as1.h, gi)
                        vk2.inc(v.tensor_single_scalar(kt2[:, :], kt1[:, :],
                                                       0.0, ALU.min))
                        v.wait_ge(as1.h, gi + 1)
                        if gi >= 1:
                            v.wait_ge(tkv.h, gi)
                        vkb.inc(v.scalar_tensor_tensor(k_bf[:, :], kt1[:, :], 0.0,
                                                       ke[:, :], ALU.max, ALU.add))
                        v.wait_ge(tv.h, gi + 1)
                        vv1.inc(v.tensor_add(v_bf[:, :], pv[:, 0:p.DKV],
                                             bv_bc[:, :]))
                    v.wait_ge(tkv.h, p.NS1G + PW // 512)
                    vend.inc(v.tensor_copy(kv_sb[:, :], pkv[:, 0:p.DKV]))
                    vend.inc(v.tensor_copy(nm_sb[0:1, :], pn[0:1, 0:p.DKV]))
                blk.vector(a_v)

                def a_sy(sy):
                    sy.wait_ge(vend.h, 2)
                    src = kv_sb.ap().rearrange("p (h v) -> p h v", h=p.KVH)
                    dst = kv_loc[0:p.DKV].rearrange("(h q) v -> q h v", q=128)
                    s1st.inc(sy.dma_start(out=dst, in_=src), 16)
                    src2 = nm_sb.ap().rearrange("a (h v) -> a h v", h=p.KVH)
                    dst2 = kv_loc[p.DKV:p.DKV + p.KVH].rearrange(
                        "(a h) v -> a h v", a=1)
                    s1st.inc(sy.dma_start(out=dst2, in_=src2), 16)
                    sy.wait_ge(s1st.h, 32)
                blk.sync(a_sy)

        with ExitStack() as cb:
            bsb = lambda n, s, d: cb.enter_context(nc.sbuf_tensor(n, s, d))
            bps = lambda n, s: cb.enter_context(nc.psum_tensor(n, s, F32))
            bufAB = [bsb(f"bufab{i}", [128, NC * TT], BF16) for i in range(2)]
            zt = bsb("zt", [128, NG * D], F32)
            arena = [bsb(f"wa{i}", [128, NC * 128], BF16) for i in range(5)]
            ab_sb = bsb("ab_sb", [128, NC * 128 + D], BF16)
            bo_s = [bsb(f"bo{i}", [128, p.SW], F32) for i in range(2)]
            qy_bf = bsb("qy_bf", [128, max(2 * TT, p.LPW)], BF16)
            t2s = bsb("t2s", [128, TT], F32)
            t1r = bsb("t1r", [128, TT], F32)
            e_t = bsb("e_t", [128, TT], F32)
            tvs = bsb("tvs", [128, p.SW], F32)

            l_tm = bsb("l_tm", [128, p.E], F32)
            prb = bsb("prb", [128, p.E], F32)
            msk = bsb("msk", [128, p.E], F32)
            wfin = bsb("wfin", [128, p.E], F32)

            midw = bsb("midw", [128, TT], BF16)
            lg_sb = e_t[0:p.E, 0:TT]
            wrow = t1r[0:p.E, 0:TT]
            wexp = t2s
            stats = bsb("stats", [128, 6 * (D // 512)], F32)
            stats4 = bsb("stats4", [128, NG * 6 * (D // 512)], F32)
            aggr = bsb("aggr", [128, 2], F32)
            scal = bsb("scal", [128, 8], F32)
            nm8 = bsb("nm8", [p.KVH, 128], F32)
            nmt = bsb("nmt", [128, p.KVH], F32)
            PS = [bps(f"ps{i}", [128, 512]) for i in range(8)]

            w_s = [Sem(cb, nc, f"w_s{i}") for i in range(5)]
            wc_o = Sem(cb, nc, "wc_o")
            wc_b = Sem(cb, nc, "wc_b")
            xf = Sem(cb, nc, "xf")
            xfq = [Sem(cb, nc, f"xfq{i}") for i in range(4)]
            xt = [Sem(cb, nc, f"xt{i}") for i in range(NG)]
            bo_sem = [Sem(cb, nc, f"bo_sem{i}") for i in range(2)]
            st = Sem(cb, nc, "st")
            qm = Sem(cb, nc, "qm")
            am = Sem(cb, nc, "am")
            vq1 = Sem(cb, nc, "vq1")
            vqp = Sem(cb, nc, "vqp")
            vqb = Sem(cb, nc, "vqb")
            va = Sem(cb, nc, "va")
            aq1 = Sem(cb, nc, "aq1")
            om = Sem(cb, nc, "om")
            vo1 = Sem(cb, nc, "vo1")
            vo2 = Sem(cb, nc, "vo2")
            vst = Sem(cb, nc, "vst")
            vl1 = Sem(cb, nc, "vl1")
            al1 = Sem(cb, nc, "al1")
            vl2 = Sem(cb, nc, "vl2")
            al2 = Sem(cb, nc, "al2")
            vl3 = Sem(cb, nc, "vl3")
            ac = Sem(cb, nc, "ac")
            tl = Sem(cb, nc, "tl")
            vf = Sem(cb, nc, "vf")
            tf1 = Sem(cb, nc, "tf1")
            vf1 = Sem(cb, nc, "vf1")
            tf2 = Sem(cb, nc, "tf2")
            vf2 = Sem(cb, nc, "vf2")
            vf2a = Sem(cb, nc, "vf2a")
            as_f = Sem(cb, nc, "as_f")
            tf3 = Sem(cb, nc, "tf3")
            vf3 = Sem(cb, nc, "vf3")
            tf4 = Sem(cb, nc, "tf4")
            tf5 = Sem(cb, nc, "tf5")
            vf4 = Sem(cb, nc, "vf4")
            f4 = Sem(cb, nc, "f4")
            v42 = Sem(cb, nc, "v42")
            vl1b = Sem(cb, nc, "vl1b")
            al1b = Sem(cb, nc, "al1b")
            vl2b = Sem(cb, nc, "vl2b")
            al4 = Sem(cb, nc, "al4")
            vse = Sem(cb, nc, "vse")

            def vedge(v, inst):
                vse.inc(inst)
                v.wait_ge(vse.h, vse.n)

            QC = NC // 4
            BLK = NC + 8 * p.NSTRIP

            def blocks_of_tile():
                out = [("wq", oc, None) for oc in range(NC)]
                for s in range(p.NSTRIP):
                    for qq in range(4):
                        out.append(("wo", (s, qq), None))
                for s in range(p.NSTRIP):
                    for qq in range(4):
                        out.append(("bw", (s, qq), None))
                return out

            def block_src(kind, idx):
                if kind == "wq":
                    return wq_t[idx].rearrange("p c o -> p (c o)")
                s, qq = idx
                base = wo_t if kind == "wo" else bw_t
                sl = base[s].rearrange("p c o -> p c o")[
                    :, qq * QC:(qq + 1) * QC, :]
                return sl.rearrange("p c o -> p (c o)")

            def slot_of(j):
                return 4 if j == 0 else (j - 1) % 4

            def slot_count(j):
                return 1 if j == 0 else (j - 1) // 4 + 1

            PST = [0] * 4
            for jj in range(1, NC + 8 * p.NSTRIP):
                PST[(jj - 1) % 4] += 1

            def wwait(t, b, tau):
                j = b - BLK * tau
                if j == 0:
                    t.wait_ge(w_s[4].h, 16 * (tau + 1))
                else:
                    par = (j - 1) % 4
                    cnt = tau * PST[par] + (j - 1) // 4 + 1
                    t.wait_ge(w_s[par].h, 16 * cnt)

            def consumed_wait(kind, idx, ptau):
                if kind == "wq":
                    return (qm.h, NC * ptau + idx + 1)
                s, qq = idx
                if kind == "wo":
                    if qq < 3:
                        return (wc_o.h, 3 * p.NSTRIP * ptau + 3 * s + qq + 1)
                    return (om.h, NSG * ptau + NG * (s + 1))
                if qq < 3:
                    return (wc_b.h, 3 * p.NSTRIP * ptau + 3 * s + qq + 1)
                return (f4.h, NSG * ptau + NG * (s + 1))

            with nc.Block() as blk:


                tb = blocks_of_tile()
                for tau in range(p.NTILE):
                    b0 = BLK * tau
                    qm0, am0, vq10, vqp0, vqb0, va0, aq10 = (
                        qm.n, am.n, vq1.n, vqp.n, vqb.n, va.n, aq1.n)
                    om0, vo10, vo20, vst0 = om.n, vo1.n, vo2.n, vst.n
                    vl10, al10, vl20, al20, vl30, tl0, vf0, ac0 = (
                        vl1.n, al1.n, vl2.n, al2.n, vl3.n, tl.n, vf.n, ac.n)
                    tf10, vf10, tf20, vf20, vf2a0, asf0, tf30, vf30 = (
                        tf1.n, vf1.n, tf2.n, vf2.n, vf2a.n, as_f.n, tf3.n, vf3.n)
                    tf40, tf50, vf40, f40, v420 = tf4.n, tf5.n, vf4.n, f4.n, v42.n
                    vl1b0, al1b0, vl2b0, al40 = vl1b.n, al1b.n, vl2b.n, al4.n

                    XFb = bufAB[tau % 2]
                    ATb = bufAB[(tau + 1) % 2]

                    def ring_wait(g, j, tau=tau):
                        if j == 0:
                            pj, ptau = 0, tau - 1
                        elif j - 4 >= 1:
                            pj, ptau = j - 4, tau
                        else:
                            ptau = tau - 1
                            pslot = (j - 1) % 4
                            pj = max(jj for jj in range(1, BLK)
                                     if (jj - 1) % 4 == pslot)
                        if ptau < 0:
                            return
                        kind, idx, _ = tb[pj]
                        sem_h, val = consumed_wait(kind, idx, ptau)
                        g.wait_ge(sem_h, val)

                    def b_g(g, tau=tau):
                        if tau > 0:
                            g.wait_ge(om.h, NSG * tau)
                        if tau == 0:
                            qn = max(1, NC // 4)
                            for qq in range(0, NC, qn):
                                xfq[qq // qn].inc(g.dma_start(
                                    out=XFb[:, qq * TT:(qq + qn) * TT],
                                    in_=x_fmt[tau][:, qq:qq + qn, :].rearrange(
                                        "p c t -> p (c t)")), 16)
                        else:
                            src = x_fmt[tau].rearrange("p c t -> p (c t)")
                            xf.inc(g.dma_start(out=XFb[:, :], in_=src), 16)
                        for j, (kind, idx, _) in enumerate(tb):
                            if j == 0 and tau > 0:
                                continue
                            ring_wait(g, j)
                            par = slot_of(j)
                            w_s[par].inc(
                                g.dma_start(out=arena[par][:, :],
                                            in_=block_src(kind, idx)), 16)
                            if tau == 0 and j == 3:
                                g.wait_ge(cc.h, 1)
                                srck = kv_glob[0:p.DKV].rearrange(
                                    "(h q) v -> q h v", q=128)
                                dstk = kvt_bf.ap().rearrange(
                                    "p (h v) -> p h v", h=p.KVH)
                                kvl.inc(g.dma_start(out=dstk, in_=srck), 16)
                                kvl.inc(g.dma_start(
                                    out=nm8[:, :],
                                    in_=kv_glob[p.DKV:p.DKV + p.KVH]), 16)
                                abl.inc(g.dma_start(out=ab_sb[:, :],
                                                    in_=ab_t.ap()), 16)
                        if tau < p.NTILE - 1:
                            g.wait_ge(qm.h, NC * tau + 1)
                            w_s[4].inc(g.dma_start(
                                out=arena[4][:, :],
                                in_=block_src("wq", 0)), 16)
                    blk.gpsimd(b_g)

                    def b_sy(sy, tau=tau, al40=al40, vo10=vo10):
                        for gi in range(NG):
                            if tau > 0:
                                sy.wait_ge(al4.h, al40 - NG + gi + 1)
                                r0 = (tau - 1) * TT + gi * 128
                                st.inc(sy.dma_start(
                                    out=y_out[r0:r0 + 128, :],
                                    in_=zt[:, gi * D:(gi + 1) * D]), 16)
                                sy.wait_ge(st.h, st.n)
                            r0 = tau * TT + gi * 128
                            xt[gi].inc(sy.dma_start(
                                out=zt[:, gi * D:(gi + 1) * D],
                                in_=x_tm[r0:r0 + 128, :]), 16)
                        for s in range(p.NSTRIP):
                            if s >= 2:
                                sy.wait_ge(vo1.h, vo10 + NG * (s - 2) + NG)
                            bo_sem[s % 2].inc(sy.dma_start(
                                out=bo_s[s % 2][:, :],
                                in_=bo_bc_d[:, s * p.SW:(s + 1) * p.SW]), 16)
                    blk.sync(b_sy)

                    def q_t(t, tau=tau, vqp0=vqp0, vqb0=vqb0, va0=va0,
                            v420=v420, aq10=aq10):
                        if tau > 0:
                            t.wait_ge(xf.h, 16 * tau)
                            t.wait_ge(v42.h, v420)
                        for oc in range(NC):
                            b = b0 + oc
                            par = slot_of(oc)
                            wwait(t, b, tau)
                            if oc >= 2:
                                t.wait_ge(aq1.h, aq10 + 3 * (oc - 1))
                            for kc in range(NC):
                                if tau == 0 and oc == 0 and \
                                        kc % max(1, NC // 4) == 0:
                                    t.wait_ge(xfq[kc // max(1, NC // 4)].h, 16)
                                inst = t.matmul(
                                    PS[oc % 2][:, 0:TT],
                                    arena[par][:, kc * 128:(kc + 1) * 128],
                                    XFb[:, kc * TT:(kc + 1) * TT],
                                    start=(kc == 0), stop=(kc == NC - 1))
                            qm.inc(inst)
                            if tau > 0 and oc >= 1:
                                emit_attn(t, oc - 1)
                        if tau == 0:
                            t.wait_ge(kvl.h, 32)
                            t_n.inc(t.matmul(PS[7][:, 0:p.KVH], nm8[:, :],
                                             ident_f[0:p.KVH, 0:p.KVH],
                                             start=True, stop=True))
                            for j in range(NC):
                                emit_attn(t, j)
                        else:
                            emit_attn(t, NC - 1)
                    def emit_attn(t, j, tau=tau, vqb0=vqb0, va0=va0):
                        t.wait_ge(vqb.h, vqb0 + j + 1)
                        if j >= 2:
                            t.wait_ge(va.h, va0 + j - 1)
                        h = j // p.NREP
                        am.inc(t.matmul(PS[2 + j % 2][:, 0:TT],
                                        kvt_bf[:, h * 128:(h + 1) * 128],
                                        ATb[:, j * TT:(j + 1) * TT],
                                        start=True, stop=True))
                    blk.tensor(q_t)

                    def q_s(s, tau=tau, qm0=qm0, am0=am0, vqb0=vqb0,
                            va0=va0, aq10=aq10):
                        for oc in range(NC):
                            s.wait_ge(qm.h, qm0 + oc + 1)
                            aq1.inc(s.activation(t2s[:, 0:TT],
                                                 PS[oc % 2][:, 0:TT], AF.Relu,
                                                 bias=nbq_fm[:, oc:oc + 1],
                                                 scale=-1.0))
                            s.wait_ge(aq1.h, aq1.n)
                            if oc >= 1:
                                s.wait_ge(vqb.h, vqb0 + oc)
                            aq1.inc(s.activation(e_t[:, 0:TT], t2s[:, 0:TT],
                                                 AF.Exp, scale=-1.0))
                            s.wait_ge(aq1.h, aq1.n)
                            if oc >= 1:
                                s.wait_ge(vqb.h, vqb0 + oc)
                            aq1.inc(s.activation(t1r[:, 0:TT],
                                                 PS[oc % 2][:, 0:TT], AF.Relu,
                                                 bias=bq_fm[:, oc:oc + 1]))
                            if tau > 0 and oc >= 1:
                                j = oc - 1
                                s.wait_ge(am.h, am0 + j + 1)
                                h = j // p.NREP
                                va.inc(s.activation(
                                    ATb[:, j * TT:(j + 1) * TT],
                                    PS[2 + j % 2][:, 0:TT], AF.Identity,
                                    scale=inv_nm[:, h:h + 1]))
                        if tau == 0:
                            s.wait_ge(v_n.h, 2)
                            ev_js = list(range(NC))
                        else:
                            ev_js = [NC - 1]
                        for j in ev_js:
                            s.wait_ge(am.h, am0 + j + 1)
                            h = j // p.NREP
                            va.inc(s.activation(
                                ATb[:, j * TT:(j + 1) * TT],
                                PS[2 + j % 2][:, 0:TT], AF.Identity,
                                scale=inv_nm[:, h:h + 1]))
                    blk.scalar(q_s)

                    def q_v(v, tau=tau, qm0=qm0, am0=am0, aq10=aq10):
                        for oc in range(NC):
                            v.wait_ge(aq1.h, aq10 + 3 * oc + 3)
                            vqb.inc(v.tensor_add(
                                ATb[:, oc * TT:(oc + 1) * TT],
                                t1r[:, 0:TT], e_t[:, 0:TT]))
                        if tau == 0:
                            v.wait_ge(t_n.h, 1)
                            v_n.inc(v.tensor_single_scalar(
                                nmt[:, :], PS[7][:, 0:p.KVH], 1e-10, ALU.add))
                            v.wait_ge(v_n.h, v_n.n)
                            v_n.inc(v.reciprocal(inv_nm[:, :], nmt[:, :]))
                    blk.vector(q_v)

                    def o_t(t, tau=tau, va0=va0, vqp0=vqp0, vo10=vo10,
                            aq10=aq10):
                        for s in range(p.NSTRIP):
                            sp = s % 2
                            for qq in range(4):
                                b = b0 + NC + 4 * s + qq
                                par = slot_of(NC + 4 * s + qq)
                                wwait(t, b, tau)
                                if s == 0 and qq == 0:
                                    t.wait_ge(va.h, va0 + NC)
                                    t.wait_ge(aq1.h, aq10 + 3 * NC)
                                if tau == 0 and s == min(1, p.NSTRIP - 1) \
                                        and qq == 0:
                                    t.wait_ge(v_n.h, 2)
                                if s >= 2 and qq == 0:
                                    t.wait_ge(vo1.h, vo10 + NG * (s - 2) + NG)
                                for gi in range(NG):
                                    for c in range(QC):
                                        fc = qq * QC + c
                                        inst = t.matmul(
                                            PS[4 * sp + gi][:, 0:p.SW],
                                            ATb[:, fc * TT + gi * 128:
                                                fc * TT + (gi + 1) * 128],
                                            arena[par][:, c * p.SW:(c + 1) * p.SW],
                                            start=(qq == 0 and c == 0),
                                            stop=(qq == 3 and c == QC - 1))
                                    if qq == 3:
                                        om.inc(inst)
                                if qq < 3:
                                    wc_o.inc(inst)
                    blk.tensor(o_t)

                    def o_v(v, tau=tau, om0=om0):
                        nbo = (p.NSTRIP + 1) // 2
                        for s in range(p.NSTRIP):
                            for gi in range(NG):
                                v.wait_ge(om.h, om0 + NG * s + gi + 1)
                                v.wait_ge(bo_sem[s % 2].h,
                                          16 * (tau * nbo + s // 2 + 1))
                                v.wait_ge(xt[gi].h, 16 * (tau + 1))
                                vo1.inc(v.tensor_add(
                                    tvs[:, :], PS[4 * (s % 2) + gi][:, 0:p.SW],
                                    bo_s[s % 2][:, :]))
                                v.wait_ge(vo1.h, vo1.n)
                                sl = zt[:, gi * D + s * p.SW:
                                        gi * D + (s + 1) * p.SW]
                                vo2.inc(v.tensor_add(sl, tvs[:, :], sl))
                                v.wait_ge(vo2.h, vo2.n)
                                nst = D // 512
                                for i2 in range(p.SW // 512):
                                    i = (s * p.SW) // 512 + i2
                                    vst.inc(v.bn_stats(
                                        stats4[:, gi * 6 * nst + 6 * i:
                                               gi * 6 * nst + 6 * (i + 1)],
                                        zt[:, gi * D + 512 * i:
                                           gi * D + 512 * (i + 1)]))
                    blk.vector(o_v)

                    def l_v(v, tau=tau, vo20=vo20, al10=al10, al40=al40,
                            tl0=tl0, vf0=vf0):
                        nst = D // 512
                        nps = p.SW // 512
                        for gi in range(NG):
                            v.wait_ge(vst.h, vst0 + nps * (NG * (p.NSTRIP - 1)
                                                           + gi + 1))
                            if gi == 0 and tau > 0:
                                v.wait_ge(al4.h, al40)
                            vl1.inc(v.bn_aggr(aggr[:, 0:2],
                                              stats4[:, gi * 6 * nst:
                                                     (gi + 1) * 6 * nst]))
                            v.wait_ge(vl1.h, vl1.n)
                            v.wait_ge(al1.h, al10 + gi + 1)
                            rec = v.reciprocal(scal[:, gi:gi + 1],
                                               stats[:, 1:2])
                            nmop = v.tensor_single_scalar(stats[:, 0:1],
                                                          aggr[:, 0:1],
                                                          -1.0, ALU.mult)
                            vedge(v, rec)
                            vedge(v, nmop)
                            vl2.inc(v.tensor_mul(scal[:, 4 + gi:5 + gi],
                                                 stats[:, 0:1],
                                                 scal[:, gi:gi + 1]))
                            v.wait_ge(vl2.h, vl2.n)
                        dst = XFb.ap().rearrange("p (c t) -> p c t", c=NC)
                        for pc in range(p.LNP):
                            for gi in range(NG):
                                idx = pc * NG + gi
                                v.wait_ge(tl.h, tl0 + idx + 1)
                                for bi in range(p.NB):
                                    nch = min(4, p.NQT - 4 * bi)
                                    c0 = pc * p.NQT + bi * 4
                                    sl = dst[:, c0:c0 + nch,
                                             gi * 128:(gi + 1) * 128]
                                    vf.inc(v.tensor_copy(
                                        sl,
                                        PS[6 + bi][:, 0:128 * nch].rearrange(
                                            "p (c t) -> p c t", c=nch)))
                    blk.vector(l_v)

                    def l_s(s, tau=tau, vl10=vl10, vl20=vl20, tl0=tl0,
                            al20=al20):
                        for gi in range(NG):
                            s.wait_ge(vl1.h, vl10 + gi + 1)
                            al1.inc(s.activation(stats[:, 1:2], aggr[:, 1:2],
                                                 AF.Sqrt, bias=eps_ln))
                        for pc in range(p.LNP):
                            for gi in range(NG):
                                idx = pc * NG + gi
                                if pc == 0:
                                    s.wait_ge(vl2.h, vl20 + gi + 1)
                                z = zt[:, gi * D + pc * p.LPW:
                                       gi * D + (pc + 1) * p.LPW]
                                al2.inc(s.activation(
                                    z, z, AF.Identity,
                                    bias=scal[:, 4 + gi:5 + gi],
                                    scale=scal[:, gi:gi + 1]))
                                s.wait_ge(al2.h, al2.n)
                                if idx >= 2 and 2 * TT >= p.LPW:
                                    s.wait_ge(tl.h, tl0 + idx - 1)
                                elif idx >= 1 and 2 * TT < p.LPW:
                                    s.wait_ge(tl.h, tl0 + idx)
                                use2 = (2 * TT >= p.LPW) and idx % 2 == 1
                                dstq = (t2s.ap().bitcast(BF16)[:, 0:p.LPW]
                                        if use2 else qy_bf[:, 0:p.LPW])
                                ac.inc(s.activation(dstq, z, AF.Identity))
                    blk.scalar(l_s)

                    def f_t(t, tau=tau, vf0=vf0, vf10=vf10, vf20=vf20,
                            vf2a0=vf2a0, vf30=vf30, vf40=vf40, v420=v420,
                            ac0=ac0, tl0=tl0, vo20=vo20):
                        if tau == 0:
                            t.wait_ge(abl.h, 16)
                        t.wait_ge(vo2.h, vo20 + NG * p.NSTRIP)

                        def emit_chunks(pcx):
                            t.wait_ge(vf.h, vf0 + (pcx + 1) * NG * p.NB)
                            for c in range(pcx * p.NQT, (pcx + 1) * p.NQT):
                                inst = t.matmul(
                                    PS[0][0:p.E, 0:TT],
                                    rw_bf[:, c * p.E:(c + 1) * p.E],
                                    XFb[:, c * TT:(c + 1) * TT],
                                    start=(c == 0), stop=(c == NC - 1))
                                if c == NC - 1:
                                    tf1.inc(inst)
                                inst = t.matmul(
                                    PS[4][:, 0:TT],
                                    ab_sb[:, c * 128:(c + 1) * 128],
                                    XFb[:, c * TT:(c + 1) * TT],
                                    start=(c == 0), stop=(c == NC - 1))
                                if c == NC - 1:
                                    tf5.inc(inst)

                        for pc in range(p.LNP):
                            for gi in range(NG):
                                idx = pc * NG + gi
                                t.wait_ge(ac.h, ac0 + idx + 1)
                                if idx >= 1:
                                    t.wait_ge(vf.h, vf0 + idx * p.NB)
                                use2 = (2 * TT >= p.LPW) and idx % 2 == 1
                                srcq = (t2s.ap().bitcast(BF16) if use2
                                        else qy_bf)
                                for j in range(p.NQT):
                                    bi, jo = j // 4, j % 4
                                    inst = t.matmul(
                                        PS[6 + bi][:, jo * 128:(jo + 1) * 128],
                                        srcq[:, j * 128:(j + 1) * 128],
                                        ident_bf, start=True, stop=True)
                                tl.inc(inst)
                            if pc >= 1:
                                emit_chunks(pc - 1)
                        emit_chunks(p.LNP - 1)
                        def emit_tf2(gi):
                            if gi == 0:
                                t.wait_ge(vf1.h, vf10 + 1)
                            else:
                                t.wait_ge(vf2a.h, vf2a0 + gi)
                            tf2.inc(t.matmul(
                                PS[5][:, 0:p.E],
                                lg_sb[:, gi * 128:(gi + 1) * 128],
                                ident_f[0:p.E, 0:p.E],
                                start=True, stop=True))

                        def emit_tf3(gi):
                            t.wait_ge(vf2.h, vf20 + gi + 1)
                            tf3.inc(t.matmul(
                                PS[6][0:p.E, gi * 128:(gi + 1) * 128],
                                wfin[:, :], ident_f, start=True, stop=True))

                        def emit_tf4():
                            t.wait_ge(vf3.h, vf30 + 1)
                            tf4.inc(t.matmul(PS[7][:, 0:TT], e_mat,
                                             wrow[:, 0:TT],
                                             start=True, stop=True))
                        for s in range(p.NSTRIP):
                            sp = s % 2
                            for qq in range(4):
                                b = b0 + NC + 4 * p.NSTRIP + 4 * s + qq
                                par = slot_of(NC + 4 * p.NSTRIP + 4 * s + qq)
                                wwait(t, b, tau)
                                if s == 0 and qq == 0:
                                    t.wait_ge(vf1.h, vf10 + 1)
                                if s == 0 and qq == 1:
                                    emit_tf2(0)
                                    emit_tf3(0)
                                    if NG >= 2:
                                        emit_tf2(1)
                                if s == 0 and qq == 2:
                                    if NG >= 2:
                                        emit_tf3(1)
                                    for g2i in range(2, NG):
                                        emit_tf2(g2i)
                                        emit_tf3(g2i)
                                if s == 0 and qq == 3:
                                    emit_tf4()
                                if s == 1 and qq == 0:
                                    t.wait_ge(vf4.h, vf40 + 2)
                                if s >= 2 and qq == 0:
                                    t.wait_ge(v42.h, v420 + NG * (s - 2) + NG)
                                for gi in range(NG):
                                    for c in range(QC):
                                        fc = qq * QC + c
                                        inst = t.matmul(
                                            PS[4 * sp + gi][:, 0:p.SW],
                                            XFb[:, fc * TT + gi * 128:
                                                fc * TT + (gi + 1) * 128],
                                            arena[par][:, c * p.SW:(c + 1) * p.SW],
                                            start=(qq == 0 and c == 0),
                                            stop=False)
                                    if qq == 3:
                                        if s == 0 and gi == 0:
                                            t.wait_ge(vf4.h, vf40 + 2)
                                        f4.inc(t.matmul(
                                            PS[4 * sp + gi][:, 0:p.SW],
                                            midw[:, gi * 128:(gi + 1) * 128],
                                            ab_sb[:, NC * 128 + s * p.SW:
                                                  NC * 128 + (s + 1) * p.SW],
                                            start=False, stop=True))
                                if qq < 3:
                                    wc_b.inc(inst)
                    blk.tensor(f_t)

                    def f_s(s, tau=tau, vl1b0=vl1b0, vl2b0=vl2b0):
                        for gi in range(NG):
                            s.wait_ge(vl1b.h, vl1b0 + gi + 1)
                            as_f.inc(s.activation(prb[:, :], l_tm[:, :], AF.Exp,
                                                  bias=scal[:, 0:1],
                                                  accum_out=scal[:, 1:2]))
                        for gi in range(NG):
                            s.wait_ge(vl2b.h, vl2b0 + 2 * gi + 1)
                            al1b.inc(s.activation(scal[:, 4:5], aggr[:, 1:2],
                                                  AF.Sqrt, bias=eps_ln))
                            s.wait_ge(vl2b.h, vl2b0 + 2 * gi + 2)
                            al4.inc(s.activation(zt[:, gi * D:(gi + 1) * D],
                                                 zt[:, gi * D:(gi + 1) * D],
                                                 AF.Identity, bias=scal[:, 7:8],
                                                 scale=scal[:, 5:6]))
                    blk.scalar(f_s)

                    def f_v(v, tau=tau, tf10=tf10, tf20=tf20, tf30=tf30,
                            tf40=tf40, tf50=tf50, f40=f40, v420=v420,
                            asf0=asf0, al20=al20, al1b0=al1b0, al40=al40,
                            vf2a0=vf2a0):
                        v.wait_ge(tf1.h, tf10 + 1)
                        v.wait_ge(al2.h, al20 + NG * p.LNP)
                        vf1.inc(v.tensor_copy(lg_sb[:, :], PS[0][0:p.E, 0:TT]))
                        for gi in range(NG):
                            v.wait_ge(tf2.h, tf20 + gi + 1)
                            vf2a.inc(v.tensor_copy(l_tm[:, :], PS[5][:, 0:p.E]))
                            v.wait_ge(vf2a.h, vf2a.n)
                            vl1b.inc(v.tensor_reduce(scal[:, 0:1], l_tm[:, :],
                                                     AX.X, ALU.max, negate=True))
                            v.wait_ge(as_f.h, asf0 + gi + 1)
                            rec = v.reciprocal(scal[:, 2:3], scal[:, 1:2])
                            vedge(v, rec)
                            vedge(v, v.tensor_scalar(prb[:, :], prb[:, :],
                                                     scal[:, 2:3], None,
                                                     ALU.mult))
                            m1 = v.tensor_reduce(scal[:, 3:4], prb[:, :],
                                                 AX.X, ALU.max)
                            vedge(v, m1)
                            vedge(v, v.tensor_scalar(msk[:, :], prb[:, :],
                                                     scal[:, 3:4], None,
                                                     ALU.is_equal))
                            vedge(v, v.scalar_tensor_tensor(
                                msk[:, :], msk[:, :], -2.0,
                                prb[:, :], ALU.mult, ALU.add))
                            m2 = v.tensor_reduce(scal[:, 4:5], msk[:, :],
                                                 AX.X, ALU.max)
                            vedge(v, m2)
                            vedge(v, v.tensor_scalar(msk[:, :], prb[:, :],
                                                     scal[:, 4:5], None,
                                                     ALU.is_ge))
                            dn = v.tensor_add(scal[:, 6:7], scal[:, 3:4],
                                              scal[:, 4:5])
                            vedge(v, dn)
                            rd = v.reciprocal(scal[:, 5:6], scal[:, 6:7])
                            vedge(v, rd)
                            sc2 = v.tensor_single_scalar(scal[:, 6:7],
                                                         scal[:, 5:6],
                                                         2.0, ALU.mult)
                            vedge(v, sc2)
                            vedge(v, v.tensor_mul(wfin[:, :], prb[:, :],
                                                  msk[:, :]))
                            vf2.inc(v.tensor_scalar(wfin[:, :], wfin[:, :],
                                                    scal[:, 6:7], None,
                                                    ALU.mult))
                        v.wait_ge(tf3.h, tf30 + NG)
                        vf3.inc(v.tensor_copy(wrow[:, 0:TT],
                                              PS[6][0:p.E, 0:TT]))
                        v.wait_ge(tf4.h, tf40 + 1)
                        vf4.inc(v.tensor_copy(wexp[:, 0:TT], PS[7][:, 0:TT]))
                        v.wait_ge(vf4.h, vf4.n)
                        v.wait_ge(tf5.h, tf50 + 1)
                        vf4.inc(v.tensor_mul(midw[:, 0:TT], PS[4][:, 0:TT],
                                             wexp[:, 0:TT]))
                        for s in range(p.NSTRIP):
                            for gi in range(NG):
                                v.wait_ge(f4.h, f40 + NG * s + gi + 1)
                                sl = zt[:, gi * D + s * p.SW:
                                        gi * D + (s + 1) * p.SW]
                                v42.inc(v.tensor_add(
                                    sl, PS[4 * (s % 2) + gi][:, 0:p.SW], sl))
                        for gi in range(NG):
                            v.wait_ge(v42.h,
                                      v420 + NG * (p.NSTRIP - 1) + gi + 1)
                            if gi >= 1:
                                v.wait_ge(al4.h, al40 + gi)
                            else:
                                v.wait_ge(as_f.h, asf0 + NG)
                            nst = D // 512
                            for i in range(nst):
                                ii = v.bn_stats(stats[:, 6 * i:6 * (i + 1)],
                                                zt[:, gi * D + 512 * i:
                                                   gi * D + 512 * (i + 1)])
                            vedge(v, ii)
                            vl2b.inc(v.bn_aggr(aggr[:, 0:2],
                                               stats[:, 0:6 * nst]))
                            v.wait_ge(vl2b.h, vl2b.n)
                            v.wait_ge(al1b.h, al1b0 + gi + 1)
                            rec = v.reciprocal(scal[:, 5:6], scal[:, 4:5])
                            nmop = v.tensor_single_scalar(scal[:, 0:1],
                                                          aggr[:, 0:1],
                                                          -1.0, ALU.mult)
                            vedge(v, rec)
                            vedge(v, nmop)
                            vl2b.inc(v.tensor_mul(scal[:, 7:8], scal[:, 0:1],
                                                  scal[:, 5:6]))
                    blk.vector(f_v)

                def fin_sy(sy):
                    for gi in range(NG):
                        sy.wait_ge(al4.h, NG * (p.NTILE - 1) + gi + 1)
                        r0 = (p.NTILE - 1) * TT + gi * 128
                        st.inc(sy.dma_start(out=y_out[r0:r0 + 128, :],
                                            in_=zt[:, gi * D:(gi + 1) * D]), 16)
                    sy.wait_ge(st.h, st.n)
                blk.sync(fin_sy)

    return nc



def _tile_x_fm(x, p):
    return np.ascontiguousarray(
        x.reshape(p.NTILE, p.TT, p.NC, 128).transpose(0, 3, 2, 1))


def _tile_wq(w, p):
    return np.ascontiguousarray(
        w.reshape(p.NC, 128, p.NC, 128).transpose(0, 3, 2, 1))


def _tile_wkv(w, p):
    return np.ascontiguousarray(
        w.reshape(w.shape[0], p.NC, 128).transpose(2, 1, 0))


def _tile_strip(w, p):
    return np.ascontiguousarray(
        w.reshape(p.NSTRIP, p.SW, p.NC, 128).transpose(0, 3, 2, 1))


def _tile_rw(w, p):
    return np.ascontiguousarray(
        w.reshape(p.E, p.NC, 128).transpose(2, 1, 0))


def _tile_ab(la, lb, p):
    a = la.transpose(1, 0, 2).reshape(p.D, p.E * p.R)
    a = a.reshape(p.NC, 128, p.ER).transpose(1, 0, 2).reshape(128, p.NC * p.ER)
    b = lb.reshape(p.ER, p.D)
    return np.ascontiguousarray(np.concatenate([a, b], axis=1))


def make_consts(p):
    cst_bf = np.zeros((128, 832), dtype=ml_dtypes.bfloat16)
    cst_bf[:, 0] = 1.0
    cst_bf[:, 704:832] = np.eye(128, dtype=np.float32).astype(ml_dtypes.bfloat16)
    cst_f = np.zeros((128, 258), dtype=np.float32)
    cst_f[:, 256] = 1e-5
    cst_f[:, 0:128] = np.eye(128, dtype=np.float32)
    emat = np.repeat(np.eye(p.E, dtype=np.float32), p.R, axis=1)
    cst_f[0:p.E, 128:256] = emat
    return cst_bf, cst_f


def make_core_inputs(p, x_core, wq, bq, wk, bk, wv, bv, wo, bo,
                     base_w, router_w, lora_a, lora_b):
    cst_bf, cst_f = make_consts(p)
    return {
        "x_fmt": _tile_x_fm(x_core, p),
        "x_tm": np.ascontiguousarray(x_core),
        "wq_t": _tile_wq(wq, p),
        "wk_t": _tile_wkv(wk, p),
        "wv_t": _tile_wkv(wv, p),
        "wo_t": _tile_strip(wo, p),
        "bw_t": _tile_strip(base_w, p),
        "rw_t": _tile_rw(router_w, p),
        "ab_t": _tile_ab(lora_a, lora_b, p),
        "bq_fm": np.ascontiguousarray(bq.reshape(p.NC, 128).T),
        "bk_bc": np.ascontiguousarray(
            np.broadcast_to(bk[None, :], (128, p.DKV))),
        "bv_bc": np.ascontiguousarray(
            np.broadcast_to(bv[None, :], (128, p.DKV))),
        "bo_bc": np.ascontiguousarray(
            np.broadcast_to(bo[None, :], (128, p.D))),
        "cst_bf": cst_bf,
        "cst_f": cst_f,
    }


def kernel(L0_attn_hidden_states, L0_attn_Wq, L0_attn_bq, L0_attn_Wk,
           L0_attn_bk, L0_attn_Wv, L0_attn_bv, L0_attn_Wo, L0_attn_bo,
           L0_ffn_base_W, L0_ffn_router_W, L0_ffn_lora_A, L0_ffn_lora_B):
    p = P()
    x = np.asarray(L0_attn_hidden_states, dtype=np.float32).reshape(16384, 4096)
    args = [np.asarray(a, dtype=np.float32) for a in (
        L0_attn_Wq, L0_attn_bq, L0_attn_Wk, L0_attn_bk, L0_attn_Wv,
        L0_attn_bv, L0_attn_Wo, L0_attn_bo, L0_ffn_base_W, L0_ffn_router_W,
        L0_ffn_lora_A, L0_ffn_lora_B)]
    wq, bq, wk, bk, wv, bv, wo, bo, base_w, router_w, lora_a, lora_b = args

    nc = build_program(p)
    in_maps = []
    for c in range(p.NCORE):
        xc = x[c * p.TOK:(c + 1) * p.TOK]
        in_maps.append(make_core_inputs(p, xc, wq, bq, wk, bk, wv, bv, wo, bo,
                                        base_w, router_w, lora_a, lora_b))
    res = run_bass_kernel_spmd(nc, in_maps, core_ids=list(range(p.NCORE)))
    out = np.concatenate([np.asarray(r["y"]) for r in res.results], axis=0)
    return out.reshape(1, 16384, 4096).astype(np.float32)


# version 19 (final)
# speedup vs baseline: 1.0071x; 1.0016x over previous
import numpy as np
import ml_dtypes
from contextlib import ExitStack

import concourse.bass as bass
import concourse.mybir as mybir
from concourse.bass_utils import run_bass_kernel_spmd

F32 = mybir.dt.float32
BF16 = mybir.dt.bfloat16
AF = mybir.ActivationFunctionType
ALU = mybir.AluOpType
AX = mybir.AxisListType


class P:

    def __init__(self, ncore=8, tok=2048, d=4096, dkv=1024, tt=512):
        self.NCORE = ncore
        self.TOK = tok
        self.D = d
        self.DKV = dkv
        self.TT = tt
        self.NC = d // 128
        self.KVH = dkv // 128
        self.NREP = self.NC // self.KVH
        self.NTILE = tok // tt
        self.NG = tt // 128
        self.NS1G = tok // 128
        self.NSTRIP = max(1, d // 512)
        self.SW = d // self.NSTRIP
        self.CH = self.NC // 2
        self.NKC = max(1, dkv // 512)
        self.KW = dkv // self.NKC
        self.E, self.R, self.ER = 8, 16, 128
        self.LNP = max(1, d // 1024)
        self.LPW = d // self.LNP
        self.NQT = self.LPW // 128
        self.NB = (self.NQT + 3) // 4


class Sem:
    def __init__(self, ctx, nc, name):
        self.h = ctx.enter_context(nc.semaphore(name))
        self.n = 0

    def inc(self, inst, k=1):
        inst.then_inc(self.h, k)
        self.n += k
        return self.n


def build_program(p: P):
    nc = bass.Bass("TRN2", target_bir_lowering=False)
    NC, TT, NG, D = p.NC, p.TT, p.NG, p.D
    NSG = NG * p.NSTRIP

    x_fmt = nc.dram_tensor("x_fmt", [p.NTILE, 128, NC, TT], F32, kind="ExternalInput")
    x_tm = nc.dram_tensor("x_tm", [p.TOK, D], F32, kind="ExternalInput")
    wq_t = nc.dram_tensor("wq_t", [NC, 128, NC, 128], F32, kind="ExternalInput")
    wk_t = nc.dram_tensor("wk_t", [128, NC, p.DKV], F32, kind="ExternalInput")
    wv_t = nc.dram_tensor("wv_t", [128, NC, p.DKV], F32, kind="ExternalInput")
    wo_t = nc.dram_tensor("wo_t", [p.NSTRIP, 128, NC, p.SW], F32, kind="ExternalInput")
    bw_t = nc.dram_tensor("bw_t", [p.NSTRIP, 128, NC, p.SW], F32, kind="ExternalInput")
    rw_t = nc.dram_tensor("rw_t", [128, NC, p.E], F32, kind="ExternalInput")
    ab_t = nc.dram_tensor("ab_t", [128, NC * 128 + D], F32, kind="ExternalInput")
    bq_fm_d = nc.dram_tensor("bq_fm", [128, NC], F32, kind="ExternalInput")
    bk_bc_d = nc.dram_tensor("bk_bc", [128, p.DKV], F32, kind="ExternalInput")
    bv_bc_d = nc.dram_tensor("bv_bc", [128, p.DKV], F32, kind="ExternalInput")
    bo_bc_d = nc.dram_tensor("bo_bc", [128, D], F32, kind="ExternalInput")
    cst_bf_d = nc.dram_tensor("cst_bf", [128, 832], BF16, kind="ExternalInput")
    cst_f_d = nc.dram_tensor("cst_f", [128, 258], F32, kind="ExternalInput")
    y_out = nc.dram_tensor("y", [p.TOK, D], F32, kind="ExternalOutput")

    kv_loc = nc.dram_tensor("kv_loc", [p.DKV + p.KVH, 128], F32)
    kv_glob = nc.dram_tensor("kv_glob", [p.DKV + p.KVH, 128], F32)

    with ExitStack() as glob:
        gsb = lambda n, s, d: glob.enter_context(nc.sbuf_tensor(n, s, d))
        cst_bf = gsb("cst_bf_s", [128, 832], BF16)
        cst_f = gsb("cst_f_s", [128, 258], F32)
        eps_ln = None
        bq_fm = gsb("bq_fm_s", [128, NC], F32)
        nbq_fm = gsb("nbq_fm_s", [128, NC], F32)
        rw_bf = gsb("rw_bf_s", [128, NC * p.E], BF16)
        kvt_bf = gsb("kvt_bf_s", [128, p.DKV], BF16)
        inv_nm = gsb("inv_nm_s", [128, p.KVH], F32)
        ones_col = cst_bf[:, 0:1]
        zrow_bf = cst_bf[0:1, 1:129]
        z512_bf = cst_bf[0:1, 129:641]
        ident_bf = cst_bf[:, 704:832]
        ident_f = cst_f[:, 0:128]
        e_mat = cst_f[0:8, 128:256]
        eps_ln = cst_f[:, 256:257]

        cc = Sem(glob, nc, "cc")
        kvl = Sem(glob, nc, "kvl")
        abl = Sem(glob, nc, "abl")
        t_n = Sem(glob, nc, "t_n")
        v_n = Sem(glob, nc, "v_n")

        with ExitStack() as ca:
            asb = lambda n, s, d: ca.enter_context(nc.sbuf_tensor(n, s, d))
            aps = lambda n, s: ca.enter_context(nc.psum_tensor(n, s, F32))
            wk_sb = asb("wk_sb", [128, NC * p.DKV], BF16)
            wv_sb = asb("wv_sb", [128, NC * p.DKV], BF16)
            bk_bc = asb("bk_bc_s", [128, p.DKV], F32)
            bv_bc = asb("bv_bc_s", [128, p.DKV], F32)
            xs = [asb(f"xs{i}", [128, NC * 128], BF16) for i in range(2)]
            kt1 = asb("kt1", [128, p.DKV], F32)
            kt2 = asb("kt2", [128, p.DKV], F32)
            ke = asb("ke", [128, p.DKV], F32)
            k_bf = asb("k_bf", [128, p.DKV], BF16)
            v_bf = asb("v_bf", [128, p.DKV], BF16)
            kv_sb = asb("kv_sb", [128, p.DKV], F32)
            nm_sb = asb("nm_sb", [1, p.DKV], F32)
            PW = max(512, p.DKV)
            pk = aps("pk", [128, PW])
            pv = aps("pv", [128, PW])
            pkv = aps("pkv", [128, PW])
            pn = aps("pn", [128, PW])

            ld0 = Sem(ca, nc, "ld0")
            wks = [Sem(ca, nc, f"wks{i}") for i in range(8)]
            wvs = [Sem(ca, nc, f"wvs{i}") for i in range(8)]
            xsem = [Sem(ca, nc, f"xs_s{i}") for i in range(2)]
            tk = Sem(ca, nc, "tk")
            tv = Sem(ca, nc, "tv")
            tkv = Sem(ca, nc, "tkv")
            vk1 = Sem(ca, nc, "vk1")
            vk2 = Sem(ca, nc, "vk2")
            vkb = Sem(ca, nc, "vkb")
            vv1 = Sem(ca, nc, "vv1")
            as1 = Sem(ca, nc, "as1")
            vend = Sem(ca, nc, "vend")
            s1st = Sem(ca, nc, "s1st")

            with nc.Block() as blk:
                def a_g(g):
                    for gi in range(min(2, p.NS1G)):
                        tau, off = gi // NG, (gi % NG) * 128
                        src = x_fmt[tau][:, :, off:off + 128]
                        dst = xs[gi % 2].ap().rearrange("p (c t) -> p c t", c=NC)
                        xsem[gi % 2].inc(g.dma_start(out=dst, in_=src), 16)
                    ld0.inc(g.dma_start(out=cst_bf[:, :], in_=cst_bf_d.ap()), 16)
                    ld0.inc(g.dma_start(out=cst_f[:, :], in_=cst_f_d.ap()), 16)
                    ld0.inc(g.dma_start(out=bq_fm[:, :], in_=bq_fm_d.ap()), 16)
                    ld0.inc(g.dma_start(out=bk_bc[:, :], in_=bk_bc_d.ap()), 16)
                    ld0.inc(g.dma_start(out=bv_bc[:, :], in_=bv_bc_d.ap()), 16)
                    ld0.inc(g.dma_start(
                        out=rw_bf[:, :],
                        in_=rw_t.ap().rearrange("p c e -> p (c e)")), 16)
                    qn = max(1, NC // 8)
                    for qq in range(0, NC, qn):
                        wks[qq // qn].inc(g.dma_start(
                            out=wk_sb[:, qq * p.DKV:(qq + qn) * p.DKV],
                            in_=wk_t[:, qq:qq + qn, :].rearrange(
                                "p c o -> p (c o)")), 16)
                        wvs[qq // qn].inc(g.dma_start(
                            out=wv_sb[:, qq * p.DKV:(qq + qn) * p.DKV],
                            in_=wv_t[:, qq:qq + qn, :].rearrange(
                                "p c o -> p (c o)")), 16)
                    for gi in range(2, p.NS1G):
                        if gi >= 2:
                            g.wait_ge(tv.h, gi - 1)
                        tau, off = gi // NG, (gi % NG) * 128
                        src = x_fmt[tau][:, :, off:off + 128]
                        dst = xs[gi % 2].ap().rearrange("p (c t) -> p c t", c=NC)
                        xsem[gi % 2].inc(g.dma_start(out=dst, in_=src), 16)
                    g.wait_ge(s1st.h, 32)
                    cc.inc(g.collective_compute(
                        "AllReduce", ALU.add,
                        replica_groups=[list(range(p.NCORE))],
                        ins=[kv_loc.ap().opt()],
                        outs=[kv_glob.ap().opt()]))
                blk.gpsimd(a_g)

                def a_t(t):
                    t.wait_ge(ld0.h, 96)
                    for ps in (pkv, pn):
                        for hh in range(PW // 512):
                            t.matmul(ps[:, hh * 512:(hh + 1) * 512],
                                     zrow_bf, z512_bf, start=True, stop=False)
                    for gi in range(p.NS1G):
                        par = gi % 2
                        t.wait_ge(xsem[par].h, 16 * (gi // 2 + 1))
                        if gi >= 1:
                            t.wait_ge(vk1.h, gi)
                        qn = max(1, NC // 8)
                        for c in range(NC):
                            if gi == 0 and c % qn == 0:
                                t.wait_ge(wks[c // qn].h, 16)
                            for hh in range(p.NKC):
                                inst = t.matmul(
                                    pk[:, hh * p.KW:(hh + 1) * p.KW],
                                    xs[par][:, c * 128:(c + 1) * 128],
                                    wk_sb[:, c * p.DKV + hh * p.KW:
                                          c * p.DKV + (hh + 1) * p.KW],
                                    start=(c == 0), stop=(c == NC - 1))
                        tk.inc(inst)
                        if gi >= 1:
                            t.wait_ge(vv1.h, gi)
                        for c in range(NC):
                            if gi == 0 and c % qn == 0:
                                t.wait_ge(wvs[c // qn].h, 16)
                            for hh in range(p.NKC):
                                inst = t.matmul(
                                    pv[:, hh * p.KW:(hh + 1) * p.KW],
                                    xs[par][:, c * 128:(c + 1) * 128],
                                    wv_sb[:, c * p.DKV + hh * p.KW:
                                          c * p.DKV + (hh + 1) * p.KW],
                                    start=(c == 0), stop=(c == NC - 1))
                        tv.inc(inst)
                        t.wait_ge(vkb.h, gi + 1)
                        t.wait_ge(vv1.h, gi + 1)
                        for h in range(p.KVH):
                            last_in_bank = (h % 4 == 3) or (h == p.KVH - 1)
                            inst = t.matmul(pkv[:, h * 128:(h + 1) * 128],
                                            k_bf[:, h * 128:(h + 1) * 128],
                                            v_bf[:, h * 128:(h + 1) * 128],
                                            start=False,
                                            stop=(gi == p.NS1G - 1
                                                  and last_in_bank))
                        for hh in range(p.NKC):
                            inst = t.matmul(pn[0:1, hh * p.KW:(hh + 1) * p.KW],
                                            ones_col,
                                            k_bf[:, hh * p.KW:(hh + 1) * p.KW],
                                            start=False, stop=False)
                        tkv.inc(inst)
                    for hh in range(PW // 512):
                        tkv.inc(t.matmul(pn[:, hh * 512:(hh + 1) * 512],
                                         zrow_bf, z512_bf,
                                         start=False, stop=True))
                blk.tensor(a_t)

                def a_s(s):
                    for gi in range(p.NS1G):
                        s.wait_ge(vk2.h, gi + 1)
                        as1.inc(s.activation(ke[:, :], kt2[:, :], AF.Exp))
                blk.scalar(a_s)

                def a_v(v):
                    v.wait_ge(ld0.h, 96)
                    v.tensor_single_scalar(nbq_fm[:, :], bq_fm[:, :],
                                           -1.0, ALU.mult)
                    for gi in range(p.NS1G):
                        v.wait_ge(tk.h, gi + 1)
                        vk1.inc(v.tensor_add(kt1[:, :], pk[:, 0:p.DKV], bk_bc[:, :]))
                        v.wait_ge(vk1.h, vk1.n)
                        if gi >= 1:
                            v.wait_ge(as1.h, gi)
                        vk2.inc(v.tensor_single_scalar(kt2[:, :], kt1[:, :],
                                                       0.0, ALU.min))
                        v.wait_ge(as1.h, gi + 1)
                        if gi >= 1:
                            v.wait_ge(tkv.h, gi)
                        vkb.inc(v.scalar_tensor_tensor(k_bf[:, :], kt1[:, :], 0.0,
                                                       ke[:, :], ALU.max, ALU.add))
                        v.wait_ge(tv.h, gi + 1)
                        vv1.inc(v.tensor_add(v_bf[:, :], pv[:, 0:p.DKV],
                                             bv_bc[:, :]))
                    v.wait_ge(tkv.h, p.NS1G + PW // 512)
                    vend.inc(v.tensor_copy(kv_sb[:, :], pkv[:, 0:p.DKV]))
                    vend.inc(v.tensor_copy(nm_sb[0:1, :], pn[0:1, 0:p.DKV]))
                blk.vector(a_v)

                def a_sy(sy):
                    sy.wait_ge(vend.h, 2)
                    src = kv_sb.ap().rearrange("p (h v) -> p h v", h=p.KVH)
                    dst = kv_loc[0:p.DKV].rearrange("(h q) v -> q h v", q=128)
                    s1st.inc(sy.dma_start(out=dst, in_=src), 16)
                    src2 = nm_sb.ap().rearrange("a (h v) -> a h v", h=p.KVH)
                    dst2 = kv_loc[p.DKV:p.DKV + p.KVH].rearrange(
                        "(a h) v -> a h v", a=1)
                    s1st.inc(sy.dma_start(out=dst2, in_=src2), 16)
                    sy.wait_ge(s1st.h, 32)
                blk.sync(a_sy)

        with ExitStack() as cb:
            bsb = lambda n, s, d: cb.enter_context(nc.sbuf_tensor(n, s, d))
            bps = lambda n, s: cb.enter_context(nc.psum_tensor(n, s, F32))
            bufAB = [bsb(f"bufab{i}", [128, NC * TT], BF16) for i in range(2)]
            zt = bsb("zt", [128, NG * D], F32)
            arena = [bsb(f"wa{i}", [128, NC * 128], BF16) for i in range(5)]
            ab_sb = bsb("ab_sb", [128, NC * 128 + D], BF16)
            bo_s = [bsb(f"bo{i}", [128, p.SW], F32) for i in range(2)]
            qy_bf = bsb("qy_bf", [128, max(2 * TT, p.LPW)], BF16)
            t2s = bsb("t2s", [128, TT], F32)
            t1r = bsb("t1r", [128, TT], F32)
            e_t = bsb("e_t", [128, TT], F32)
            tvs = bsb("tvs", [128, p.SW], F32)

            l_tm = bsb("l_tm", [128, p.E], F32)
            prb = bsb("prb", [128, p.E], F32)
            msk = bsb("msk", [128, p.E], F32)
            wfin = bsb("wfin", [128, p.E], F32)

            midw = bsb("midw", [128, TT], BF16)
            lg_sb = e_t[0:p.E, 0:TT]
            wrow = t1r[0:p.E, 0:TT]
            wexp = t2s
            stats = bsb("stats", [128, 6 * (D // 512)], F32)
            stats4 = bsb("stats4", [128, NG * 6 * (D // 512)], F32)
            aggr = bsb("aggr", [128, 2], F32)
            scal = bsb("scal", [128, 8], F32)
            nm8 = bsb("nm8", [p.KVH, 128], F32)
            nmt = bsb("nmt", [128, p.KVH], F32)
            PS = [bps(f"ps{i}", [128, 512]) for i in range(8)]

            w_s = [Sem(cb, nc, f"w_s{i}") for i in range(5)]
            wc_o = Sem(cb, nc, "wc_o")
            wc_b = Sem(cb, nc, "wc_b")
            xf = Sem(cb, nc, "xf")
            xfq = [Sem(cb, nc, f"xfq{i}") for i in range(4)]
            xt = [Sem(cb, nc, f"xt{i}") for i in range(NG)]
            bo_sem = [Sem(cb, nc, f"bo_sem{i}") for i in range(2)]
            st = Sem(cb, nc, "st")
            qm = Sem(cb, nc, "qm")
            am = Sem(cb, nc, "am")
            vq1 = Sem(cb, nc, "vq1")
            vqp = Sem(cb, nc, "vqp")
            vqb = Sem(cb, nc, "vqb")
            va = Sem(cb, nc, "va")
            aq1 = Sem(cb, nc, "aq1")
            om = Sem(cb, nc, "om")
            vo1 = Sem(cb, nc, "vo1")
            vo2 = Sem(cb, nc, "vo2")
            vst = Sem(cb, nc, "vst")
            vl1 = Sem(cb, nc, "vl1")
            al1 = Sem(cb, nc, "al1")
            vl2 = Sem(cb, nc, "vl2")
            al2 = Sem(cb, nc, "al2")
            vl3 = Sem(cb, nc, "vl3")
            ac = Sem(cb, nc, "ac")
            tl = Sem(cb, nc, "tl")
            vf = Sem(cb, nc, "vf")
            tf1 = Sem(cb, nc, "tf1")
            vf1 = Sem(cb, nc, "vf1")
            tf2 = Sem(cb, nc, "tf2")
            vf2 = Sem(cb, nc, "vf2")
            vf2a = Sem(cb, nc, "vf2a")
            as_f = Sem(cb, nc, "as_f")
            tf3 = Sem(cb, nc, "tf3")
            vf3 = Sem(cb, nc, "vf3")
            tf4 = Sem(cb, nc, "tf4")
            tf5 = Sem(cb, nc, "tf5")
            vf4 = Sem(cb, nc, "vf4")
            f4 = Sem(cb, nc, "f4")
            v42 = Sem(cb, nc, "v42")
            vl1b = Sem(cb, nc, "vl1b")
            al1b = Sem(cb, nc, "al1b")
            vl2b = Sem(cb, nc, "vl2b")
            al4 = Sem(cb, nc, "al4")
            vse = Sem(cb, nc, "vse")

            def vedge(v, inst):
                vse.inc(inst)
                v.wait_ge(vse.h, vse.n)

            QC = NC // 4
            BLK = NC + 8 * p.NSTRIP

            def blocks_of_tile():
                out = [("wq", oc, None) for oc in range(NC)]
                for s in range(p.NSTRIP):
                    for qq in range(4):
                        out.append(("wo", (s, qq), None))
                for s in range(p.NSTRIP):
                    for qq in range(4):
                        out.append(("bw", (s, qq), None))
                return out

            def block_src(kind, idx):
                if kind == "wq":
                    return wq_t[idx].rearrange("p c o -> p (c o)")
                s, qq = idx
                base = wo_t if kind == "wo" else bw_t
                sl = base[s].rearrange("p c o -> p c o")[
                    :, qq * QC:(qq + 1) * QC, :]
                return sl.rearrange("p c o -> p (c o)")

            def slot_of(j):
                return 4 if j == 0 else (j - 1) % 4

            def slot_count(j):
                return 1 if j == 0 else (j - 1) // 4 + 1

            PST = [0] * 4
            for jj in range(1, NC + 8 * p.NSTRIP):
                PST[(jj - 1) % 4] += 1

            def wwait(t, b, tau):
                j = b - BLK * tau
                if j == 0:
                    t.wait_ge(w_s[4].h, 16 * (tau + 1))
                else:
                    par = (j - 1) % 4
                    cnt = tau * PST[par] + (j - 1) // 4 + 1
                    t.wait_ge(w_s[par].h, 16 * cnt)

            def consumed_wait(kind, idx, ptau):
                if kind == "wq":
                    return (qm.h, NC * ptau + idx + 1)
                s, qq = idx
                if kind == "wo":
                    if qq < 3:
                        return (wc_o.h, 3 * p.NSTRIP * ptau + 3 * s + qq + 1)
                    return (om.h, NSG * ptau + NG * (s + 1))
                if qq < 3:
                    return (wc_b.h, 3 * p.NSTRIP * ptau + 3 * s + qq + 1)
                return (f4.h, NSG * ptau + NG * (s + 1))

            with nc.Block() as blk:


                tb = blocks_of_tile()
                for tau in range(p.NTILE):
                    b0 = BLK * tau
                    qm0, am0, vq10, vqp0, vqb0, va0, aq10 = (
                        qm.n, am.n, vq1.n, vqp.n, vqb.n, va.n, aq1.n)
                    om0, vo10, vo20, vst0 = om.n, vo1.n, vo2.n, vst.n
                    vl10, al10, vl20, al20, vl30, tl0, vf0, ac0 = (
                        vl1.n, al1.n, vl2.n, al2.n, vl3.n, tl.n, vf.n, ac.n)
                    tf10, vf10, tf20, vf20, vf2a0, asf0, tf30, vf30 = (
                        tf1.n, vf1.n, tf2.n, vf2.n, vf2a.n, as_f.n, tf3.n, vf3.n)
                    tf40, tf50, vf40, f40, v420 = tf4.n, tf5.n, vf4.n, f4.n, v42.n
                    vl1b0, al1b0, vl2b0, al40 = vl1b.n, al1b.n, vl2b.n, al4.n

                    XFb = bufAB[tau % 2]
                    ATb = bufAB[(tau + 1) % 2]

                    def ring_wait(g, j, tau=tau):
                        if j == 0:
                            pj, ptau = 0, tau - 1
                        elif j - 4 >= 1:
                            pj, ptau = j - 4, tau
                        else:
                            ptau = tau - 1
                            pslot = (j - 1) % 4
                            pj = max(jj for jj in range(1, BLK)
                                     if (jj - 1) % 4 == pslot)
                        if ptau < 0:
                            return
                        kind, idx, _ = tb[pj]
                        sem_h, val = consumed_wait(kind, idx, ptau)
                        g.wait_ge(sem_h, val)

                    def b_g(g, tau=tau):
                        if tau > 0:
                            g.wait_ge(om.h, NSG * tau)
                        if tau == 0:
                            qn = max(1, NC // 4)
                            for qq in range(0, NC, qn):
                                xfq[qq // qn].inc(g.dma_start(
                                    out=XFb[:, qq * TT:(qq + qn) * TT],
                                    in_=x_fmt[tau][:, qq:qq + qn, :].rearrange(
                                        "p c t -> p (c t)")), 16)
                        else:
                            src = x_fmt[tau].rearrange("p c t -> p (c t)")
                            xf.inc(g.dma_start(out=XFb[:, :], in_=src), 16)
                        for j, (kind, idx, _) in enumerate(tb):
                            if j == 0 and tau > 0:
                                continue
                            ring_wait(g, j)
                            par = slot_of(j)
                            w_s[par].inc(
                                g.dma_start(out=arena[par][:, :],
                                            in_=block_src(kind, idx)), 16)
                            if tau == 0 and j == 3:
                                g.wait_ge(cc.h, 1)
                                srck = kv_glob[0:p.DKV].rearrange(
                                    "(h q) v -> q h v", q=128)
                                dstk = kvt_bf.ap().rearrange(
                                    "p (h v) -> p h v", h=p.KVH)
                                kvl.inc(g.dma_start(out=dstk, in_=srck), 16)
                                kvl.inc(g.dma_start(
                                    out=nm8[:, :],
                                    in_=kv_glob[p.DKV:p.DKV + p.KVH]), 16)
                                abl.inc(g.dma_start(out=ab_sb[:, :],
                                                    in_=ab_t.ap()), 16)
                        if tau < p.NTILE - 1:
                            g.wait_ge(qm.h, NC * tau + 1)
                            w_s[4].inc(g.dma_start(
                                out=arena[4][:, :],
                                in_=block_src("wq", 0)), 16)
                    blk.gpsimd(b_g)

                    def b_sy(sy, tau=tau, al40=al40, vo10=vo10):
                        for gi in range(NG):
                            if tau > 0:
                                sy.wait_ge(al4.h, al40 - NG + gi + 1)
                                r0 = (tau - 1) * TT + gi * 128
                                st.inc(sy.dma_start(
                                    out=y_out[r0:r0 + 128, :],
                                    in_=zt[:, gi * D:(gi + 1) * D]), 16)
                                sy.wait_ge(st.h, st.n)
                            r0 = tau * TT + gi * 128
                            xt[gi].inc(sy.dma_start(
                                out=zt[:, gi * D:(gi + 1) * D],
                                in_=x_tm[r0:r0 + 128, :]), 16)
                        for s in range(p.NSTRIP):
                            if s >= 2:
                                sy.wait_ge(vo1.h, vo10 + NG * (s - 2) + NG)
                            bo_sem[s % 2].inc(sy.dma_start(
                                out=bo_s[s % 2][:, :],
                                in_=bo_bc_d[:, s * p.SW:(s + 1) * p.SW]), 16)
                    blk.sync(b_sy)

                    def q_t(t, tau=tau, vqp0=vqp0, vqb0=vqb0, va0=va0,
                            v420=v420, aq10=aq10):
                        if tau > 0:
                            t.wait_ge(xf.h, 16 * tau)
                            t.wait_ge(v42.h, v420)
                        for oc in range(NC):
                            b = b0 + oc
                            par = slot_of(oc)
                            wwait(t, b, tau)
                            if oc >= 2:
                                t.wait_ge(aq1.h, aq10 + 3 * (oc - 1))
                            for kc in range(NC):
                                if tau == 0 and oc == 0 and \
                                        kc % max(1, NC // 4) == 0:
                                    t.wait_ge(xfq[kc // max(1, NC // 4)].h, 16)
                                inst = t.matmul(
                                    PS[oc % 2][:, 0:TT],
                                    arena[par][:, kc * 128:(kc + 1) * 128],
                                    XFb[:, kc * TT:(kc + 1) * TT],
                                    start=(kc == 0), stop=(kc == NC - 1))
                            qm.inc(inst)
                            if tau > 0 and oc >= 1:
                                emit_attn(t, oc - 1)
                        if tau == 0:
                            t.wait_ge(kvl.h, 32)
                            t_n.inc(t.matmul(PS[7][:, 0:p.KVH], nm8[:, :],
                                             ident_f[0:p.KVH, 0:p.KVH],
                                             start=True, stop=True))
                            for j in range(NC):
                                emit_attn(t, j)
                        else:
                            emit_attn(t, NC - 1)
                    def emit_attn(t, j, tau=tau, vqb0=vqb0, va0=va0):
                        t.wait_ge(vqb.h, vqb0 + j + 1)
                        if j >= 2:
                            t.wait_ge(va.h, va0 + j - 1)
                        h = j // p.NREP
                        am.inc(t.matmul(PS[2 + j % 2][:, 0:TT],
                                        kvt_bf[:, h * 128:(h + 1) * 128],
                                        ATb[:, j * TT:(j + 1) * TT],
                                        start=True, stop=True))
                    blk.tensor(q_t)

                    def q_s(s, tau=tau, qm0=qm0, am0=am0, vqb0=vqb0,
                            va0=va0, aq10=aq10):
                        for oc in range(NC):
                            s.wait_ge(qm.h, qm0 + oc + 1)
                            aq1.inc(s.activation(t2s[:, 0:TT],
                                                 PS[oc % 2][:, 0:TT], AF.Relu,
                                                 bias=nbq_fm[:, oc:oc + 1],
                                                 scale=-1.0))
                            s.wait_ge(aq1.h, aq1.n)
                            if oc >= 1:
                                s.wait_ge(vqb.h, vqb0 + oc)
                            aq1.inc(s.activation(e_t[:, 0:TT], t2s[:, 0:TT],
                                                 AF.Exp, scale=-1.0))
                            s.wait_ge(aq1.h, aq1.n)
                            if oc >= 1:
                                s.wait_ge(vqb.h, vqb0 + oc)
                            aq1.inc(s.activation(t1r[:, 0:TT],
                                                 PS[oc % 2][:, 0:TT], AF.Relu,
                                                 bias=bq_fm[:, oc:oc + 1]))
                            if tau > 0 and oc >= 1:
                                j = oc - 1
                                s.wait_ge(am.h, am0 + j + 1)
                                h = j // p.NREP
                                va.inc(s.activation(
                                    ATb[:, j * TT:(j + 1) * TT],
                                    PS[2 + j % 2][:, 0:TT], AF.Identity,
                                    scale=inv_nm[:, h:h + 1]))
                        if tau == 0:
                            s.wait_ge(v_n.h, 2)
                            ev_js = list(range(NC))
                        else:
                            ev_js = [NC - 1]
                        for j in ev_js:
                            s.wait_ge(am.h, am0 + j + 1)
                            h = j // p.NREP
                            va.inc(s.activation(
                                ATb[:, j * TT:(j + 1) * TT],
                                PS[2 + j % 2][:, 0:TT], AF.Identity,
                                scale=inv_nm[:, h:h + 1]))
                    blk.scalar(q_s)

                    def q_v(v, tau=tau, qm0=qm0, am0=am0, aq10=aq10):
                        for oc in range(NC):
                            v.wait_ge(aq1.h, aq10 + 3 * oc + 3)
                            vqb.inc(v.tensor_add(
                                ATb[:, oc * TT:(oc + 1) * TT],
                                t1r[:, 0:TT], e_t[:, 0:TT]))
                        if tau == 0:
                            v.wait_ge(t_n.h, 1)
                            v_n.inc(v.tensor_single_scalar(
                                nmt[:, :], PS[7][:, 0:p.KVH], 1e-10, ALU.add))
                            v.wait_ge(v_n.h, v_n.n)
                            v_n.inc(v.reciprocal(inv_nm[:, :], nmt[:, :]))
                    blk.vector(q_v)

                    def o_t(t, tau=tau, va0=va0, vqp0=vqp0, vo10=vo10,
                            aq10=aq10):
                        for s in range(p.NSTRIP):
                            sp = s % 2
                            for qq in range(4):
                                b = b0 + NC + 4 * s + qq
                                par = slot_of(NC + 4 * s + qq)
                                wwait(t, b, tau)
                                if s == 0 and qq == 0:
                                    t.wait_ge(va.h, va0 + NC)
                                    t.wait_ge(aq1.h, aq10 + 3 * NC)
                                if tau == 0 and s == min(1, p.NSTRIP - 1) \
                                        and qq == 0:
                                    t.wait_ge(v_n.h, 2)
                                if s >= 2 and qq == 0:
                                    t.wait_ge(vo1.h, vo10 + NG * (s - 2) + NG)
                                for gi in range(NG):
                                    for c in range(QC):
                                        fc = qq * QC + c
                                        inst = t.matmul(
                                            PS[4 * sp + gi][:, 0:p.SW],
                                            ATb[:, fc * TT + gi * 128:
                                                fc * TT + (gi + 1) * 128],
                                            arena[par][:, c * p.SW:(c + 1) * p.SW],
                                            start=(qq == 0 and c == 0),
                                            stop=(qq == 3 and c == QC - 1))
                                    if qq == 3:
                                        om.inc(inst)
                                if qq < 3:
                                    wc_o.inc(inst)
                    blk.tensor(o_t)

                    def o_v(v, tau=tau, om0=om0):
                        nbo = (p.NSTRIP + 1) // 2
                        for s in range(p.NSTRIP):
                            for gi in range(NG):
                                v.wait_ge(om.h, om0 + NG * s + gi + 1)
                                v.wait_ge(bo_sem[s % 2].h,
                                          16 * (tau * nbo + s // 2 + 1))
                                v.wait_ge(xt[gi].h, 16 * (tau + 1))
                                vo1.inc(v.tensor_add(
                                    tvs[:, :], PS[4 * (s % 2) + gi][:, 0:p.SW],
                                    bo_s[s % 2][:, :]))
                                v.wait_ge(vo1.h, vo1.n)
                                sl = zt[:, gi * D + s * p.SW:
                                        gi * D + (s + 1) * p.SW]
                                vo2.inc(v.tensor_add(sl, tvs[:, :], sl))
                                v.wait_ge(vo2.h, vo2.n)
                                nst = D // 512
                                for i2 in range(p.SW // 512):
                                    i = (s * p.SW) // 512 + i2
                                    vst.inc(v.bn_stats(
                                        stats4[:, gi * 6 * nst + 6 * i:
                                               gi * 6 * nst + 6 * (i + 1)],
                                        zt[:, gi * D + 512 * i:
                                           gi * D + 512 * (i + 1)]))
                    blk.vector(o_v)

                    def l_v(v, tau=tau, vo20=vo20, al10=al10, al40=al40,
                            tl0=tl0, vf0=vf0):
                        nst = D // 512
                        nps = p.SW // 512
                        for gi in range(NG):
                            v.wait_ge(vst.h, vst0 + nps * (NG * (p.NSTRIP - 1)
                                                           + gi + 1))
                            if gi == 0 and tau > 0:
                                v.wait_ge(al4.h, al40)
                            vl1.inc(v.bn_aggr(aggr[:, 0:2],
                                              stats4[:, gi * 6 * nst:
                                                     (gi + 1) * 6 * nst]))
                            v.wait_ge(vl1.h, vl1.n)
                            v.wait_ge(al1.h, al10 + gi + 1)
                            rec = v.reciprocal(scal[:, gi:gi + 1],
                                               stats[:, 1:2])
                            nmop = v.tensor_single_scalar(stats[:, 0:1],
                                                          aggr[:, 0:1],
                                                          -1.0, ALU.mult)
                            vedge(v, rec)
                            vedge(v, nmop)
                            vl2.inc(v.tensor_mul(scal[:, 4 + gi:5 + gi],
                                                 stats[:, 0:1],
                                                 scal[:, gi:gi + 1]))
                            v.wait_ge(vl2.h, vl2.n)
                        dst = XFb.ap().rearrange("p (c t) -> p c t", c=NC)
                        for pc in range(p.LNP):
                            for gi in range(NG):
                                idx = pc * NG + gi
                                v.wait_ge(tl.h, tl0 + idx + 1)
                                for bi in range(p.NB):
                                    nch = min(4, p.NQT - 4 * bi)
                                    c0 = pc * p.NQT + bi * 4
                                    sl = dst[:, c0:c0 + nch,
                                             gi * 128:(gi + 1) * 128]
                                    vf.inc(v.tensor_copy(
                                        sl,
                                        PS[6 + bi][:, 0:128 * nch].rearrange(
                                            "p (c t) -> p c t", c=nch)))
                    blk.vector(l_v)

                    def l_s(s, tau=tau, vl10=vl10, vl20=vl20, tl0=tl0,
                            al20=al20):
                        for gi in range(NG):
                            s.wait_ge(vl1.h, vl10 + gi + 1)
                            al1.inc(s.activation(stats[:, 1:2], aggr[:, 1:2],
                                                 AF.Sqrt, bias=eps_ln))
                        for pc in range(p.LNP):
                            for gi in range(NG):
                                idx = pc * NG + gi
                                if pc == 0:
                                    s.wait_ge(vl2.h, vl20 + gi + 1)
                                z = zt[:, gi * D + pc * p.LPW:
                                       gi * D + (pc + 1) * p.LPW]
                                al2.inc(s.activation(
                                    z, z, AF.Identity,
                                    bias=scal[:, 4 + gi:5 + gi],
                                    scale=scal[:, gi:gi + 1]))
                                s.wait_ge(al2.h, al2.n)
                                if idx >= 1:
                                    s.wait_ge(tl.h, tl0 + idx)
                                ac.inc(s.activation(qy_bf[:, 0:p.LPW], z,
                                                    AF.Identity))
                    blk.scalar(l_s)

                    def f_t(t, tau=tau, vf0=vf0, vf10=vf10, vf20=vf20,
                            vf2a0=vf2a0, vf30=vf30, vf40=vf40, v420=v420,
                            ac0=ac0, tl0=tl0, vo20=vo20):
                        if tau == 0:
                            t.wait_ge(abl.h, 16)
                        t.wait_ge(vo2.h, vo20 + NG * p.NSTRIP)

                        def emit_chunks(pcx):
                            t.wait_ge(vf.h, vf0 + (pcx + 1) * NG * p.NB)
                            for c in range(pcx * p.NQT, (pcx + 1) * p.NQT):
                                inst = t.matmul(
                                    PS[0][0:p.E, 0:TT],
                                    rw_bf[:, c * p.E:(c + 1) * p.E],
                                    XFb[:, c * TT:(c + 1) * TT],
                                    start=(c == 0), stop=(c == NC - 1))
                                if c == NC - 1:
                                    tf1.inc(inst)
                                inst = t.matmul(
                                    PS[4][:, 0:TT],
                                    ab_sb[:, c * 128:(c + 1) * 128],
                                    XFb[:, c * TT:(c + 1) * TT],
                                    start=(c == 0), stop=(c == NC - 1))
                                if c == NC - 1:
                                    tf5.inc(inst)

                        for pc in range(p.LNP):
                            for gi in range(NG):
                                idx = pc * NG + gi
                                t.wait_ge(ac.h, ac0 + idx + 1)
                                if idx >= 1:
                                    t.wait_ge(vf.h, vf0 + idx * p.NB)
                                srcq = qy_bf
                                for j in range(p.NQT):
                                    bi, jo = j // 4, j % 4
                                    inst = t.matmul(
                                        PS[6 + bi][:, jo * 128:(jo + 1) * 128],
                                        srcq[:, j * 128:(j + 1) * 128],
                                        ident_bf, start=True, stop=True)
                                tl.inc(inst)
                            if pc >= 1:
                                emit_chunks(pc - 1)
                        emit_chunks(p.LNP - 1)
                        def emit_tf2(gi):
                            if gi == 0:
                                t.wait_ge(vf1.h, vf10 + 1)
                            else:
                                t.wait_ge(vf2a.h, vf2a0 + gi)
                            tf2.inc(t.matmul(
                                PS[5][:, 0:p.E],
                                lg_sb[:, gi * 128:(gi + 1) * 128],
                                ident_f[0:p.E, 0:p.E],
                                start=True, stop=True))

                        def emit_tf3(gi):
                            t.wait_ge(vf2.h, vf20 + gi + 1)
                            tf3.inc(t.matmul(
                                PS[6][0:p.E, gi * 128:(gi + 1) * 128],
                                wfin[:, :], ident_f, start=True, stop=True))

                        def emit_tf4():
                            t.wait_ge(vf3.h, vf30 + 1)
                            tf4.inc(t.matmul(PS[7][:, 0:TT], e_mat,
                                             wrow[:, 0:TT],
                                             start=True, stop=True))
                        for s in range(p.NSTRIP):
                            sp = s % 2
                            for qq in range(4):
                                b = b0 + NC + 4 * p.NSTRIP + 4 * s + qq
                                par = slot_of(NC + 4 * p.NSTRIP + 4 * s + qq)
                                wwait(t, b, tau)
                                if s == 0 and qq == 0:
                                    t.wait_ge(vf1.h, vf10 + 1)
                                if s == 0 and qq == 1:
                                    emit_tf2(0)
                                    emit_tf3(0)
                                    if NG >= 2:
                                        emit_tf2(1)
                                if s == 0 and qq == 2:
                                    if NG >= 2:
                                        emit_tf3(1)
                                    for g2i in range(2, NG):
                                        emit_tf2(g2i)
                                        emit_tf3(g2i)
                                if s == 0 and qq == 3:
                                    emit_tf4()
                                if s == 1 and qq == 0:
                                    t.wait_ge(vf4.h, vf40 + 2)
                                if s >= 2 and qq == 0:
                                    t.wait_ge(v42.h, v420 + NG * (s - 2) + NG)
                                for gi in range(NG):
                                    for c in range(QC):
                                        fc = qq * QC + c
                                        inst = t.matmul(
                                            PS[4 * sp + gi][:, 0:p.SW],
                                            XFb[:, fc * TT + gi * 128:
                                                fc * TT + (gi + 1) * 128],
                                            arena[par][:, c * p.SW:(c + 1) * p.SW],
                                            start=(qq == 0 and c == 0),
                                            stop=False)
                                    if qq == 3:
                                        if s == 0 and gi == 0:
                                            t.wait_ge(vf4.h, vf40 + 2)
                                        f4.inc(t.matmul(
                                            PS[4 * sp + gi][:, 0:p.SW],
                                            midw[:, gi * 128:(gi + 1) * 128],
                                            ab_sb[:, NC * 128 + s * p.SW:
                                                  NC * 128 + (s + 1) * p.SW],
                                            start=False, stop=True))
                                if qq < 3:
                                    wc_b.inc(inst)
                    blk.tensor(f_t)

                    def f_s(s, tau=tau, vl1b0=vl1b0, vl2b0=vl2b0):
                        for gi in range(NG):
                            s.wait_ge(vl1b.h, vl1b0 + gi + 1)
                            as_f.inc(s.activation(prb[:, :], l_tm[:, :], AF.Exp,
                                                  bias=scal[:, 0:1],
                                                  accum_out=scal[:, 1:2]))
                        for gi in range(NG):
                            s.wait_ge(vl2b.h, vl2b0 + 2 * gi + 1)
                            al1b.inc(s.activation(scal[:, 4:5], aggr[:, 1:2],
                                                  AF.Sqrt, bias=eps_ln))
                            s.wait_ge(vl2b.h, vl2b0 + 2 * gi + 2)
                            al4.inc(s.activation(zt[:, gi * D:(gi + 1) * D],
                                                 zt[:, gi * D:(gi + 1) * D],
                                                 AF.Identity, bias=scal[:, 7:8],
                                                 scale=scal[:, 5:6]))
                    blk.scalar(f_s)

                    def f_v(v, tau=tau, tf10=tf10, tf20=tf20, tf30=tf30,
                            tf40=tf40, tf50=tf50, f40=f40, v420=v420,
                            asf0=asf0, al20=al20, al1b0=al1b0, al40=al40,
                            vf2a0=vf2a0):
                        v.wait_ge(tf1.h, tf10 + 1)
                        v.wait_ge(al2.h, al20 + NG * p.LNP)
                        vf1.inc(v.tensor_copy(lg_sb[:, :], PS[0][0:p.E, 0:TT]))
                        for gi in range(NG):
                            v.wait_ge(tf2.h, tf20 + gi + 1)
                            vf2a.inc(v.tensor_copy(l_tm[:, :], PS[5][:, 0:p.E]))
                            v.wait_ge(vf2a.h, vf2a.n)
                            vl1b.inc(v.tensor_reduce(scal[:, 0:1], l_tm[:, :],
                                                     AX.X, ALU.max, negate=True))
                            v.wait_ge(as_f.h, asf0 + gi + 1)
                            rec = v.reciprocal(scal[:, 2:3], scal[:, 1:2])
                            vedge(v, rec)
                            vedge(v, v.tensor_scalar(prb[:, :], prb[:, :],
                                                     scal[:, 2:3], None,
                                                     ALU.mult))
                            m1 = v.tensor_reduce(scal[:, 3:4], prb[:, :],
                                                 AX.X, ALU.max)
                            vedge(v, m1)
                            vedge(v, v.tensor_scalar(msk[:, :], prb[:, :],
                                                     scal[:, 3:4], None,
                                                     ALU.is_equal))
                            vedge(v, v.scalar_tensor_tensor(
                                msk[:, :], msk[:, :], -2.0,
                                prb[:, :], ALU.mult, ALU.add))
                            m2 = v.tensor_reduce(scal[:, 4:5], msk[:, :],
                                                 AX.X, ALU.max)
                            vedge(v, m2)
                            vedge(v, v.tensor_scalar(msk[:, :], prb[:, :],
                                                     scal[:, 4:5], None,
                                                     ALU.is_ge))
                            dn = v.tensor_add(scal[:, 6:7], scal[:, 3:4],
                                              scal[:, 4:5])
                            vedge(v, dn)
                            rd = v.reciprocal(scal[:, 5:6], scal[:, 6:7])
                            vedge(v, rd)
                            sc2 = v.tensor_single_scalar(scal[:, 6:7],
                                                         scal[:, 5:6],
                                                         2.0, ALU.mult)
                            vedge(v, sc2)
                            vedge(v, v.tensor_mul(wfin[:, :], prb[:, :],
                                                  msk[:, :]))
                            vf2.inc(v.tensor_scalar(wfin[:, :], wfin[:, :],
                                                    scal[:, 6:7], None,
                                                    ALU.mult))
                        v.wait_ge(tf3.h, tf30 + NG)
                        vf3.inc(v.tensor_copy(wrow[:, 0:TT],
                                              PS[6][0:p.E, 0:TT]))
                        v.wait_ge(tf4.h, tf40 + 1)
                        vf4.inc(v.tensor_copy(wexp[:, 0:TT], PS[7][:, 0:TT]))
                        v.wait_ge(vf4.h, vf4.n)
                        v.wait_ge(tf5.h, tf50 + 1)
                        vf4.inc(v.tensor_mul(midw[:, 0:TT], PS[4][:, 0:TT],
                                             wexp[:, 0:TT]))
                        for s in range(p.NSTRIP):
                            for gi in range(NG):
                                v.wait_ge(f4.h, f40 + NG * s + gi + 1)
                                sl = zt[:, gi * D + s * p.SW:
                                        gi * D + (s + 1) * p.SW]
                                v42.inc(v.tensor_add(
                                    sl, PS[4 * (s % 2) + gi][:, 0:p.SW], sl))
                        for gi in range(NG):
                            v.wait_ge(v42.h,
                                      v420 + NG * (p.NSTRIP - 1) + gi + 1)
                            if gi >= 1:
                                v.wait_ge(al4.h, al40 + gi)
                            else:
                                v.wait_ge(as_f.h, asf0 + NG)
                            nst = D // 512
                            for i in range(nst):
                                ii = v.bn_stats(stats[:, 6 * i:6 * (i + 1)],
                                                zt[:, gi * D + 512 * i:
                                                   gi * D + 512 * (i + 1)])
                            vedge(v, ii)
                            vl2b.inc(v.bn_aggr(aggr[:, 0:2],
                                               stats[:, 0:6 * nst]))
                            v.wait_ge(vl2b.h, vl2b.n)
                            v.wait_ge(al1b.h, al1b0 + gi + 1)
                            rec = v.reciprocal(scal[:, 5:6], scal[:, 4:5])
                            nmop = v.tensor_single_scalar(scal[:, 0:1],
                                                          aggr[:, 0:1],
                                                          -1.0, ALU.mult)
                            vedge(v, rec)
                            vedge(v, nmop)
                            vl2b.inc(v.tensor_mul(scal[:, 7:8], scal[:, 0:1],
                                                  scal[:, 5:6]))
                    blk.vector(f_v)

                def fin_sy(sy):
                    for gi in range(NG):
                        sy.wait_ge(al4.h, NG * (p.NTILE - 1) + gi + 1)
                        r0 = (p.NTILE - 1) * TT + gi * 128
                        st.inc(sy.dma_start(out=y_out[r0:r0 + 128, :],
                                            in_=zt[:, gi * D:(gi + 1) * D]), 16)
                    sy.wait_ge(st.h, st.n)
                blk.sync(fin_sy)

    return nc



def _tile_x_fm(x, p):
    return np.ascontiguousarray(
        x.reshape(p.NTILE, p.TT, p.NC, 128).transpose(0, 3, 2, 1))


def _tile_wq(w, p):
    return np.ascontiguousarray(
        w.reshape(p.NC, 128, p.NC, 128).transpose(0, 3, 2, 1))


def _tile_wkv(w, p):
    return np.ascontiguousarray(
        w.reshape(w.shape[0], p.NC, 128).transpose(2, 1, 0))


def _tile_strip(w, p):
    return np.ascontiguousarray(
        w.reshape(p.NSTRIP, p.SW, p.NC, 128).transpose(0, 3, 2, 1))


def _tile_rw(w, p):
    return np.ascontiguousarray(
        w.reshape(p.E, p.NC, 128).transpose(2, 1, 0))


def _tile_ab(la, lb, p):
    a = la.transpose(1, 0, 2).reshape(p.D, p.E * p.R)
    a = a.reshape(p.NC, 128, p.ER).transpose(1, 0, 2).reshape(128, p.NC * p.ER)
    b = lb.reshape(p.ER, p.D)
    return np.ascontiguousarray(np.concatenate([a, b], axis=1))


def make_consts(p):
    cst_bf = np.zeros((128, 832), dtype=ml_dtypes.bfloat16)
    cst_bf[:, 0] = 1.0
    cst_bf[:, 704:832] = np.eye(128, dtype=np.float32).astype(ml_dtypes.bfloat16)
    cst_f = np.zeros((128, 258), dtype=np.float32)
    cst_f[:, 256] = 1e-5
    cst_f[:, 0:128] = np.eye(128, dtype=np.float32)
    emat = np.repeat(np.eye(p.E, dtype=np.float32), p.R, axis=1)
    cst_f[0:p.E, 128:256] = emat
    return cst_bf, cst_f


def make_core_inputs(p, x_core, wq, bq, wk, bk, wv, bv, wo, bo,
                     base_w, router_w, lora_a, lora_b):
    cst_bf, cst_f = make_consts(p)
    return {
        "x_fmt": _tile_x_fm(x_core, p),
        "x_tm": np.ascontiguousarray(x_core),
        "wq_t": _tile_wq(wq, p),
        "wk_t": _tile_wkv(wk, p),
        "wv_t": _tile_wkv(wv, p),
        "wo_t": _tile_strip(wo, p),
        "bw_t": _tile_strip(base_w, p),
        "rw_t": _tile_rw(router_w, p),
        "ab_t": _tile_ab(lora_a, lora_b, p),
        "bq_fm": np.ascontiguousarray(bq.reshape(p.NC, 128).T),
        "bk_bc": np.ascontiguousarray(
            np.broadcast_to(bk[None, :], (128, p.DKV))),
        "bv_bc": np.ascontiguousarray(
            np.broadcast_to(bv[None, :], (128, p.DKV))),
        "bo_bc": np.ascontiguousarray(
            np.broadcast_to(bo[None, :], (128, p.D))),
        "cst_bf": cst_bf,
        "cst_f": cst_f,
    }


def kernel(L0_attn_hidden_states, L0_attn_Wq, L0_attn_bq, L0_attn_Wk,
           L0_attn_bk, L0_attn_Wv, L0_attn_bv, L0_attn_Wo, L0_attn_bo,
           L0_ffn_base_W, L0_ffn_router_W, L0_ffn_lora_A, L0_ffn_lora_B):
    p = P()
    x = np.asarray(L0_attn_hidden_states, dtype=np.float32).reshape(16384, 4096)
    args = [np.asarray(a, dtype=np.float32) for a in (
        L0_attn_Wq, L0_attn_bq, L0_attn_Wk, L0_attn_bk, L0_attn_Wv,
        L0_attn_bv, L0_attn_Wo, L0_attn_bo, L0_ffn_base_W, L0_ffn_router_W,
        L0_ffn_lora_A, L0_ffn_lora_B)]
    wq, bq, wk, bk, wv, bv, wo, bo, base_w, router_w, lora_a, lora_b = args

    nc = build_program(p)
    in_maps = []
    for c in range(p.NCORE):
        xc = x[c * p.TOK:(c + 1) * p.TOK]
        in_maps.append(make_core_inputs(p, xc, wq, bq, wk, bk, wv, bv, wo, bo,
                                        base_w, router_w, lora_a, lora_b))
    res = run_bass_kernel_spmd(nc, in_maps, core_ids=list(range(p.NCORE)))
    out = np.concatenate([np.asarray(r["y"]) for r in res.results], axis=0)
    return out.reshape(1, 16384, 4096).astype(np.float32)


# version 21
# speedup vs baseline: 1.0163x; 1.0092x over previous
import numpy as np
import ml_dtypes
from contextlib import ExitStack

import concourse.bass as bass
import concourse.mybir as mybir
from concourse.bass_utils import run_bass_kernel_spmd

F32 = mybir.dt.float32
BF16 = mybir.dt.bfloat16
AF = mybir.ActivationFunctionType
ALU = mybir.AluOpType
AX = mybir.AxisListType


class P:

    def __init__(self, ncore=8, tok=2048, d=4096, dkv=1024, tt=512):
        self.NCORE = ncore
        self.TOK = tok
        self.D = d
        self.DKV = dkv
        self.TT = tt
        self.NC = d // 128
        self.KVH = dkv // 128
        self.NREP = self.NC // self.KVH
        self.NTILE = tok // tt
        self.NG = tt // 128
        self.NS1G = tok // 128
        self.NSTRIP = max(1, d // 512)
        self.SW = d // self.NSTRIP
        self.CH = self.NC // 2
        self.NKC = max(1, dkv // 512)
        self.KW = dkv // self.NKC
        self.E, self.R, self.ER = 8, 16, 128
        self.LNP = max(1, d // 1024)
        self.LPW = d // self.LNP
        self.NQT = self.LPW // 128
        self.NB = (self.NQT + 3) // 4


class Sem:
    def __init__(self, ctx, nc, name):
        self.h = ctx.enter_context(nc.semaphore(name))
        self.n = 0

    def inc(self, inst, k=1):
        inst.then_inc(self.h, k)
        self.n += k
        return self.n


def build_program(p: P):
    nc = bass.Bass("TRN2", target_bir_lowering=False)
    NC, TT, NG, D = p.NC, p.TT, p.NG, p.D
    NSG = NG * p.NSTRIP

    x_fmt = nc.dram_tensor("x_fmt", [p.NTILE, 128, NC, TT], F32, kind="ExternalInput")
    x_tm = nc.dram_tensor("x_tm", [p.TOK, D], F32, kind="ExternalInput")
    wq_t = nc.dram_tensor("wq_t", [NC, 128, NC, 128], F32, kind="ExternalInput")
    wk_t = nc.dram_tensor("wk_t", [128, NC, p.DKV], F32, kind="ExternalInput")
    wv_t = nc.dram_tensor("wv_t", [128, NC, p.DKV], F32, kind="ExternalInput")
    wo_t = nc.dram_tensor("wo_t", [p.NSTRIP, 128, NC, p.SW], F32, kind="ExternalInput")
    bw_t = nc.dram_tensor("bw_t", [p.NSTRIP, 128, NC, p.SW], F32, kind="ExternalInput")
    rw_t = nc.dram_tensor("rw_t", [128, NC, p.E], F32, kind="ExternalInput")
    ab_t = nc.dram_tensor("ab_t", [128, NC * 128 + D], F32, kind="ExternalInput")
    bq_fm_d = nc.dram_tensor("bq_fm", [128, NC], F32, kind="ExternalInput")
    bk_bc_d = nc.dram_tensor("bk_bc", [128, p.DKV], F32, kind="ExternalInput")
    bv_bc_d = nc.dram_tensor("bv_bc", [128, p.DKV], F32, kind="ExternalInput")
    bo_bc_d = nc.dram_tensor("bo_bc", [128, D], F32, kind="ExternalInput")
    cst_bf_d = nc.dram_tensor("cst_bf", [128, 832], BF16, kind="ExternalInput")
    cst_f_d = nc.dram_tensor("cst_f", [128, 258], F32, kind="ExternalInput")
    y_out = nc.dram_tensor("y", [p.TOK, D], F32, kind="ExternalOutput")

    kv_loc = nc.dram_tensor("kv_loc", [p.DKV + p.KVH, 128], F32)
    kv_glob = nc.dram_tensor("kv_glob", [p.DKV + p.KVH, 128], F32)

    with ExitStack() as glob:
        gsb = lambda n, s, d: glob.enter_context(nc.sbuf_tensor(n, s, d))
        cst_bf = gsb("cst_bf_s", [128, 832], BF16)
        cst_f = gsb("cst_f_s", [128, 258], F32)
        eps_ln = None
        bq_fm = gsb("bq_fm_s", [128, NC], F32)
        nbq_fm = gsb("nbq_fm_s", [128, NC], F32)
        rw_bf = gsb("rw_bf_s", [128, NC * p.E], BF16)
        kvt_bf = gsb("kvt_bf_s", [128, p.DKV], BF16)
        inv_nm = gsb("inv_nm_s", [128, p.KVH], F32)
        ones_col = cst_bf[:, 0:1]
        zrow_bf = cst_bf[0:1, 1:129]
        z512_bf = cst_bf[0:1, 129:641]
        ident_bf = cst_bf[:, 704:832]
        ident_f = cst_f[:, 0:128]
        e_mat = cst_f[0:8, 128:256]
        eps_ln = cst_f[:, 256:257]

        cc = Sem(glob, nc, "cc")
        kvl = Sem(glob, nc, "kvl")
        abl = Sem(glob, nc, "abl")
        t_n = Sem(glob, nc, "t_n")
        v_n = Sem(glob, nc, "v_n")

        with ExitStack() as ca:
            asb = lambda n, s, d: ca.enter_context(nc.sbuf_tensor(n, s, d))
            aps = lambda n, s: ca.enter_context(nc.psum_tensor(n, s, F32))
            wk_sb = asb("wk_sb", [128, NC * p.DKV], BF16)
            wv_sb = asb("wv_sb", [128, NC * p.DKV], BF16)
            bk_bc = asb("bk_bc_s", [128, p.DKV], F32)
            bv_bc = asb("bv_bc_s", [128, p.DKV], F32)
            xs = [asb(f"xs{i}", [128, NC * 128], BF16) for i in range(2)]
            kt1 = asb("kt1", [128, p.DKV], F32)
            kt2 = asb("kt2", [128, p.DKV], F32)
            ke = asb("ke", [128, p.DKV], F32)
            k_bfs = [asb(f"k_bf{i}", [128, p.DKV], BF16) for i in range(2)]
            v_bfs = [asb(f"v_bf{i}", [128, p.DKV], BF16) for i in range(2)]
            kv_sb = asb("kv_sb", [128, p.DKV], F32)
            nm_sb = asb("nm_sb", [1, p.DKV], F32)
            PW = max(512, p.DKV)
            pk = aps("pk", [128, PW])
            pv = aps("pv", [128, PW])
            pkv = aps("pkv", [128, PW])
            pn = aps("pn", [128, PW])

            ld0 = Sem(ca, nc, "ld0")
            wks = [Sem(ca, nc, f"wks{i}") for i in range(8)]
            wvs = [Sem(ca, nc, f"wvs{i}") for i in range(8)]
            xsem = [Sem(ca, nc, f"xs_s{i}") for i in range(2)]
            tk = Sem(ca, nc, "tk")
            tv = Sem(ca, nc, "tv")
            tkv = Sem(ca, nc, "tkv")
            vk1 = Sem(ca, nc, "vk1")
            vk2 = Sem(ca, nc, "vk2")
            vkb = Sem(ca, nc, "vkb")
            vv1 = Sem(ca, nc, "vv1")
            as1 = Sem(ca, nc, "as1")
            vend = Sem(ca, nc, "vend")
            s1st = Sem(ca, nc, "s1st")

            with nc.Block() as blk:
                def a_g(g):
                    for gi in range(min(2, p.NS1G)):
                        tau, off = gi // NG, (gi % NG) * 128
                        src = x_fmt[tau][:, :, off:off + 128]
                        dst = xs[gi % 2].ap().rearrange("p (c t) -> p c t", c=NC)
                        xsem[gi % 2].inc(g.dma_start(out=dst, in_=src), 16)
                    ld0.inc(g.dma_start(out=cst_bf[:, :], in_=cst_bf_d.ap()), 16)
                    ld0.inc(g.dma_start(out=cst_f[:, :], in_=cst_f_d.ap()), 16)
                    ld0.inc(g.dma_start(out=bq_fm[:, :], in_=bq_fm_d.ap()), 16)
                    ld0.inc(g.dma_start(out=bk_bc[:, :], in_=bk_bc_d.ap()), 16)
                    ld0.inc(g.dma_start(out=bv_bc[:, :], in_=bv_bc_d.ap()), 16)
                    ld0.inc(g.dma_start(
                        out=rw_bf[:, :],
                        in_=rw_t.ap().rearrange("p c e -> p (c e)")), 16)
                    qn = max(1, NC // 8)
                    for qq in range(0, NC, qn):
                        wks[qq // qn].inc(g.dma_start(
                            out=wk_sb[:, qq * p.DKV:(qq + qn) * p.DKV],
                            in_=wk_t[:, qq:qq + qn, :].rearrange(
                                "p c o -> p (c o)")), 16)
                        wvs[qq // qn].inc(g.dma_start(
                            out=wv_sb[:, qq * p.DKV:(qq + qn) * p.DKV],
                            in_=wv_t[:, qq:qq + qn, :].rearrange(
                                "p c o -> p (c o)")), 16)
                    for gi in range(2, p.NS1G):
                        if gi >= 2:
                            g.wait_ge(tv.h, gi - 1)
                        tau, off = gi // NG, (gi % NG) * 128
                        src = x_fmt[tau][:, :, off:off + 128]
                        dst = xs[gi % 2].ap().rearrange("p (c t) -> p c t", c=NC)
                        xsem[gi % 2].inc(g.dma_start(out=dst, in_=src), 16)
                    g.wait_ge(s1st.h, 32)
                    cc.inc(g.collective_compute(
                        "AllReduce", ALU.add,
                        replica_groups=[list(range(p.NCORE))],
                        ins=[kv_loc.ap().opt()],
                        outs=[kv_glob.ap().opt()]))
                blk.gpsimd(a_g)

                def kv_mms(t, gj):
                    kb, vb = k_bfs[gj % 2], v_bfs[gj % 2]
                    for h in range(p.KVH):
                        last_in_bank = (h % 4 == 3) or (h == p.KVH - 1)
                        t.matmul(pkv[:, h * 128:(h + 1) * 128],
                                 kb[:, h * 128:(h + 1) * 128],
                                 vb[:, h * 128:(h + 1) * 128],
                                 start=False,
                                 stop=(gj == p.NS1G - 1 and last_in_bank))
                    for hh in range(p.NKC):
                        inst = t.matmul(pn[0:1, hh * p.KW:(hh + 1) * p.KW],
                                        ones_col,
                                        kb[:, hh * p.KW:(hh + 1) * p.KW],
                                        start=False, stop=False)
                    tkv.inc(inst)

                def a_t(t):
                    t.wait_ge(ld0.h, 96)
                    for ps in (pkv, pn):
                        for hh in range(PW // 512):
                            t.matmul(ps[:, hh * 512:(hh + 1) * 512],
                                     zrow_bf, z512_bf, start=True, stop=False)
                    for gi in range(p.NS1G):
                        par = gi % 2
                        t.wait_ge(xsem[par].h, 16 * (gi // 2 + 1))
                        if gi >= 1:
                            t.wait_ge(vk1.h, gi)
                        qn = max(1, NC // 8)
                        for c in range(NC):
                            if gi == 0 and c % qn == 0:
                                t.wait_ge(wks[c // qn].h, 16)
                            for hh in range(p.NKC):
                                inst = t.matmul(
                                    pk[:, hh * p.KW:(hh + 1) * p.KW],
                                    xs[par][:, c * 128:(c + 1) * 128],
                                    wk_sb[:, c * p.DKV + hh * p.KW:
                                          c * p.DKV + (hh + 1) * p.KW],
                                    start=(c == 0), stop=(c == NC - 1))
                        tk.inc(inst)
                        if gi >= 1:
                            t.wait_ge(vv1.h, gi)
                        for c in range(NC):
                            if gi == 0 and c % qn == 0:
                                t.wait_ge(wvs[c // qn].h, 16)
                            for hh in range(p.NKC):
                                inst = t.matmul(
                                    pv[:, hh * p.KW:(hh + 1) * p.KW],
                                    xs[par][:, c * 128:(c + 1) * 128],
                                    wv_sb[:, c * p.DKV + hh * p.KW:
                                          c * p.DKV + (hh + 1) * p.KW],
                                    start=(c == 0), stop=(c == NC - 1))
                        tv.inc(inst)
                        if gi >= 1:
                            t.wait_ge(vkb.h, gi)
                            t.wait_ge(vv1.h, gi)
                            kv_mms(t, gi - 1)
                    t.wait_ge(vkb.h, p.NS1G)
                    t.wait_ge(vv1.h, p.NS1G)
                    kv_mms(t, p.NS1G - 1)
                    for hh in range(PW // 512):
                        tkv.inc(t.matmul(pn[:, hh * 512:(hh + 1) * 512],
                                         zrow_bf, z512_bf,
                                         start=False, stop=True))
                blk.tensor(a_t)

                def a_s(s):
                    for gi in range(p.NS1G):
                        s.wait_ge(vk2.h, gi + 1)
                        as1.inc(s.activation(ke[:, :], kt2[:, :], AF.Exp))
                blk.scalar(a_s)

                def a_v(v):
                    v.wait_ge(ld0.h, 96)
                    v.tensor_single_scalar(nbq_fm[:, :], bq_fm[:, :],
                                           -1.0, ALU.mult)
                    for gi in range(p.NS1G):
                        v.wait_ge(tk.h, gi + 1)
                        v.wait_ge(vkb.h, vkb.n)
                        vk1.inc(v.tensor_add(kt1[:, :], pk[:, 0:p.DKV], bk_bc[:, :]))
                        v.wait_ge(vk1.h, vk1.n)
                        if gi >= 1:
                            v.wait_ge(as1.h, gi)
                        vk2.inc(v.tensor_single_scalar(kt2[:, :], kt1[:, :],
                                                       0.0, ALU.min))
                        v.wait_ge(as1.h, gi + 1)
                        if gi >= 2:
                            v.wait_ge(tkv.h, gi - 1)
                        vkb.inc(v.scalar_tensor_tensor(k_bfs[gi % 2][:, :],
                                                       kt1[:, :], 0.0,
                                                       ke[:, :], ALU.max, ALU.add))
                        v.wait_ge(tv.h, gi + 1)
                        vv1.inc(v.tensor_add(v_bfs[gi % 2][:, :], pv[:, 0:p.DKV],
                                             bv_bc[:, :]))
                    v.wait_ge(tkv.h, p.NS1G + PW // 512)
                    vend.inc(v.tensor_copy(kv_sb[:, :], pkv[:, 0:p.DKV]))
                    vend.inc(v.tensor_copy(nm_sb[0:1, :], pn[0:1, 0:p.DKV]))
                blk.vector(a_v)

                def a_sy(sy):
                    sy.wait_ge(vend.h, 2)
                    src = kv_sb.ap().rearrange("p (h v) -> p h v", h=p.KVH)
                    dst = kv_loc[0:p.DKV].rearrange("(h q) v -> q h v", q=128)
                    s1st.inc(sy.dma_start(out=dst, in_=src), 16)
                    src2 = nm_sb.ap().rearrange("a (h v) -> a h v", h=p.KVH)
                    dst2 = kv_loc[p.DKV:p.DKV + p.KVH].rearrange(
                        "(a h) v -> a h v", a=1)
                    s1st.inc(sy.dma_start(out=dst2, in_=src2), 16)
                    sy.wait_ge(s1st.h, 32)
                blk.sync(a_sy)

        with ExitStack() as cb:
            bsb = lambda n, s, d: cb.enter_context(nc.sbuf_tensor(n, s, d))
            bps = lambda n, s: cb.enter_context(nc.psum_tensor(n, s, F32))
            bufAB = [bsb(f"bufab{i}", [128, NC * TT], BF16) for i in range(2)]
            zt = bsb("zt", [128, NG * D], F32)
            arena = [bsb(f"wa{i}", [128, NC * 128], BF16) for i in range(5)]
            ab_sb = bsb("ab_sb", [128, NC * 128 + D], BF16)
            bo_s = [bsb(f"bo{i}", [128, p.SW], F32) for i in range(2)]
            qy_bf = bsb("qy_bf", [128, max(2 * TT, p.LPW)], BF16)
            t2s = bsb("t2s", [128, TT], F32)
            t1r = bsb("t1r", [128, TT], F32)
            e_t = bsb("e_t", [128, TT], F32)
            tvs = bsb("tvs", [128, p.SW], F32)

            l_tm = bsb("l_tm", [128, p.E], F32)
            prb = bsb("prb", [128, p.E], F32)
            msk = bsb("msk", [128, p.E], F32)
            wfin = bsb("wfin", [128, p.E], F32)

            midw = bsb("midw", [128, TT], BF16)
            lg_sb = e_t[0:p.E, 0:TT]
            wrow = t1r[0:p.E, 0:TT]
            wexp = t2s
            stats = bsb("stats", [128, 6 * (D // 512)], F32)
            stats4 = bsb("stats4", [128, NG * 6 * (D // 512)], F32)
            aggr = bsb("aggr", [128, 2], F32)
            scal = bsb("scal", [128, 8], F32)
            nm8 = bsb("nm8", [p.KVH, 128], F32)
            nmt = bsb("nmt", [128, p.KVH], F32)
            PS = [bps(f"ps{i}", [128, 512]) for i in range(8)]

            w_s = [Sem(cb, nc, f"w_s{i}") for i in range(5)]
            wc_o = Sem(cb, nc, "wc_o")
            wc_b = Sem(cb, nc, "wc_b")
            xf = Sem(cb, nc, "xf")
            xfq = [Sem(cb, nc, f"xfq{i}") for i in range(4)]
            xt = [Sem(cb, nc, f"xt{i}") for i in range(NG)]
            bo_sem = [Sem(cb, nc, f"bo_sem{i}") for i in range(2)]
            st = Sem(cb, nc, "st")
            qm = Sem(cb, nc, "qm")
            am = Sem(cb, nc, "am")
            vq1 = Sem(cb, nc, "vq1")
            vqp = Sem(cb, nc, "vqp")
            vqb = Sem(cb, nc, "vqb")
            va = Sem(cb, nc, "va")
            aq1 = Sem(cb, nc, "aq1")
            om = Sem(cb, nc, "om")
            vo1 = Sem(cb, nc, "vo1")
            vo2 = Sem(cb, nc, "vo2")
            vst = Sem(cb, nc, "vst")
            vl1 = Sem(cb, nc, "vl1")
            al1 = Sem(cb, nc, "al1")
            vl2 = Sem(cb, nc, "vl2")
            al2 = Sem(cb, nc, "al2")
            vl3 = Sem(cb, nc, "vl3")
            ac = Sem(cb, nc, "ac")
            tl = Sem(cb, nc, "tl")
            vf = Sem(cb, nc, "vf")
            tf1 = Sem(cb, nc, "tf1")
            vf1 = Sem(cb, nc, "vf1")
            tf2 = Sem(cb, nc, "tf2")
            vf2 = Sem(cb, nc, "vf2")
            vf2a = Sem(cb, nc, "vf2a")
            as_f = Sem(cb, nc, "as_f")
            tf3 = Sem(cb, nc, "tf3")
            vf3 = Sem(cb, nc, "vf3")
            tf4 = Sem(cb, nc, "tf4")
            tf5 = Sem(cb, nc, "tf5")
            vf4 = Sem(cb, nc, "vf4")
            f4 = Sem(cb, nc, "f4")
            v42 = Sem(cb, nc, "v42")
            vl1b = Sem(cb, nc, "vl1b")
            al1b = Sem(cb, nc, "al1b")
            vl2b = Sem(cb, nc, "vl2b")
            al4 = Sem(cb, nc, "al4")
            vse = Sem(cb, nc, "vse")

            def vedge(v, inst):
                vse.inc(inst)
                v.wait_ge(vse.h, vse.n)

            QC = NC // 4
            BLK = NC + 8 * p.NSTRIP

            def blocks_of_tile():
                out = [("wq", oc, None) for oc in range(NC)]
                for s in range(p.NSTRIP):
                    for qq in range(4):
                        out.append(("wo", (s, qq), None))
                for s in range(p.NSTRIP):
                    for qq in range(4):
                        out.append(("bw", (s, qq), None))
                return out

            def block_src(kind, idx):
                if kind == "wq":
                    return wq_t[idx].rearrange("p c o -> p (c o)")
                s, qq = idx
                base = wo_t if kind == "wo" else bw_t
                sl = base[s].rearrange("p c o -> p c o")[
                    :, qq * QC:(qq + 1) * QC, :]
                return sl.rearrange("p c o -> p (c o)")

            def slot_of(j):
                return 4 if j == 0 else (j - 1) % 4

            def slot_count(j):
                return 1 if j == 0 else (j - 1) // 4 + 1

            PST = [0] * 4
            for jj in range(1, NC + 8 * p.NSTRIP):
                PST[(jj - 1) % 4] += 1

            def wwait(t, b, tau):
                j = b - BLK * tau
                if j == 0:
                    t.wait_ge(w_s[4].h, 16 * (tau + 1))
                else:
                    par = (j - 1) % 4
                    cnt = tau * PST[par] + (j - 1) // 4 + 1
                    t.wait_ge(w_s[par].h, 16 * cnt)

            def consumed_wait(kind, idx, ptau):
                if kind == "wq":
                    return (qm.h, NC * ptau + idx + 1)
                s, qq = idx
                if kind == "wo":
                    if qq < 3:
                        return (wc_o.h, 3 * p.NSTRIP * ptau + 3 * s + qq + 1)
                    return (om.h, NSG * ptau + NG * (s + 1))
                if qq < 3:
                    return (wc_b.h, 3 * p.NSTRIP * ptau + 3 * s + qq + 1)
                return (f4.h, NSG * ptau + NG * (s + 1))

            with nc.Block() as blk:


                tb = blocks_of_tile()
                for tau in range(p.NTILE):
                    b0 = BLK * tau
                    qm0, am0, vq10, vqp0, vqb0, va0, aq10 = (
                        qm.n, am.n, vq1.n, vqp.n, vqb.n, va.n, aq1.n)
                    om0, vo10, vo20, vst0 = om.n, vo1.n, vo2.n, vst.n
                    vl10, al10, vl20, al20, vl30, tl0, vf0, ac0 = (
                        vl1.n, al1.n, vl2.n, al2.n, vl3.n, tl.n, vf.n, ac.n)
                    tf10, vf10, tf20, vf20, vf2a0, asf0, tf30, vf30 = (
                        tf1.n, vf1.n, tf2.n, vf2.n, vf2a.n, as_f.n, tf3.n, vf3.n)
                    tf40, tf50, vf40, f40, v420 = tf4.n, tf5.n, vf4.n, f4.n, v42.n
                    vl1b0, al1b0, vl2b0, al40 = vl1b.n, al1b.n, vl2b.n, al4.n

                    XFb = bufAB[tau % 2]
                    ATb = bufAB[(tau + 1) % 2]

                    def ring_wait(g, j, tau=tau):
                        if j == 0:
                            pj, ptau = 0, tau - 1
                        elif j - 4 >= 1:
                            pj, ptau = j - 4, tau
                        else:
                            ptau = tau - 1
                            pslot = (j - 1) % 4
                            pj = max(jj for jj in range(1, BLK)
                                     if (jj - 1) % 4 == pslot)
                        if ptau < 0:
                            return
                        kind, idx, _ = tb[pj]
                        sem_h, val = consumed_wait(kind, idx, ptau)
                        g.wait_ge(sem_h, val)

                    def b_g(g, tau=tau):
                        if tau > 0:
                            g.wait_ge(om.h, NSG * tau)
                        if tau == 0:
                            qn = max(1, NC // 4)
                            for qq in range(0, NC, qn):
                                xfq[qq // qn].inc(g.dma_start(
                                    out=XFb[:, qq * TT:(qq + qn) * TT],
                                    in_=x_fmt[tau][:, qq:qq + qn, :].rearrange(
                                        "p c t -> p (c t)")), 16)
                        else:
                            src = x_fmt[tau].rearrange("p c t -> p (c t)")
                            xf.inc(g.dma_start(out=XFb[:, :], in_=src), 16)
                        for j, (kind, idx, _) in enumerate(tb):
                            if j == 0 and tau > 0:
                                continue
                            ring_wait(g, j)
                            par = slot_of(j)
                            w_s[par].inc(
                                g.dma_start(out=arena[par][:, :],
                                            in_=block_src(kind, idx)), 16)
                            if tau == 0 and j == 3:
                                g.wait_ge(cc.h, 1)
                                srck = kv_glob[0:p.DKV].rearrange(
                                    "(h q) v -> q h v", q=128)
                                dstk = kvt_bf.ap().rearrange(
                                    "p (h v) -> p h v", h=p.KVH)
                                kvl.inc(g.dma_start(out=dstk, in_=srck), 16)
                                kvl.inc(g.dma_start(
                                    out=nm8[:, :],
                                    in_=kv_glob[p.DKV:p.DKV + p.KVH]), 16)
                                abl.inc(g.dma_start(out=ab_sb[:, :],
                                                    in_=ab_t.ap()), 16)
                        if tau < p.NTILE - 1:
                            g.wait_ge(qm.h, NC * tau + 1)
                            w_s[4].inc(g.dma_start(
                                out=arena[4][:, :],
                                in_=block_src("wq", 0)), 16)
                    blk.gpsimd(b_g)

                    def b_sy(sy, tau=tau, al40=al40, vo10=vo10):
                        for gi in range(NG):
                            if tau > 0:
                                sy.wait_ge(al4.h, al40 - NG + gi + 1)
                                r0 = (tau - 1) * TT + gi * 128
                                st.inc(sy.dma_start(
                                    out=y_out[r0:r0 + 128, :],
                                    in_=zt[:, gi * D:(gi + 1) * D]), 16)
                                sy.wait_ge(st.h, st.n)
                            r0 = tau * TT + gi * 128
                            xt[gi].inc(sy.dma_start(
                                out=zt[:, gi * D:(gi + 1) * D],
                                in_=x_tm[r0:r0 + 128, :]), 16)
                        for s in range(p.NSTRIP):
                            if s >= 2:
                                sy.wait_ge(vo1.h, vo10 + NG * (s - 2) + NG)
                            bo_sem[s % 2].inc(sy.dma_start(
                                out=bo_s[s % 2][:, :],
                                in_=bo_bc_d[:, s * p.SW:(s + 1) * p.SW]), 16)
                    blk.sync(b_sy)

                    def q_t(t, tau=tau, vqp0=vqp0, vqb0=vqb0, va0=va0,
                            v420=v420, aq10=aq10):
                        if tau > 0:
                            t.wait_ge(xf.h, 16 * tau)
                            t.wait_ge(v42.h, v420)
                        for oc in range(NC):
                            b = b0 + oc
                            par = slot_of(oc)
                            wwait(t, b, tau)
                            if oc >= 2:
                                t.wait_ge(aq1.h, aq10 + 3 * (oc - 1))
                            for kc in range(NC):
                                if tau == 0 and oc == 0 and \
                                        kc % max(1, NC // 4) == 0:
                                    t.wait_ge(xfq[kc // max(1, NC // 4)].h, 16)
                                inst = t.matmul(
                                    PS[oc % 2][:, 0:TT],
                                    arena[par][:, kc * 128:(kc + 1) * 128],
                                    XFb[:, kc * TT:(kc + 1) * TT],
                                    start=(kc == 0), stop=(kc == NC - 1))
                            qm.inc(inst)
                            if tau > 0 and oc >= 1:
                                emit_attn(t, oc - 1)
                        if tau == 0:
                            t.wait_ge(kvl.h, 32)
                            t_n.inc(t.matmul(PS[7][:, 0:p.KVH], nm8[:, :],
                                             ident_f[0:p.KVH, 0:p.KVH],
                                             start=True, stop=True))
                            for j in range(NC):
                                emit_attn(t, j)
                        else:
                            emit_attn(t, NC - 1)
                    def emit_attn(t, j, tau=tau, vqb0=vqb0, va0=va0):
                        t.wait_ge(vqb.h, vqb0 + j + 1)
                        if j >= 2:
                            t.wait_ge(va.h, va0 + j - 1)
                        h = j // p.NREP
                        am.inc(t.matmul(PS[2 + j % 2][:, 0:TT],
                                        kvt_bf[:, h * 128:(h + 1) * 128],
                                        ATb[:, j * TT:(j + 1) * TT],
                                        start=True, stop=True))
                    blk.tensor(q_t)

                    def q_s(s, tau=tau, qm0=qm0, am0=am0, vqb0=vqb0,
                            va0=va0, aq10=aq10):
                        for oc in range(NC):
                            s.wait_ge(qm.h, qm0 + oc + 1)
                            aq1.inc(s.activation(t2s[:, 0:TT],
                                                 PS[oc % 2][:, 0:TT], AF.Relu,
                                                 bias=nbq_fm[:, oc:oc + 1],
                                                 scale=-1.0))
                            s.wait_ge(aq1.h, aq1.n)
                            if oc >= 1:
                                s.wait_ge(vqb.h, vqb0 + oc)
                            aq1.inc(s.activation(e_t[:, 0:TT], t2s[:, 0:TT],
                                                 AF.Exp, scale=-1.0))
                            s.wait_ge(aq1.h, aq1.n)
                            if oc >= 1:
                                s.wait_ge(vqb.h, vqb0 + oc)
                            aq1.inc(s.activation(t1r[:, 0:TT],
                                                 PS[oc % 2][:, 0:TT], AF.Relu,
                                                 bias=bq_fm[:, oc:oc + 1]))
                            if tau > 0 and oc >= 1:
                                j = oc - 1
                                s.wait_ge(am.h, am0 + j + 1)
                                h = j // p.NREP
                                va.inc(s.activation(
                                    ATb[:, j * TT:(j + 1) * TT],
                                    PS[2 + j % 2][:, 0:TT], AF.Identity,
                                    scale=inv_nm[:, h:h + 1]))
                        if tau == 0:
                            s.wait_ge(v_n.h, 2)
                            ev_js = list(range(NC))
                        else:
                            ev_js = [NC - 1]
                        for j in ev_js:
                            s.wait_ge(am.h, am0 + j + 1)
                            h = j // p.NREP
                            va.inc(s.activation(
                                ATb[:, j * TT:(j + 1) * TT],
                                PS[2 + j % 2][:, 0:TT], AF.Identity,
                                scale=inv_nm[:, h:h + 1]))
                    blk.scalar(q_s)

                    def q_v(v, tau=tau, qm0=qm0, am0=am0, aq10=aq10):
                        for oc in range(NC):
                            v.wait_ge(aq1.h, aq10 + 3 * oc + 3)
                            vqb.inc(v.tensor_add(
                                ATb[:, oc * TT:(oc + 1) * TT],
                                t1r[:, 0:TT], e_t[:, 0:TT]))
                        if tau == 0:
                            v.wait_ge(t_n.h, 1)
                            v_n.inc(v.tensor_single_scalar(
                                nmt[:, :], PS[7][:, 0:p.KVH], 1e-10, ALU.add))
                            v.wait_ge(v_n.h, v_n.n)
                            v_n.inc(v.reciprocal(inv_nm[:, :], nmt[:, :]))
                    blk.vector(q_v)

                    def o_t(t, tau=tau, va0=va0, vqp0=vqp0, vo10=vo10,
                            aq10=aq10):
                        for s in range(p.NSTRIP):
                            sp = s % 2
                            for qq in range(4):
                                b = b0 + NC + 4 * s + qq
                                par = slot_of(NC + 4 * s + qq)
                                wwait(t, b, tau)
                                if s == 0 and qq == 0:
                                    t.wait_ge(va.h, va0 + NC)
                                    t.wait_ge(aq1.h, aq10 + 3 * NC)
                                if tau == 0 and s == min(1, p.NSTRIP - 1) \
                                        and qq == 0:
                                    t.wait_ge(v_n.h, 2)
                                if s >= 2 and qq == 0:
                                    t.wait_ge(vo1.h, vo10 + NG * (s - 2) + NG)
                                for gi in range(NG):
                                    for c in range(QC):
                                        fc = qq * QC + c
                                        inst = t.matmul(
                                            PS[4 * sp + gi][:, 0:p.SW],
                                            ATb[:, fc * TT + gi * 128:
                                                fc * TT + (gi + 1) * 128],
                                            arena[par][:, c * p.SW:(c + 1) * p.SW],
                                            start=(qq == 0 and c == 0),
                                            stop=(qq == 3 and c == QC - 1))
                                    if qq == 3:
                                        om.inc(inst)
                                if qq < 3:
                                    wc_o.inc(inst)
                    blk.tensor(o_t)

                    def o_v(v, tau=tau, om0=om0):
                        nbo = (p.NSTRIP + 1) // 2
                        for s in range(p.NSTRIP):
                            for gi in range(NG):
                                v.wait_ge(om.h, om0 + NG * s + gi + 1)
                                v.wait_ge(bo_sem[s % 2].h,
                                          16 * (tau * nbo + s // 2 + 1))
                                v.wait_ge(xt[gi].h, 16 * (tau + 1))
                                vo1.inc(v.tensor_add(
                                    tvs[:, :], PS[4 * (s % 2) + gi][:, 0:p.SW],
                                    bo_s[s % 2][:, :]))
                                v.wait_ge(vo1.h, vo1.n)
                                sl = zt[:, gi * D + s * p.SW:
                                        gi * D + (s + 1) * p.SW]
                                vo2.inc(v.tensor_add(sl, tvs[:, :], sl))
                                v.wait_ge(vo2.h, vo2.n)
                                nst = D // 512
                                for i2 in range(p.SW // 512):
                                    i = (s * p.SW) // 512 + i2
                                    vst.inc(v.bn_stats(
                                        stats4[:, gi * 6 * nst + 6 * i:
                                               gi * 6 * nst + 6 * (i + 1)],
                                        zt[:, gi * D + 512 * i:
                                           gi * D + 512 * (i + 1)]))
                    blk.vector(o_v)

                    def l_v(v, tau=tau, vo20=vo20, al10=al10, al40=al40,
                            tl0=tl0, vf0=vf0):
                        nst = D // 512
                        nps = p.SW // 512
                        for gi in range(NG):
                            v.wait_ge(vst.h, vst0 + nps * (NG * (p.NSTRIP - 1)
                                                           + gi + 1))
                            if gi == 0 and tau > 0:
                                v.wait_ge(al4.h, al40)
                            vl1.inc(v.bn_aggr(aggr[:, 0:2],
                                              stats4[:, gi * 6 * nst:
                                                     (gi + 1) * 6 * nst]))
                            v.wait_ge(vl1.h, vl1.n)
                            v.wait_ge(al1.h, al10 + gi + 1)
                            rec = v.reciprocal(scal[:, gi:gi + 1],
                                               stats[:, 1:2])
                            nmop = v.tensor_single_scalar(stats[:, 0:1],
                                                          aggr[:, 0:1],
                                                          -1.0, ALU.mult)
                            vedge(v, rec)
                            vedge(v, nmop)
                            vl2.inc(v.tensor_mul(scal[:, 4 + gi:5 + gi],
                                                 stats[:, 0:1],
                                                 scal[:, gi:gi + 1]))
                            v.wait_ge(vl2.h, vl2.n)
                        dst = XFb.ap().rearrange("p (c t) -> p c t", c=NC)
                        for pc in range(p.LNP):
                            for gi in range(NG):
                                idx = pc * NG + gi
                                v.wait_ge(tl.h, tl0 + idx + 1)
                                for bi in range(p.NB):
                                    nch = min(4, p.NQT - 4 * bi)
                                    c0 = pc * p.NQT + bi * 4
                                    sl = dst[:, c0:c0 + nch,
                                             gi * 128:(gi + 1) * 128]
                                    vf.inc(v.tensor_copy(
                                        sl,
                                        PS[6 + bi][:, 0:128 * nch].rearrange(
                                            "p (c t) -> p c t", c=nch)))
                    blk.vector(l_v)

                    def l_s(s, tau=tau, vl10=vl10, vl20=vl20, tl0=tl0,
                            al20=al20):
                        for gi in range(NG):
                            s.wait_ge(vl1.h, vl10 + gi + 1)
                            al1.inc(s.activation(stats[:, 1:2], aggr[:, 1:2],
                                                 AF.Sqrt, bias=eps_ln))
                        for pc in range(p.LNP):
                            for gi in range(NG):
                                idx = pc * NG + gi
                                if pc == 0:
                                    s.wait_ge(vl2.h, vl20 + gi + 1)
                                z = zt[:, gi * D + pc * p.LPW:
                                       gi * D + (pc + 1) * p.LPW]
                                al2.inc(s.activation(
                                    z, z, AF.Identity,
                                    bias=scal[:, 4 + gi:5 + gi],
                                    scale=scal[:, gi:gi + 1]))
                                s.wait_ge(al2.h, al2.n)
                                if idx >= 1:
                                    s.wait_ge(tl.h, tl0 + idx)
                                ac.inc(s.activation(qy_bf[:, 0:p.LPW], z,
                                                    AF.Identity))
                    blk.scalar(l_s)

                    def f_t(t, tau=tau, vf0=vf0, vf10=vf10, vf20=vf20,
                            vf2a0=vf2a0, vf30=vf30, vf40=vf40, v420=v420,
                            ac0=ac0, tl0=tl0, vo20=vo20):
                        if tau == 0:
                            t.wait_ge(abl.h, 16)
                        t.wait_ge(vo2.h, vo20 + NG * p.NSTRIP)

                        def emit_chunks(pcx):
                            t.wait_ge(vf.h, vf0 + (pcx + 1) * NG * p.NB)
                            for c in range(pcx * p.NQT, (pcx + 1) * p.NQT):
                                inst = t.matmul(
                                    PS[0][0:p.E, 0:TT],
                                    rw_bf[:, c * p.E:(c + 1) * p.E],
                                    XFb[:, c * TT:(c + 1) * TT],
                                    start=(c == 0), stop=(c == NC - 1))
                                if c == NC - 1:
                                    tf1.inc(inst)
                                inst = t.matmul(
                                    PS[4][:, 0:TT],
                                    ab_sb[:, c * 128:(c + 1) * 128],
                                    XFb[:, c * TT:(c + 1) * TT],
                                    start=(c == 0), stop=(c == NC - 1))
                                if c == NC - 1:
                                    tf5.inc(inst)

                        for pc in range(p.LNP):
                            for gi in range(NG):
                                idx = pc * NG + gi
                                t.wait_ge(ac.h, ac0 + idx + 1)
                                if idx >= 1:
                                    t.wait_ge(vf.h, vf0 + idx * p.NB)
                                srcq = qy_bf
                                for j in range(p.NQT):
                                    bi, jo = j // 4, j % 4
                                    inst = t.matmul(
                                        PS[6 + bi][:, jo * 128:(jo + 1) * 128],
                                        srcq[:, j * 128:(j + 1) * 128],
                                        ident_bf, start=True, stop=True)
                                tl.inc(inst)
                            if pc >= 1:
                                emit_chunks(pc - 1)
                        emit_chunks(p.LNP - 1)
                        def emit_tf2(gi):
                            if gi == 0:
                                t.wait_ge(vf1.h, vf10 + 1)
                            else:
                                t.wait_ge(vf2a.h, vf2a0 + gi)
                            tf2.inc(t.matmul(
                                PS[5][:, 0:p.E],
                                lg_sb[:, gi * 128:(gi + 1) * 128],
                                ident_f[0:p.E, 0:p.E],
                                start=True, stop=True))

                        def emit_tf3(gi):
                            t.wait_ge(vf2.h, vf20 + gi + 1)
                            tf3.inc(t.matmul(
                                PS[6][0:p.E, gi * 128:(gi + 1) * 128],
                                wfin[:, :], ident_f, start=True, stop=True))

                        def emit_tf4():
                            t.wait_ge(vf3.h, vf30 + 1)
                            tf4.inc(t.matmul(PS[7][:, 0:TT], e_mat,
                                             wrow[:, 0:TT],
                                             start=True, stop=True))
                        for s in range(p.NSTRIP):
                            sp = s % 2
                            for qq in range(4):
                                b = b0 + NC + 4 * p.NSTRIP + 4 * s + qq
                                par = slot_of(NC + 4 * p.NSTRIP + 4 * s + qq)
                                wwait(t, b, tau)
                                if s == 0 and qq == 0:
                                    t.wait_ge(vf1.h, vf10 + 1)
                                if s == 0 and qq == 1:
                                    emit_tf2(0)
                                    emit_tf3(0)
                                    if NG >= 2:
                                        emit_tf2(1)
                                if s == 0 and qq == 2:
                                    if NG >= 2:
                                        emit_tf3(1)
                                    for g2i in range(2, NG):
                                        emit_tf2(g2i)
                                        emit_tf3(g2i)
                                if s == 0 and qq == 3:
                                    emit_tf4()
                                if s == 1 and qq == 0:
                                    t.wait_ge(vf4.h, vf40 + 2)
                                if s >= 2 and qq == 0:
                                    t.wait_ge(v42.h, v420 + NG * (s - 2) + NG)
                                for gi in range(NG):
                                    for c in range(QC):
                                        fc = qq * QC + c
                                        inst = t.matmul(
                                            PS[4 * sp + gi][:, 0:p.SW],
                                            XFb[:, fc * TT + gi * 128:
                                                fc * TT + (gi + 1) * 128],
                                            arena[par][:, c * p.SW:(c + 1) * p.SW],
                                            start=(qq == 0 and c == 0),
                                            stop=False)
                                    if qq == 3:
                                        if s == 0 and gi == 0:
                                            t.wait_ge(vf4.h, vf40 + 2)
                                        f4.inc(t.matmul(
                                            PS[4 * sp + gi][:, 0:p.SW],
                                            midw[:, gi * 128:(gi + 1) * 128],
                                            ab_sb[:, NC * 128 + s * p.SW:
                                                  NC * 128 + (s + 1) * p.SW],
                                            start=False, stop=True))
                                if qq < 3:
                                    wc_b.inc(inst)
                    blk.tensor(f_t)

                    def f_s(s, tau=tau, vl1b0=vl1b0, vl2b0=vl2b0):
                        for gi in range(NG):
                            s.wait_ge(vl1b.h, vl1b0 + gi + 1)
                            as_f.inc(s.activation(prb[:, :], l_tm[:, :], AF.Exp,
                                                  bias=scal[:, 0:1],
                                                  accum_out=scal[:, 1:2]))
                        for gi in range(NG):
                            s.wait_ge(vl2b.h, vl2b0 + 2 * gi + 1)
                            al1b.inc(s.activation(scal[:, 4:5], aggr[:, 1:2],
                                                  AF.Sqrt, bias=eps_ln))
                            s.wait_ge(vl2b.h, vl2b0 + 2 * gi + 2)
                            al4.inc(s.activation(zt[:, gi * D:(gi + 1) * D],
                                                 zt[:, gi * D:(gi + 1) * D],
                                                 AF.Identity, bias=scal[:, 7:8],
                                                 scale=scal[:, 5:6]))
                    blk.scalar(f_s)

                    def f_v(v, tau=tau, tf10=tf10, tf20=tf20, tf30=tf30,
                            tf40=tf40, tf50=tf50, f40=f40, v420=v420,
                            asf0=asf0, al20=al20, al1b0=al1b0, al40=al40,
                            vf2a0=vf2a0):
                        v.wait_ge(tf1.h, tf10 + 1)
                        v.wait_ge(al2.h, al20 + NG * p.LNP)
                        vf1.inc(v.tensor_copy(lg_sb[:, :], PS[0][0:p.E, 0:TT]))
                        for gi in range(NG):
                            v.wait_ge(tf2.h, tf20 + gi + 1)
                            vf2a.inc(v.tensor_copy(l_tm[:, :], PS[5][:, 0:p.E]))
                            v.wait_ge(vf2a.h, vf2a.n)
                            vl1b.inc(v.tensor_reduce(scal[:, 0:1], l_tm[:, :],
                                                     AX.X, ALU.max, negate=True))
                            v.wait_ge(as_f.h, asf0 + gi + 1)
                            rec = v.reciprocal(scal[:, 2:3], scal[:, 1:2])
                            vedge(v, rec)
                            vedge(v, v.tensor_scalar(prb[:, :], prb[:, :],
                                                     scal[:, 2:3], None,
                                                     ALU.mult))
                            m1 = v.tensor_reduce(scal[:, 3:4], prb[:, :],
                                                 AX.X, ALU.max)
                            vedge(v, m1)
                            vedge(v, v.tensor_scalar(msk[:, :], prb[:, :],
                                                     scal[:, 3:4], None,
                                                     ALU.is_equal))
                            vedge(v, v.scalar_tensor_tensor(
                                msk[:, :], msk[:, :], -2.0,
                                prb[:, :], ALU.mult, ALU.add))
                            m2 = v.tensor_reduce(scal[:, 4:5], msk[:, :],
                                                 AX.X, ALU.max)
                            vedge(v, m2)
                            vedge(v, v.tensor_scalar(msk[:, :], prb[:, :],
                                                     scal[:, 4:5], None,
                                                     ALU.is_ge))
                            dn = v.tensor_add(scal[:, 6:7], scal[:, 3:4],
                                              scal[:, 4:5])
                            vedge(v, dn)
                            rd = v.reciprocal(scal[:, 5:6], scal[:, 6:7])
                            vedge(v, rd)
                            sc2 = v.tensor_single_scalar(scal[:, 6:7],
                                                         scal[:, 5:6],
                                                         2.0, ALU.mult)
                            vedge(v, sc2)
                            vedge(v, v.tensor_mul(wfin[:, :], prb[:, :],
                                                  msk[:, :]))
                            vf2.inc(v.tensor_scalar(wfin[:, :], wfin[:, :],
                                                    scal[:, 6:7], None,
                                                    ALU.mult))
                        v.wait_ge(tf3.h, tf30 + NG)
                        vf3.inc(v.tensor_copy(wrow[:, 0:TT],
                                              PS[6][0:p.E, 0:TT]))
                        v.wait_ge(tf4.h, tf40 + 1)
                        vf4.inc(v.tensor_copy(wexp[:, 0:TT], PS[7][:, 0:TT]))
                        v.wait_ge(vf4.h, vf4.n)
                        v.wait_ge(tf5.h, tf50 + 1)
                        vf4.inc(v.tensor_mul(midw[:, 0:TT], PS[4][:, 0:TT],
                                             wexp[:, 0:TT]))
                        for s in range(p.NSTRIP):
                            for gi in range(NG):
                                v.wait_ge(f4.h, f40 + NG * s + gi + 1)
                                sl = zt[:, gi * D + s * p.SW:
                                        gi * D + (s + 1) * p.SW]
                                v42.inc(v.tensor_add(
                                    sl, PS[4 * (s % 2) + gi][:, 0:p.SW], sl))
                        for gi in range(NG):
                            v.wait_ge(v42.h,
                                      v420 + NG * (p.NSTRIP - 1) + gi + 1)
                            if gi >= 1:
                                v.wait_ge(al4.h, al40 + gi)
                            else:
                                v.wait_ge(as_f.h, asf0 + NG)
                            nst = D // 512
                            for i in range(nst):
                                ii = v.bn_stats(stats[:, 6 * i:6 * (i + 1)],
                                                zt[:, gi * D + 512 * i:
                                                   gi * D + 512 * (i + 1)])
                            vedge(v, ii)
                            vl2b.inc(v.bn_aggr(aggr[:, 0:2],
                                               stats[:, 0:6 * nst]))
                            v.wait_ge(vl2b.h, vl2b.n)
                            v.wait_ge(al1b.h, al1b0 + gi + 1)
                            rec = v.reciprocal(scal[:, 5:6], scal[:, 4:5])
                            nmop = v.tensor_single_scalar(scal[:, 0:1],
                                                          aggr[:, 0:1],
                                                          -1.0, ALU.mult)
                            vedge(v, rec)
                            vedge(v, nmop)
                            vl2b.inc(v.tensor_mul(scal[:, 7:8], scal[:, 0:1],
                                                  scal[:, 5:6]))
                    blk.vector(f_v)

                def fin_sy(sy):
                    for gi in range(NG):
                        sy.wait_ge(al4.h, NG * (p.NTILE - 1) + gi + 1)
                        r0 = (p.NTILE - 1) * TT + gi * 128
                        st.inc(sy.dma_start(out=y_out[r0:r0 + 128, :],
                                            in_=zt[:, gi * D:(gi + 1) * D]), 16)
                    sy.wait_ge(st.h, st.n)
                blk.sync(fin_sy)

    return nc



def _tile_x_fm(x, p):
    return np.ascontiguousarray(
        x.reshape(p.NTILE, p.TT, p.NC, 128).transpose(0, 3, 2, 1))


def _tile_wq(w, p):
    return np.ascontiguousarray(
        w.reshape(p.NC, 128, p.NC, 128).transpose(0, 3, 2, 1))


def _tile_wkv(w, p):
    return np.ascontiguousarray(
        w.reshape(w.shape[0], p.NC, 128).transpose(2, 1, 0))


def _tile_strip(w, p):
    return np.ascontiguousarray(
        w.reshape(p.NSTRIP, p.SW, p.NC, 128).transpose(0, 3, 2, 1))


def _tile_rw(w, p):
    return np.ascontiguousarray(
        w.reshape(p.E, p.NC, 128).transpose(2, 1, 0))


def _tile_ab(la, lb, p):
    a = la.transpose(1, 0, 2).reshape(p.D, p.E * p.R)
    a = a.reshape(p.NC, 128, p.ER).transpose(1, 0, 2).reshape(128, p.NC * p.ER)
    b = lb.reshape(p.ER, p.D)
    return np.ascontiguousarray(np.concatenate([a, b], axis=1))


def make_consts(p):
    cst_bf = np.zeros((128, 832), dtype=ml_dtypes.bfloat16)
    cst_bf[:, 0] = 1.0
    cst_bf[:, 704:832] = np.eye(128, dtype=np.float32).astype(ml_dtypes.bfloat16)
    cst_f = np.zeros((128, 258), dtype=np.float32)
    cst_f[:, 256] = 1e-5
    cst_f[:, 0:128] = np.eye(128, dtype=np.float32)
    emat = np.repeat(np.eye(p.E, dtype=np.float32), p.R, axis=1)
    cst_f[0:p.E, 128:256] = emat
    return cst_bf, cst_f


def make_core_inputs(p, x_core, wq, bq, wk, bk, wv, bv, wo, bo,
                     base_w, router_w, lora_a, lora_b):
    cst_bf, cst_f = make_consts(p)
    return {
        "x_fmt": _tile_x_fm(x_core, p),
        "x_tm": np.ascontiguousarray(x_core),
        "wq_t": _tile_wq(wq, p),
        "wk_t": _tile_wkv(wk, p),
        "wv_t": _tile_wkv(wv, p),
        "wo_t": _tile_strip(wo, p),
        "bw_t": _tile_strip(base_w, p),
        "rw_t": _tile_rw(router_w, p),
        "ab_t": _tile_ab(lora_a, lora_b, p),
        "bq_fm": np.ascontiguousarray(bq.reshape(p.NC, 128).T),
        "bk_bc": np.ascontiguousarray(
            np.broadcast_to(bk[None, :], (128, p.DKV))),
        "bv_bc": np.ascontiguousarray(
            np.broadcast_to(bv[None, :], (128, p.DKV))),
        "bo_bc": np.ascontiguousarray(
            np.broadcast_to(bo[None, :], (128, p.D))),
        "cst_bf": cst_bf,
        "cst_f": cst_f,
    }


def kernel(L0_attn_hidden_states, L0_attn_Wq, L0_attn_bq, L0_attn_Wk,
           L0_attn_bk, L0_attn_Wv, L0_attn_bv, L0_attn_Wo, L0_attn_bo,
           L0_ffn_base_W, L0_ffn_router_W, L0_ffn_lora_A, L0_ffn_lora_B):
    p = P()
    x = np.asarray(L0_attn_hidden_states, dtype=np.float32).reshape(16384, 4096)
    args = [np.asarray(a, dtype=np.float32) for a in (
        L0_attn_Wq, L0_attn_bq, L0_attn_Wk, L0_attn_bk, L0_attn_Wv,
        L0_attn_bv, L0_attn_Wo, L0_attn_bo, L0_ffn_base_W, L0_ffn_router_W,
        L0_ffn_lora_A, L0_ffn_lora_B)]
    wq, bq, wk, bk, wv, bv, wo, bo, base_w, router_w, lora_a, lora_b = args

    nc = build_program(p)
    in_maps = []
    for c in range(p.NCORE):
        xc = x[c * p.TOK:(c + 1) * p.TOK]
        in_maps.append(make_core_inputs(p, xc, wq, bq, wk, bk, wv, bv, wo, bo,
                                        base_w, router_w, lora_a, lora_b))
    res = run_bass_kernel_spmd(nc, in_maps, core_ids=list(range(p.NCORE)))
    out = np.concatenate([np.asarray(r["y"]) for r in res.results], axis=0)
    return out.reshape(1, 16384, 4096).astype(np.float32)
